# version 06
# baseline (speedup 1.0000x reference)
import math
import numpy as np
import concourse.bass as bass
import concourse.mybir as mybir
from concourse.bass_utils import run_bass_kernel_spmd

F32 = mybir.dt.float32
F32R = mybir.dt.float32r
BF16 = mybir.dt.bfloat16
U32 = mybir.dt.uint32
U8 = mybir.dt.uint8
AX = mybir.AxisListType
ALU = mybir.AluOpType
AF = mybir.ActivationFunctionType

NCORES = 8
BATCH = 8
T = 2048
D = 1024
PD = 8


ENGS = ("PE", "ACT", "DVE", "POOL", "SP")
DMA_QUEUES = ("SP", "POOL", "ACT")
DMA_ROT = 8


def _ap_box(ap):
    t = ap.tensor
    name = t.name
    try:
        space = str(ap.space)
    except Exception:
        space = "?"
    es = mybir.dt.size(ap.dtype)
    lo = hi = 0
    for step, count in ap.ap:
        d = step * (count - 1)
        if d >= 0:
            hi += d
        else:
            lo += d
    off = ap.offset if isinstance(ap.offset, int) else 0
    if "SBUF" in space or "PSUM" in space:
        step0 = ap.ap[0][0] if len(ap.ap) else 1
        pb = (off // step0) if step0 > 0 else 0
        pn = ap.partition_size()
        lo = hi = 0
        for step, count in ap.ap[1:]:
            d = step * (count - 1)
            if d >= 0:
                hi += d
            else:
                lo += d
        off_free = off - pb * step0
        return (name, "S", pb, pb + pn,
                (off_free + lo) * es, (off_free + hi + 1) * es)
    return (name, "D", 0, 1, (off + lo) * es, (off + hi + 1) * es)


def _overlap(a, b):
    return (a[0] == b[0] and a[2] < b[3] and b[2] < a[3]
            and a[4] < b[5] and b[4] < a[5])


class Op:
    __slots__ = ("eng", "fn", "idx", "gidx", "needs_inc", "rank", "is_dma",
                 "note")

    def __init__(self, eng, fn, gidx, is_dma, note=""):
        self.eng = eng
        self.fn = fn
        self.gidx = gidx
        self.is_dma = is_dma
        self.needs_inc = False
        self.rank = None
        self.note = note


class Prog:
    def __init__(self, nc):
        self.nc = nc
        self.ops = []
        self.edges = {}
        self.writes = {}
        self.reads = {}
        self.eng_ops = {e: [] for e in ENGS}
        self.psum_tensors = set()

    def _dep_read(self, box, gidx, deps):
        for (wb, wg) in self.writes.get(box[0], ()):
            if _overlap(box, wb):
                deps.add(wg)

    def _dep_write(self, box, gidx, deps):
        for (wb, wg) in self.writes.get(box[0], ()):
            if _overlap(box, wb):
                deps.add(wg)
        for (rb, rg) in self.reads.get(box[0], ()):
            if _overlap(box, rb):
                deps.add(rg)

    def _record_read(self, box, gidx):
        self.reads.setdefault(box[0], []).append((box, gidx))

    def _record_write(self, box, gidx):
        ws = self.writes.setdefault(box[0], [])
        ws[:] = [(b, g) for (b, g) in ws
                 if not (b[2] >= box[2] and b[3] <= box[3]
                         and b[4] >= box[4] and b[5] <= box[5])]
        ws.append((box, gidx))
        rs = self.reads.get(box[0])
        if rs:
            rs[:] = [(b, g) for (b, g) in rs
                     if not (b[2] >= box[2] and b[3] <= box[3]
                             and b[4] >= box[4] and b[5] <= box[5])]

    def _expand_psum(self, ap, box):
        if "PSUM" in str(ap.space):
            sh = ap.tensor.shape
            es = mybir.dt.size(ap.tensor.dtype)
            return (box[0], "S", 0, 128, 0, int(np.prod(sh[1:])) * es)
        return box

    def op(self, eng, fn, reads=(), writes=(), note="", extra_deps=()):
        gidx = len(self.ops)
        deps = set(extra_deps)
        boxes_r = []
        boxes_w = []
        for ap in reads:
            b = self._expand_psum(ap, _ap_box(ap))
            boxes_r.append(b)
            self._dep_read(b, gidx, deps)
        for ap in writes:
            b = self._expand_psum(ap, _ap_box(ap))
            boxes_w.append(b)
            self._dep_write(b, gidx, deps)
        for b in boxes_r:
            self._record_read(b, gidx)
        for b in boxes_w:
            self._record_write(b, gidx)
        o = Op(eng, fn, gidx, is_dma=False, note=note)
        self.ops.append(o)
        self.eng_ops[eng].append(o)
        deps2 = set()
        for d in deps:
            p = self.ops[d]
            if p.eng == "PE" and eng == "PE" and not p.is_dma:
                continue
            deps2.add(d)
        if deps2:
            self.edges[gidx] = deps2
            for d in deps2:
                self.ops[d].needs_inc = True
        return gidx

    def dma(self, queue, out, in_, reads=(), writes=(), note="", **kw):
        assert queue in DMA_QUEUES
        gidx = len(self.ops)
        deps = set()
        allr = list(reads) + [in_]
        allw = list(writes) + [out]
        boxes_r = []
        boxes_w = []
        for ap in allr:
            b = self._expand_psum(ap, _ap_box(ap))
            boxes_r.append(b)
            self._dep_read(b, gidx, deps)
        for ap in allw:
            b = self._expand_psum(ap, _ap_box(ap))
            boxes_w.append(b)
            self._dep_write(b, gidx, deps)
        for b in boxes_r:
            self._record_read(b, gidx)
        for b in boxes_w:
            self._record_write(b, gidx)

        def fn(eng, _out=out, _in=in_, _kw=kw):
            return eng.dma_start(out=_out, in_=_in, **_kw)

        o = Op(queue, fn, gidx, is_dma=True, note=note)
        self.ops.append(o)
        self.eng_ops[queue].append(o)
        deps3 = set(deps)
        if deps3:
            self.edges[gidx] = deps3
            for d in deps3:
                self.ops[d].needs_inc = True
        return gidx

    def barrier(self):
        lasts = [self.eng_ops[e][-1].gidx for e in ENGS if self.eng_ops[e]]
        for e in ENGS:
            self.op(e, lambda eng: eng.nop(), extra_deps=lasts,
                    note="barrier")

    def finalize(self, final_waits=True):
        nc = self.nc
        for e in ENGS:
            rank = 0
            for o in self.eng_ops[e]:
                if o.needs_inc:
                    rank += 1
                    o.rank = rank
        self._ctxs = []

        def sem(name):
            g = nc.semaphore(name)
            s = g.__enter__()
            self._ctxs.append(g)
            return s

        eng_sem = {e: sem(f"c_{e}") for e in ENGS}
        dma_sems = {e: [sem(f"d_{e}{i}") for i in range(DMA_ROT)]
                    for e in DMA_QUEUES}
        for e in DMA_QUEUES:
            cnt = 0
            for o in self.eng_ops[e]:
                if o.is_dma:
                    o.rank = cnt
                    cnt += 1

        def wait_target(p):
            if p.is_dma:
                s = dma_sems[p.eng][p.rank % DMA_ROT]
                return (s, 16 * (p.rank // DMA_ROT + 1))
            return (eng_sem[p.eng], p.rank)

        blk_g = nc.Block()
        blk = blk_g.__enter__()

        def mk_body(e):
            def body(eng):
                waited = {}
                for o in self.eng_ops[e]:
                    if o.is_dma and o.rank >= DMA_ROT:
                        s = dma_sems[e][o.rank % DMA_ROT]
                        v = 16 * (o.rank // DMA_ROT)
                        if waited.get(id(s), -1) < v:
                            eng.wait_ge(s, v)
                            waited[id(s)] = v
                    deps = self.edges.get(o.gidx)
                    if deps:
                        tgts = {}
                        for d in deps:
                            p = self.ops[d]
                            s, v = wait_target(p)
                            key = id(s)
                            if tgts.get(key, (None, -1))[1] < v:
                                tgts[key] = (s, v)
                        for s, v in tgts.values():
                            if waited.get(id(s), -1) >= v:
                                continue
                            eng.wait_ge(s, v)
                            waited[id(s)] = v
                    ins = o.fn(eng)
                    if o.is_dma:
                        s = dma_sems[e][o.rank % DMA_ROT]
                        ins.then_inc(s, 16)
                    elif o.needs_inc:
                        ins.then_inc(eng_sem[e], 1)
                if final_waits and e in DMA_QUEUES:
                    cnt = sum(1 for o in self.eng_ops[e] if o.is_dma)
                    if cnt:
                        for i in range(DMA_ROT):
                            n = cnt // DMA_ROT + (1 if i < cnt % DMA_ROT else 0)
                            if n:
                                eng.wait_ge(dma_sems[e][i], 16 * n)
            return body

        blk.sync(mk_body("SP"))
        blk.tensor(mk_body("PE"))
        blk.vector(mk_body("DVE"))
        blk.scalar(mk_body("ACT"))
        blk.gpsimd(mk_body("POOL"))
        blk_g.__exit__(None, None, None)
        for g in reversed(self._ctxs):
            g.__exit__(None, None, None)


class Alloc:

    def __init__(self, nc):
        self.nc = nc
        self.ctxs = []
        self.tensors = {}

    def sbuf(self, name, shape, dtype=F32):
        g = self.nc.sbuf_tensor(name, list(shape), dtype)
        t = g.__enter__()
        self.ctxs.append(g)
        self.tensors[name] = t
        return t

    def psum(self, name, shape, dtype=F32):
        g = self.nc.psum_tensor(name, list(shape), dtype)
        t = g.__enter__()
        self.ctxs.append(g)
        self.tensors[name] = t
        return t

    def close(self):
        for g in reversed(self.ctxs):
            g.__exit__(None, None, None)


def r32(ap):
    return ap.bitcast(F32R)


def w32(ap):
    return ap.bitcast(F32R)


XW = 2052
XOFF = 4


def xi_ap(xi, pc, t0, n, shifted=False):
    off = XW * pc + XOFF + t0 - (1 if (shifted and pc >= 4) else 0)
    return xi[:, off:off + n]




def host_consts():
    c = {}
    I = np.eye(128, dtype=np.float32)
    r = np.arange(128)
    c["c_ident"] = I
    c["c_identb"] = I.copy()
    c["c_ones"] = np.ones((128, 128), np.float32)
    c["c_onesrow"] = np.ones((1, 128), np.float32)
    c["c_negones"] = np.full((128, 1), -1.0, np.float32)
    c["c_onesr"] = np.ones((128, 128), np.float32)
    c["c_negbig"] = (-1e30 * I).astype(np.float32)
    c["c_low01"] = (r[:, None] > r[None, :]).astype(np.float32)
    c["c_upp01"] = (r[:, None] < r[None, :]).astype(np.float32)
    c["c_i32"] = np.eye(32, dtype=np.float32)
    c["c_uppib"] = (r[:, None] <= r[None, :]).astype(np.float32)
    c["c_fvals"] = np.tile(np.array([[1e-5, -2.772588722239781]], np.float32),
                           (128, 1))
    return c


CONST_SPECS = [("c_ident", (128, 128), F32), ("c_identb", (128, 128), BF16),
               ("c_ones", (128, 128), F32), ("c_onesrow", (1, 128), F32),
               ("c_negones", (128, 1), F32R), ("c_onesr", (128, 128), F32R), ("c_negbig", (128, 128), F32),
               ("c_low01", (128, 128), F32), ("c_upp01", (128, 128), F32),
               ("c_i32", (32, 32), F32), ("c_uppib", (128, 128), BF16)]


LN16 = 2.772588722239781
FCONSTS = [1e-5, -LN16]


def declare_consts(nc, al, p):
    cs = {}
    for name, shape, dt_ in CONST_SPECS:
        d = nc.dram_tensor(name, list(shape), F32, kind="ExternalInput").ap()
        t = al.sbuf(name + "_s", shape, dt_)
        q = "POOL" if dt_ != F32 else "SP"
        p.dma(q, t[:], d, note=f"ld{name}")
        cs[name] = t
    d = nc.dram_tensor("c_fvals", [128, len(FCONSTS)], F32,
                       kind="ExternalInput").ap()
    t = al.sbuf("c_fvals_s", (128, len(FCONSTS)), F32)
    p.dma("SP", t[:], d, note="ldfvals")
    for i, v in enumerate(FCONSTS):
        nc.const_aps.register(F32, v, t[:, i:i + 1])
    cs["c_fvals"] = t
    return cs



def emit_ln_stats(p, cs, b, x, meanb, rstdb, rstd_tm, tag, xfn=None):
    ones = cs["c_ones"]
    gscr = b["gscr"]
    ps_s, ps_q, ps_t = b["ps4"], b["ps5"], b["ps6"]
    if xfn is None:
        xfn = lambda pc, t0, n: xi_ap(x, pc, t0, n)
    for tc in range(4):
        t0 = tc * 512
        for half in range(2):
            h0 = half * 256
            for pc in range(PD):
                xs = xfn(pc, t0 + h0, 256)
                sq = gscr[:, pc * 256:pc * 256 + 256]
                p.op("ACT", lambda e, xs=xs, sq=sq: e.activation(sq, xs,
                                                                 AF.Square),
                    reads=[xs], writes=[sq], note=f"{tag}sq")
            for pc in range(PD):
                sq = gscr[:, pc * 256:pc * 256 + 256]
                p.op("PE", lambda e, sq=sq, pc=pc, h0=h0: e.matmul(
                    ps_q[:, h0:h0 + 256], ones[:], sq,
                    start=(pc == 0), stop=(pc == PD - 1)),
                    reads=[ones[:], sq], writes=[ps_q[:]], note=f"{tag}ssq")
        for pc in range(PD):
            xs = xfn(pc, t0, 512)
            p.op("PE", lambda e, xs=xs, pc=pc: e.matmul(
                ps_s[:], ones[:], xs, start=(pc == 0),
                stop=(pc == PD - 1)),
                reads=[ones[:], xs], writes=[ps_s[:]], note=f"{tag}sum")
        mb = meanb[:, t0:t0 + 512]
        rb = rstdb[:, t0:t0 + 512]
        tv = gscr[:, 0:512]
        p.op("ACT", lambda e, mb=mb: e.activation(w32(mb), ps_s[:],
                                                  AF.Identity, scale=1.0 / D),
             reads=[ps_s[:]], writes=[mb], note=f"{tag}mean")
        p.op("ACT", lambda e, mb=mb, tv=tv: e.activation(tv, mb, AF.Square),
             reads=[mb], writes=[tv], note=f"{tag}msq")
        p.op("DVE", lambda e, tv=tv: e.scalar_tensor_tensor(
            tv, ps_q[:], 1.0 / D, tv, ALU.mult, ALU.subtract),
            reads=[ps_q[:], tv], writes=[tv], note=f"{tag}var")
        p.op("ACT", lambda e, tv=tv: e.activation(tv, tv, AF.Ln, bias=1e-5),
             reads=[tv, cs["c_fvals"][:]], writes=[tv], note=f"{tag}lnv")
        p.op("ACT", lambda e, tv=tv, rb=rb: e.activation(rb, tv, AF.Exp,
                                                         scale=-0.5),
             reads=[tv], writes=[rb], note=f"{tag}rstd")
        for c4 in range(4):
            cc = tc * 4 + c4
            sl = rstdb[:, t0 + c4 * 128:t0 + c4 * 128 + 128]
            p.op("PE", lambda e, sl=sl, c4=c4: e.transpose(
                ps_t[:, c4 * 128:c4 * 128 + 128], sl, cs["c_ident"][:]),
                reads=[sl, cs["c_ident"][:]], writes=[ps_t[:]],
                note=f"{tag}rtT")
        for c4 in range(4):
            cc = tc * 4 + c4
            p.op("DVE", lambda e, cc=cc, c4=c4: e.tensor_copy(
                rstd_tm[:, cc:cc + 1], ps_t[:, c4 * 128:c4 * 128 + 1]),
                reads=[ps_t[:]], writes=[rstd_tm[:, cc:cc + 1]],
                note=f"{tag}rtC")
        p.op("DVE", lambda e, mb=mb, rb=rb: e.tensor_mul(w32(mb), mb, rb),
             reads=[mb, rb], writes=[mb], note=f"{tag}mr")


def emit_l0a_passA(p, cs, b, x, rstdb, mr, wi_d, zx_d, tag):
    negones = cs["c_negones"]
    wim = b["wim"]
    zstg = b["zstg"]
    nws = b["nws"]
    for m in range(35):
        mw = 128 if m < 34 else 32
        sl = wim[:, (m % 2) * 1024:(m % 2) * 1024 + 1024]
        sl3 = sl.rearrange("p (k w) -> p k w", k=8)[:, :, 0:mw]
        p.dma("POOL", sl3.bitcast(F32R), wi_d.rearrange("(k p) j -> p k j",
              p=128)[:, :, m * 128:m * 128 + mw], note=f"{tag}wiA{m}")
        for k in range(8):
            wk = sl[:, k * 128:k * 128 + mw]
            p.op("PE", lambda e, wk=wk, k=k, mw=mw: e.matmul(
                b["ps4"][0:1, 0:mw], negones[:], r32(wk),
                start=(k == 0), stop=(k == 7)),
                reads=[negones[:], wk], writes=[b["ps4"][:]], note=f"{tag}ws{m}")
        p.op("DVE", lambda e, mw=mw: e.tensor_copy(w32(nws[0:1, 0:mw]),
                                                   b["ps4"][0:1, 0:mw]),
             reads=[b["ps4"][:]], writes=[nws[:]], note=f"{tag}wsc{m}")
        for tc in range(4):
            ps = b[f"ps{tc}"]
            for k in range(8):
                wk = sl[:, k * 128:k * 128 + mw]
                xs = xi_ap(x, k, tc * 512, 512)
                p.op("PE", lambda e, wk=wk, xs=xs, ps=ps, k=k, mw=mw: e.matmul(
                    ps[0:mw, :], r32(wk), r32(xs), start=(k == 0), stop=False),
                    reads=[wk, xs], writes=[ps[:]], note=f"{tag}mm{m}")
            mrr = mr[0:1, tc * 512:tc * 512 + 512]
            p.op("PE", lambda e, mrr=mrr, ps=ps, mw=mw: e.matmul(
                ps[0:mw, :], r32(nws[0:1, 0:mw]), r32(mrr), start=False,
                stop=True),
                reads=[nws[:], mrr], writes=[ps[:]], note=f"{tag}r1{m}")
            stg = zstg[:, tc * 512:tc * 512 + 512]
            rs = rstdb[:, tc * 512:tc * 512 + 512]
            p.op("DVE", lambda e, ps=ps, stg=stg, rs=rs, mw=mw: e.tensor_mul(
                stg[0:mw, :], ps[0:mw, :], rs[0:mw, :]),
                reads=[ps[:], rs], writes=[stg], note=f"{tag}ev{m}")
        p.dma("SP", zx_d[m * 128:m * 128 + mw, :], zstg[0:mw, :],
              note=f"{tag}zxo{m}")


def emit_wo_precast(p, b, wo_d, wo_bf_d, tag):
    stg = b["wos"]
    for i in range(16):
        sl = stg[:, (i % 2) * 1024:(i % 2) * 1024 + 1024]
        p.dma("POOL", sl, wo_d[i * 128:i * 128 + 128, :], note=f"{tag}wc{i}")
        p.dma("SP", wo_bf_d[i * 128:i * 128 + 128, :], sl, note=f"{tag}ws{i}")


def emit_l0a_passB(p, cs, b, ws, zx_d, yf_d, xm_d, dirn, xres, tag):
    fwd = (dirn == 0)
    m32a, m32b = b["m32a"], b["m32b"]
    gscr = b["gscr"]
    dtp = m32a[0:32, :]
    cl = m32b[0:32, :]
    t1 = gscr[0:32, 0:2048]
    t2 = b["u8k"][0:32, 0:2048]

    dtr = b["dtr"]
    p.dma("SP", dtr[:], zx_d[4352:4384, :], note=f"{tag}dtld")
    p.op("DVE", lambda e: e.tensor_scalar(t1, dtr[:], ws["dtb"][:, 0:1], None,
                                          ALU.add),
         reads=[dtr[:], ws["dtb"][:]], writes=[t1], note=f"{tag}dt+b")
    p.op("DVE", lambda e: e.tensor_scalar_max(t2, t1, 0.0),
         reads=[t1], writes=[t2], note=f"{tag}relu")
    p.op("ACT", lambda e: e.activation(dtp, t1, AF.Abs),
         reads=[t1], writes=[dtp], note=f"{tag}abs")
    p.op("ACT", lambda e: e.activation(dtp, dtp, AF.Exp, scale=-1.0),
         reads=[dtp], writes=[dtp], note=f"{tag}exp")
    p.op("ACT", lambda e: e.activation(dtp, dtp, AF.Ln, bias=1.0),
         reads=[dtp], writes=[dtp], note=f"{tag}ln1p")
    p.op("DVE", lambda e: e.tensor_add(dtp, dtp, t2),
         reads=[dtp, t2], writes=[dtp], note=f"{tag}sp")
    nea = b["nea"]
    p.op("ACT", lambda e: e.activation(nea[:, 0:1], ws["alog"][:, 0:1], AF.Exp),
         reads=[ws["alog"][:]], writes=[nea[:]], note=f"{tag}ea")
    p.op("DVE", lambda e: e.tensor_scalar_mul(nea[:, 0:1], nea[:, 0:1], -1.0),
         reads=[nea[:]], writes=[nea[:]], note=f"{tag}nea")
    p.op("DVE", lambda e: e.tensor_scalar(t1, dtp, nea[:, 0:1], None, ALU.mult),
         reads=[dtp, nea[:]], writes=[t1], note=f"{tag}L")
    for cc in range(16):
        s = slice(cc * 128, cc * 128 + 128)
        def scan(e, s=s):
            i = t1[:, s] if fwd else t1[:, s][:, ::-1]
            o = cl[:, s] if fwd else cl[:, s][:, ::-1]
            return e.tensor_tensor_scan(o, i, i, 0.0, ALU.add, ALU.bypass)
        p.op("DVE", scan, reads=[t1[:, s]], writes=[cl[:, s]],
             note=f"{tag}cs{cc}")
    p.op("DVE", lambda e: e.tensor_mul(
        b["i32s"][:], cs["c_i32"][:],
        ws["dvec"][:, 0:1].broadcast_to((32, 32))),
        reads=[cs["c_i32"][:], ws["dvec"][:]], writes=[b["i32s"][:]],
        note=f"{tag}dsel")
    p.op("PE", lambda e: e.matmul(b["ps5"][:, 0:32], cs["c_ones"][0:32, :],
                                  b["i32s"][:], start=True, stop=True),
         reads=[cs["c_ones"][:], b["i32s"][:]], writes=[b["ps5"][:]],
         note=f"{tag}dbm")
    p.op("DVE", lambda e: e.tensor_copy(gscr[:, 0:32], b["ps5"][:, 0:32]),
         reads=[b["ps5"][:]], writes=[gscr[:, 0:32]], note=f"{tag}dbr")
    p.op("DVE", lambda e: e.tensor_copy(
        b["dbt"][:, :].rearrange("p (h w) -> p h w", w=64),
        gscr[:, 0:32].unsqueeze(2).broadcast_to((128, 32, 64))),
        reads=[gscr[:, 0:32]], writes=[b["dbt"][:]], note=f"{tag}dbc")
    p.op("DVE", lambda e: e.memset(b["hst"][:], 0.0),
         reads=[], writes=[b["hst"][:]], note=f"{tag}h0")

    supers = range(4) if fwd else range(3, -1, -1)
    zx3 = zx_d.rearrange("(m p) t -> p m t", p=128)
    doff = 3 if fwd else 0
    xbc3 = b["xbc"][:, :].rearrange("p (m w) -> p m w", w=516)
    for S in supers:
        s0 = S * 512
        if fwd:
            if S == 0:
                p.op("DVE", lambda e: e.memset(xbc3[:, :, 0:3], 0.0),
                     reads=[], writes=[b["xbc"][:]], note=f"{tag}halo0")
                p.dma("SP", xbc3[:, :, 3:515], zx3[:, 16:34, s0:s0 + 512],
                      note=f"{tag}xbcl{S}")
            else:
                p.dma("SP", xbc3[:, :, 0:515], zx3[:, 16:34, s0 - 3:s0 + 512],
                      note=f"{tag}xbcl{S}")
        else:
            if S == 3:
                p.op("DVE", lambda e: e.memset(xbc3[:, :, 512:515], 0.0),
                     reads=[], writes=[b["xbc"][:]], note=f"{tag}halo1")
                p.dma("SP", xbc3[:, :, 0:512], zx3[:, 16:34, s0:s0 + 512],
                      note=f"{tag}xbcl{S}")
            else:
                p.dma("SP", xbc3[:, :, 0:515], zx3[:, 16:34, s0:s0 + 515],
                      note=f"{tag}xbcl{S}")
        for hh in range(2):
            mb = slice(hh * 9, hh * 9 + 9)
            csc = b["cscr"][:, :].rearrange("p (m w) -> p m w", w=512)
            csc2 = b["ygT"][:, 0:4608].rearrange("p (m w) -> p m w", w=512)
            for d in range(4):
                wcol = d if fwd else 3 - d
                wb = ws["convw"][:, wcol::4][:, mb]
                wbc = wb.unsqueeze(2).broadcast_to((128, 9, 512))
                src = xbc3[:, mb, d:d + 512]
                if d == 0:
                    p.op("DVE", lambda e, src=src, wbc=wbc, csc=csc:
                         e.tensor_tensor(csc[:, 0:9, :], src, wbc, ALU.mult),
                         reads=[b["xbc"][:], ws["convw"][:]],
                         writes=[b["cscr"][:]], note=f"{tag}cv0")
                else:
                    p.op("DVE", lambda e, src=src, wbc=wbc, csc2=csc2:
                         e.tensor_tensor(csc2[:, 0:9, :], src, wbc, ALU.mult),
                         reads=[b["xbc"][:], ws["convw"][:]],
                         writes=[b["ygT"][:]], note=f"{tag}cvm{d}")
                    p.op("DVE", lambda e, csc=csc, csc2=csc2:
                         e.tensor_add(csc[:, 0:9, :], csc[:, 0:9, :],
                                      csc2[:, 0:9, :]),
                         reads=[b["cscr"][:], b["ygT"][:]],
                         writes=[b["cscr"][:]], note=f"{tag}cva{d}")
            cb = ws["convb"][:, mb].unsqueeze(2).broadcast_to((128, 9, 512))
            p.op("DVE", lambda e, csc=csc, cb=cb: e.tensor_add(
                csc[:, 0:9, :], csc[:, 0:9, :], cb),
                reads=[ws["convb"][:], b["cscr"][:]], writes=[b["cscr"][:]],
                note=f"{tag}cvb")
            dst = xbc3[:, mb, doff:doff + 512]
            p.op("ACT", lambda e, dst=dst, csc=csc: e.activation(
                dst, csc[:, 0:9, :], AF.Silu),
                reads=[b["cscr"][:]], writes=[b["xbc"][:]], note=f"{tag}cvs")
        chunks = range(4) if fwd else range(3, -1, -1)
        for c in chunks:
            emit_l0a_chunk(p, cs, b, ws, fwd, 4 * S + c, doff + c * 128,
                           xbc3, zx3, tag)
        if not fwd:
            p.dma("SP", b["zz"][:, :].rearrange("p (m w) -> p m w", w=512),
                  yf_d.rearrange("(m p) t -> p m t", p=128)[:, :, s0:s0 + 512],
                  note=f"{tag}yfl{S}")
        emit_l0a_super_out(p, cs, b, ws, fwd, S, yf_d, xm_d, xres, tag)


def emit_l0a_chunk(p, cs, b, ws, fwd, cc, off_c, xbc3, zx3, tag):
    m32a, m32b = b["m32a"], b["m32b"]
    gscr = b["gscr"]
    tmv, eclcb, clcb = b["tmv"], b["eclcb"], b["clcb"]
    ps0, ps1, ps2, ps3, ps6 = b["ps0"], b["ps1"], b["ps2"], b["ps3"], b["ps6"]
    pb7 = b["pb7"]
    ccol = cc * 128
    csl = slice(ccol, ccol + 128)
    dtp = m32a[0:32, :]
    cl = m32b[0:32, :]
    xbaB = xbc3[:, 16, off_c:off_c + 128]
    xbaC = xbc3[:, 17, off_c:off_c + 128]
    zz = b["zz"]
    zld, zsl = zz[:, 0:2048], zz[:, 2048:4096]

    p.dma("SP", zld.rearrange("p (m w) -> p m w", w=128), zx3[:, 0:16, csl],
          note=f"{tag}zld{cc}")
    p.op("ACT", lambda e: e.activation(zsl, zld, AF.Silu),
         reads=[zld], writes=[zsl], note=f"{tag}zsl{cc}")

    CLc, NCLc, DTPc, WFc, ECLc = 0, 32, 64, 96, 128
    for i, s_ in enumerate([cl[:, csl], dtp[:, csl]]):
        p.op("PE", lambda e, s_=s_, i=i: e.transpose(
            ps6[:, i * 64:i * 64 + 32], s_, cs["c_i32"][:]),
            reads=[s_, cs["c_i32"][:]], writes=[ps6[:]], note=f"{tag}tp{cc}")
    p.op("DVE", lambda e: e.tensor_copy(tmv[:, 0:32], ps6[:, 0:32]),
         reads=[ps6[:]], writes=[tmv[:, 0:32]], note=f"{tag}tmv{cc}")
    p.op("DVE", lambda e: e.tensor_copy(tmv[:, 64:96], ps6[:, 64:96]),
         reads=[ps6[:]], writes=[tmv[:, 64:96]], note=f"{tag}tmd{cc}")
    p.op("DVE", lambda e: e.tensor_scalar_mul(tmv[:, 32:64], tmv[:, 0:32],
                                              -1.0),
         reads=[tmv[:, 0:32]], writes=[tmv[:, 32:64]], note=f"{tag}ncl{cc}")
    p.op("ACT", lambda e: e.activation(tmv[:, 128:160], tmv[:, 0:32], AF.Exp),
         reads=[tmv[:, 0:32]], writes=[tmv[:, 128:160]], note=f"{tag}ecl{cc}")

    gcol2 = ccol + (127 if fwd else 0)
    p.op("DVE", lambda e: e.tensor_mul(
        b["i32s"][:], cs["c_i32"][:],
        cl[:, gcol2:gcol2 + 1].broadcast_to((32, 32))),
        reads=[cs["c_i32"][:], cl[:, gcol2:gcol2 + 1]],
        writes=[b["i32s"][:]], note=f"{tag}ec32{cc}")
    p.op("PE", lambda e: e.matmul(ps6[:, 160:192], cs["c_ones"][0:32, :],
                                  b["i32s"][:], start=True, stop=True),
         reads=[cs["c_ones"][:], b["i32s"][:]], writes=[ps6[:]],
         note=f"{tag}ecb{cc}")
    p.op("DVE", lambda e: e.tensor_copy(clcb[:], ps6[:, 160:192]),
         reads=[ps6[:]], writes=[clcb[:]], note=f"{tag}ecc{cc}")
    p.op("ACT", lambda e: e.activation(eclcb[:], clcb[:], AF.Exp),
         reads=[clcb[:]], writes=[eclcb[:]], note=f"{tag}ece{cc}")
    p.op("DVE", lambda e: e.tensor_sub(tmv[:, 96:128], clcb[:], tmv[:, 0:32]),
         reads=[clcb[:], tmv[:, 0:32]], writes=[tmv[:, 96:128]],
         note=f"{tag}wf1{cc}")
    p.op("ACT", lambda e: e.activation(tmv[:, 96:128], tmv[:, 96:128], AF.Exp),
         reads=[tmv[:, 96:128]], writes=[tmv[:, 96:128]], note=f"{tag}wf2{cc}")
    p.op("DVE", lambda e: e.tensor_mul(tmv[:, 96:128], tmv[:, 96:128],
                                       tmv[:, 64:96]),
         reads=[tmv[:, 96:128], tmv[:, 64:96]], writes=[tmv[:, 96:128]],
         note=f"{tag}wf3{cc}")

    p.op("PE", lambda e: e.matmul(ps3[:, 0:128], xbaB, xbaC, start=True,
                                  stop=True),
         reads=[b["xbc"][:]], writes=[ps3[:]], note=f"{tag}st{cc}")
    p.op("PE", lambda e: e.transpose(pb7[:, 0:128], xbaB, cs["c_identb"][:]),
         reads=[b["xbc"][:], cs["c_identb"][:]], writes=[pb7[:]],
         note=f"{tag}btT{cc}")
    p.op("DVE", lambda e: e.tensor_copy(b["btm"][:], pb7[:, 0:128]),
         reads=[pb7[:]], writes=[b["btm"][:]], note=f"{tag}btC{cc}")
    for g4 in range(4):
        for j in range(4):
            hp = g4 * 4 + j
            src = xbc3[:, hp, off_c:off_c + 128]
            p.op("PE", lambda e, src=src, j=j: e.transpose(
                pb7[:, 512 + j * 128:512 + j * 128 + 128], src,
                cs["c_identb"][:]),
                reads=[b["xbc"][:], cs["c_identb"][:]], writes=[pb7[:]],
                note=f"{tag}xT{cc}")
        dst = b["xT"][:, g4 * 512:g4 * 512 + 512]
        p.op("DVE", lambda e, dst=dst: e.tensor_copy(dst, pb7[:, 512:1024]),
             reads=[pb7[:]], writes=[dst], note=f"{tag}xTe{cc}")
        h0 = g4 * 8
        wfb = tmv[:, WFc + h0:WFc + h0 + 8].unsqueeze(2).broadcast_to(
            (128, 8, 64))
        dstw = b["xW"][:, g4 * 512:g4 * 512 + 512].rearrange(
            "p (h w) -> p h w", w=64)
        p.op("DVE", lambda e, dstw=dstw, wfb=wfb: e.tensor_tensor(
            dstw, pb7[:, 512:1024].rearrange("p (h w) -> p h w", w=64), wfb,
            ALU.mult),
            reads=[pb7[:], tmv[:]], writes=[b["xW"][:]], note=f"{tag}xW{cc}")

    for h in range(32):
        gsl_ = ps6[:, 192 + (h % 2) * 128:192 + (h % 2) * 128 + 128]
        esl = b["egr"][:, (h % 2) * 128:(h % 2) * 128 + 128]
        msl = b["mt"][:, (h % 2) * 128:(h % 2) * 128 + 128]
        clb = tmv[:, CLc + h:CLc + h + 1].broadcast_to((128, 128))
        p.op("PE", lambda e, gsl_=gsl_, clb=clb: e.transpose(
            gsl_, clb, cs["c_ident"][:]),
            reads=[tmv[:], cs["c_ident"][:]], writes=[ps6[:]],
            note=f"{tag}g1h{h}")
        mask = cs["c_low01"] if fwd else cs["c_upp01"]
        p.op("PE", lambda e, gsl_=gsl_, mask=mask: e.matmul(
            gsl_, cs["c_negbig"][:], mask[:], start=False, stop=True),
            reads=[cs["c_negbig"][:], mask[:]], writes=[ps6[:]],
            note=f"{tag}g2h{h}")
        p.op("ACT", lambda e, gsl_=gsl_, esl=esl, h=h: e.activation(
            esl, gsl_, AF.Exp, bias=tmv[:, NCLc + h:NCLc + h + 1]),
            reads=[ps6[:], tmv[:]], writes=[esl], note=f"{tag}Eh{h}")
        p.op("DVE", lambda e, esl=esl, msl=msl, h=h: e.scalar_tensor_tensor(
            msl, esl, tmv[:, DTPc + h:DTPc + h + 1], ps3[:, 0:128],
            ALU.mult, ALU.mult),
            reads=[esl, tmv[:], ps3[:]], writes=[msl], note=f"{tag}Mh{h}")
        hs = slice(h * 64, h * 64 + 64)
        qs = slice((h % 8) * 64, (h % 8) * 64 + 64)
        p.op("PE", lambda e, msl=msl, hs=hs, qs=qs: e.matmul(
            ps0[:, qs], msl, b["xT"][:, hs], start=True, stop=True),
            reads=[msl, b["xT"][:, hs]], writes=[ps0[:]], note=f"{tag}yih{h}")
        p.op("PE", lambda e, hs=hs, qs=qs: e.matmul(
            ps1[:, qs], xbaC, b["hst"][:, hs], start=True, stop=True),
            reads=[b["xbc"][:], b["hst"][:, hs]], writes=[ps1[:]],
            note=f"{tag}yoh{h}")
        p.op("PE", lambda e, hs=hs, qs=qs: e.matmul(
            ps2[:, qs], b["btm"][:], b["xW"][:, hs], start=True, stop=True),
            reads=[b["btm"][:], b["xW"][:, hs]], writes=[ps2[:]],
            note=f"{tag}hah{h}")
        if h % 8 == 7:
            h0 = h - 7
            hgs = slice(h0 * 64, h0 * 64 + 512)
            hv = b["hst"][:, hgs].rearrange("p (h w) -> p h w", w=64)
            ecb = eclcb[:, h0:h0 + 8].unsqueeze(2).broadcast_to((128, 8, 64))
            p.op("DVE", lambda e, hv=hv, ecb=ecb: e.tensor_tensor(
                hv, hv, ecb, ALU.mult),
                reads=[b["hst"][:, hgs], eclcb[:]], writes=[b["hst"][:, hgs]],
                note=f"{tag}hd{h0}")
            p.op("DVE", lambda e, hgs=hgs: e.tensor_add(
                b["hst"][:, hgs], b["hst"][:, hgs], ps2[:]),
                reads=[b["hst"][:, hgs], ps2[:]], writes=[b["hst"][:, hgs]],
                note=f"{tag}hu{h0}")
            gg = slice(h0 * 64, h0 * 64 + 512)
            ecb2 = tmv[:, ECLc + h0:ECLc + h0 + 8].unsqueeze(2).broadcast_to(
                (128, 8, 64))
            ysc = gscr[:, 0:512]
            xdsc = gscr[:, 512:1024]
            p.op("DVE", lambda e, ecb2=ecb2, ysc=ysc: e.tensor_tensor(
                ysc.rearrange("p (h w) -> p h w", w=64),
                ps1[:].rearrange("p (h w) -> p h w", w=64), ecb2, ALU.mult),
                reads=[ps1[:], tmv[:]], writes=[ysc], note=f"{tag}q1_{h0}")
            p.op("DVE", lambda e, ysc=ysc: e.tensor_add(ysc, ysc, ps0[:]),
                 reads=[ysc, ps0[:]], writes=[ysc], note=f"{tag}q2_{h0}")
            p.op("DVE", lambda e, gg=gg, xdsc=xdsc: e.tensor_tensor(
                xdsc, b["xT"][:, gg], b["dbt"][:, gg], ALU.mult),
                reads=[b["xT"][:, gg], b["dbt"][:, gg]], writes=[xdsc],
                note=f"{tag}q3_{h0}")
            p.op("DVE", lambda e, gg=gg, ysc=ysc, xdsc=xdsc: e.tensor_add(
                b["ytm"][:, gg], ysc, xdsc),
                reads=[ysc, xdsc], writes=[b["ytm"][:, gg]],
                note=f"{tag}q4_{h0}")

    c_in_super = cc % 4
    for g8 in range(2):
        for j in range(8):
            k = g8 * 8 + j
            src = b["ytm"][:, k * 128:k * 128 + 128]
            p.op("PE", lambda e, src=src, j=j: e.transpose(
                pb7[:, j * 128:j * 128 + 128], src, cs["c_identb"][:]),
                reads=[src, cs["c_identb"][:]], writes=[pb7[:]],
                note=f"{tag}ygT{cc}")
        dst2 = b["ygT"][:, :].rearrange("p (k c w) -> p k c w", c=4, w=128)[
            :, g8 * 8:g8 * 8 + 8, c_in_super, :]
        zsl3 = zsl.rearrange("p (m w) -> p m w", w=128)[:, g8 * 8:g8 * 8 + 8, :]
        p.op("DVE", lambda e, dst2=dst2, zsl3=zsl3: e.tensor_tensor(
            dst2, pb7[:, 0:1024].rearrange("p (j w) -> p j w", w=128), zsl3,
            ALU.mult),
            reads=[pb7[:], zsl], writes=[b["ygT"][:]], note=f"{tag}ygE{cc}")


def emit_l0a_super_out(p, cs, b, ws, fwd, S, yf_d, xm_d, xres, tag):
    s0 = S * 512
    yf3 = yf_d.rearrange("(m p) t -> p m t", p=128)
    wo3 = ws["wo_bf_d"].rearrange("(k p) j -> p k j", p=128)
    pss = [b["ps0"], b["ps1"], b["ps2"], b["ps3"]]
    yfs = b["u8k"][:, :].bitcast(BF16)
    xmh = b["u8k"]
    for mg in range(2):
        for k in range(16):
            sl = b["wos"][:, (k % 2) * 512:(k % 2) * 512 + 512]
            p.dma("SP", sl, wo3[:, k, mg * 512:mg * 512 + 512],
                  note=f"{tag}wol{S}")
            rhs = b["ygT"][:, k * 512:k * 512 + 512]
            for m in range(4):
                lh = sl[:, m * 128:m * 128 + 128]
                p.op("PE", lambda e, lh=lh, rhs=rhs, m=m, k=k: e.matmul(
                    pss[m][:], lh, rhs, start=(k == 0), stop=(k == 15)),
                    reads=[sl, rhs], writes=[pss[m][:]], note=f"{tag}wo{S}")
        if fwd:
            for m in range(4):
                dst = yfs[:, m * 512:m * 512 + 512]
                p.op("DVE", lambda e, dst=dst, m=m: e.tensor_copy(dst,
                                                                  pss[m][:]),
                     reads=[pss[m][:]], writes=[dst], note=f"{tag}yfs{S}")
            p.dma("SP", yf3[:, mg * 4:mg * 4 + 4, s0:s0 + 512],
                  yfs[:, 0:2048].rearrange("p (m w) -> p m w", w=512),
                  note=f"{tag}yfo{S}")
        else:
            for m in range(4):
                mm = mg * 4 + m
                xsl = xmh[:, m * 512:m * 512 + 512]
                ysl = b["zz"][:, mm * 512:mm * 512 + 512]
                p.op("DVE", lambda e, xsl=xsl, ysl=ysl, m=m: e.tensor_add(
                    xsl, pss[m][:], ysl),
                    reads=[pss[m][:], ysl], writes=[xsl], note=f"{tag}xm{S}")
            p.dma("SP",
                  xm_d.rearrange("(m p) w -> p m w", p=128)[:, mg * 4:mg * 4 + 4, :],
                  xmh[:, :].rearrange("p (m w) -> p m w", w=512),
                  note=f"{tag}xmo{S}")
    if fwd:
        return
    wm3 = ws["wm_d"].rearrange("(k p) j -> p k j", p=128)
    for mg in range(2):
        for k in range(8):
            sl = b["wms"][:, 0:512]
            p.dma("POOL", sl, wm3[:, k, mg * 512:mg * 512 + 512],
                  note=f"{tag}wml{S}")
            xsl = b["xml"][:, (k % 2) * 512:(k % 2) * 512 + 512]
            p.dma("POOL", xsl, xm_d.rearrange("(g p) w -> p g w", p=128)[:, k, :],
                  note=f"{tag}xml{S}")
            for m in range(4):
                lh = sl[:, m * 128:m * 128 + 128]
                p.op("PE", lambda e, lh=lh, xsl=xsl, m=m, k=k: e.matmul(
                    pss[m][:], lh, xsl, start=(k == 0), stop=(k == 7)),
                    reads=[sl, xsl], writes=[pss[m][:]], note=f"{tag}wm{S}")
        for m in range(4):
            mm = mg * 4 + m
            th = b["th"][:, (m % 2) * 512:(m % 2) * 512 + 512]
            p.op("ACT", lambda e, th=th, mm=mm, m=m: e.activation(
                th, pss[m][:], AF.Tanh, bias=ws["bmr"][:, mm:mm + 1]),
                reads=[pss[m][:], ws["bmr"][:]], writes=[th], note=f"{tag}th{S}")
            xsl2 = b["gscr"][:, (m % 2) * 512:(m % 2) * 512 + 512]
            p.dma("POOL", xsl2, xm_d.rearrange("(g p) w -> p g w", p=128)[
                :, mm, :], note=f"{tag}xre{S}")
            tmp = b["gscr"][:, 1024 + (m % 2) * 512:1024 + (m % 2) * 512 + 512]
            p.op("DVE", lambda e, th=th, tmp=tmp, xsl2=xsl2:
                 e.scalar_tensor_tensor(tmp, th, 0.1, xsl2, ALU.mult, ALU.add),
                 reads=[th, xsl2], writes=[tmp], note=f"{tag}al{S}")
            xr = xi_ap(xres, mm, s0, 512)
            p.op("DVE", lambda e, tmp=tmp, xr=xr: e.tensor_add(
                w32(xr), xr, tmp),
                reads=[tmp, xr], writes=[xr], note=f"{tag}xr{S}")


def emit_load_x(p, cs, b, x_d, xres, tag="ldx"):
    xstg = b["zz"][:, :].bitcast(F32)
    pads = xres.rearrange("p (c w) -> p c w", w=XW)[:, :, 0:XOFF]
    p.op("DVE", lambda e: e.memset(pads, 0.0),
         reads=[], writes=[xres], note=f"{tag}pad")
    p.op("DVE", lambda e: e.tensor_copy(pads.bitcast(F32R).bitcast(F32)
                                        if False else w32(pads), pads),
         reads=[xres], writes=[xres], note=f"{tag}padr")
    for tt in range(16):
        stg = xstg[:, (tt % 2) * 1024:(tt % 2) * 1024 + 1024]
        p.dma("SP", stg, x_d[tt * 128:tt * 128 + 128, :], note=f"{tag}d{tt}")
        for g in range(2):
            for j in range(4):
                pc = g * 4 + j
                p.op("PE", lambda e, stg=stg, pc=pc, j=j: e.transpose(
                    b["ps6"][:, j * 128:j * 128 + 128],
                    stg[:, pc * 128:pc * 128 + 128], cs["c_ident"][:]),
                    reads=[stg, cs["c_ident"][:]], writes=[b["ps6"][:]],
                    note=f"{tag}t{tt}")
            for j in range(4):
                pc = g * 4 + j
                dst = xi_ap(xres, pc, tt * 128, 128)
                p.op("DVE", lambda e, dst=dst, j=j: e.tensor_copy(
                    w32(dst), b["ps6"][:, j * 128:j * 128 + 128]),
                    reads=[b["ps6"][:]], writes=[dst], note=f"{tag}e{tt}")


def emit_store_x(p, cs, b, xres, out_d, tag="stx"):
    xstg = b["zz"][:, :].bitcast(F32)
    for tt in range(16):
        stg = xstg[:, (tt % 2) * 1024:(tt % 2) * 1024 + 1024]
        for g in range(2):
            for j in range(4):
                pc = g * 4 + j
                src = xi_ap(xres, pc, tt * 128, 128)
                p.op("PE", lambda e, src=src, j=j: e.transpose(
                    b["ps6"][:, j * 128:j * 128 + 128], src, cs["c_ident"][:]),
                    reads=[src, cs["c_ident"][:]], writes=[b["ps6"][:]],
                    note=f"{tag}t{tt}")
            for j in range(4):
                pc = g * 4 + j
                p.op("DVE", lambda e, stg=stg, pc=pc, j=j: e.tensor_copy(
                    stg[:, pc * 128:pc * 128 + 128],
                    b["ps6"][:, j * 128:j * 128 + 128]),
                    reads=[b["ps6"][:]], writes=[stg], note=f"{tag}e{tt}")
        p.dma("SP", out_d[tt * 128:tt * 128 + 128, :], stg, note=f"{tag}d{tt}")


def alloc_main_bufs(al):
    b = {}
    b["xres"] = al.sbuf("xres", (128, 8 * 2052), F32)
    b["meanb"] = al.sbuf("meanb", (128, 2048), F32)
    b["rstdb"] = al.sbuf("rstdb", (128, 2048), F32)
    b["rstd_tm"] = al.sbuf("rstd_tm", (128, 16), F32)
    b["gscr"] = al.sbuf("gscr", (128, 2048), F32)
    b["m32a"] = al.sbuf("m32a", (128, 2048), F32)
    b["m32b"] = al.sbuf("m32b", (128, 2048), F32)
    b["dtr"] = al.sbuf("dtr", (32, 2048), BF16)
    b["nea"] = al.sbuf("nea", (32, 1), F32)
    b["i32s"] = al.sbuf("i32s", (32, 32), F32)
    b["nws"] = al.sbuf("nws", (1, 512), F32)
    b["dbt"] = al.sbuf("dbt", (128, 2048), BF16)
    b["xbc"] = al.sbuf("xbc", (128, 18 * 516), BF16)
    b["cscr"] = al.sbuf("cscr", (128, 4608), BF16)
    b["zz"] = al.sbuf("zz", (128, 4096), BF16)
    b["tmv"] = al.sbuf("tmv", (128, 160), F32)
    b["eclcb"] = al.sbuf("eclcb", (128, 32), F32)
    b["clcb"] = al.sbuf("clcb", (128, 32), F32)
    b["btm"] = al.sbuf("btm", (128, 128), BF16)
    b["xT"] = al.sbuf("xT", (128, 2048), BF16)
    b["xW"] = al.sbuf("xW", (128, 2048), BF16)
    b["mt"] = al.sbuf("mt", (128, 256), BF16)
    b["egr"] = al.sbuf("egr", (128, 256), F32)
    b["hst"] = al.sbuf("hst", (128, 2048), BF16)
    b["ygT"] = al.sbuf("ygT", (128, 8192), BF16)
    b["u8k"] = al.sbuf("u8k", (128, 2048), F32)
    b["wos"] = al.sbuf("wos", (128, 2048), BF16)
    b["wms"] = al.sbuf("wms", (128, 512), BF16)
    b["xml"] = al.sbuf("xml", (128, 1024), BF16)
    b["wim"] = al.sbuf("wim", (128, 2048), F32)
    b["zstg"] = b["dbt"]
    b["ytm"] = b["cscr"]
    b["th"] = b["u8k"]
    for i in range(7):
        b[f"ps{i}"] = al.psum(f"ps{i}", (128, 512), F32)
    b["pb7"] = al.psum("pb7", (128, 1024), BF16)
    return b


def host_l0a_weights(inp):
    w = {}
    for sfx in ("f", "b"):
        w[f"wi_{sfx}"] = np.ascontiguousarray(
            np.asarray(inp[f"L0_attn_Wi_{sfx}"], np.float32))
        conv = np.asarray(inp[f"L0_attn_conv_{sfx}"], np.float32)
        w[f"convw_{sfx}"] = np.ascontiguousarray(
            conv.T.reshape(18, 128, 4).transpose(1, 0, 2).reshape(128, 72))
        convb = np.asarray(inp[f"L0_attn_convb_{sfx}"], np.float32)
        w[f"convb_{sfx}"] = np.ascontiguousarray(convb.reshape(18, 128).T)
        w[f"alog_{sfx}"] = np.asarray(
            inp[f"L0_attn_Alog_{sfx}"], np.float32).reshape(32, 1)
        w[f"dtb_{sfx}"] = np.asarray(
            inp[f"L0_attn_dtb_{sfx}"], np.float32).reshape(32, 1)
        w[f"dvec_{sfx}"] = np.asarray(
            inp[f"L0_attn_D_{sfx}"], np.float32).reshape(32, 1)
        w[f"wo_{sfx}"] = np.ascontiguousarray(
            np.asarray(inp[f"L0_attn_Wo_{sfx}"], np.float32))
    w["wm"] = np.ascontiguousarray(np.asarray(inp["L0_attn_Wm"], np.float32))
    w["bmr"] = np.ascontiguousarray(
        np.asarray(inp["L0_attn_bm"], np.float32).reshape(8, 128).T)
    return w


L0A_WSPECS = [("wi_f", (1024, 4384)), ("convw_f", (128, 72)),
              ("convb_f", (128, 18)), ("alog_f", (32, 1)), ("dtb_f", (32, 1)),
              ("dvec_f", (32, 1)), ("wo_f", (2048, 1024)),
              ("wi_b", (1024, 4384)), ("convw_b", (128, 72)),
              ("convb_b", (128, 18)), ("alog_b", (32, 1)), ("dtb_b", (32, 1)),
              ("dvec_b", (32, 1)), ("wo_b", (2048, 1024)),
              ("wm", (1024, 1024)), ("bmr", (128, 8))]


def emit_l0_attn(p, al, cs, b, wd, tag="l0a"):
    nc = p.nc
    zx_d = wd["zx_d"]
    yf_d = wd["yf_d"]
    xm_d = wd["xm_d"]
    wo_bf_d = wd["wo_bf_d"]
    emit_ln_stats(p, cs, b, b["xres"][:], b["meanb"][:], b["rstdb"][:],
                  b["rstd_tm"][:], tag + "ln")
    SMALLW = [("convw", (128, 72)), ("convb", (128, 18)), ("alog", (32, 1)),
              ("dtb", (32, 1)), ("dvec", (32, 1))]
    for nm, shape in SMALLW:
        b[f"{nm}_s"] = al.sbuf(f"{nm}_s", shape, F32)
    b["bmr_s"] = al.sbuf("bmr_s", (128, 8), F32)
    p.dma("POOL", b["bmr_s"][:], wd["bmr"], note="ldwbmr")
    for dirn, sfx in ((0, "f"), (1, "b")):
        for nm, shape in SMALLW:
            p.dma("POOL", b[f"{nm}_s"][:], wd[f"{nm}_{sfx}"],
                  note=f"ldw{nm}{sfx}")
        ws = {"alog": b["alog_s"], "dtb": b["dtb_s"],
              "dvec": b["dvec_s"], "convw": b["convw_s"],
              "convb": b["convb_s"], "wo_bf_d": wo_bf_d,
              "wm_d": wd["wm"], "bmr": b["bmr_s"]}
        emit_l0a_passA(p, cs, b, b["xres"][:], b["rstdb"][:], b["meanb"][:],
                       wd[f"wi_{sfx}"], zx_d, tag + "A" + sfx)
        emit_wo_precast(p, b, wd[f"wo_{sfx}"], wo_bf_d, tag + "W" + sfx)
        emit_l0a_passB(p, cs, b, ws, zx_d, yf_d, xm_d, dirn, b["xres"][:],
                       tag + "B" + sfx)


def build_l0a_only():
    nc = bass.Bass("TRN2", target_bir_lowering=False, debug=False)
    x_d = nc.dram_tensor("x", [2048, 1024], F32, kind="ExternalInput").ap()
    out_d = nc.dram_tensor("out", [2048, 1024], F32, kind="ExternalOutput").ap()
    wd = {}
    for nm, shape in L0A_WSPECS:
        wd[nm] = nc.dram_tensor(nm, list(shape), F32, kind="ExternalInput").ap()
    wd["zx_d"] = nc.dram_tensor("zx_d", [4480, 2048], BF16).ap()
    wd["yf_d"] = nc.dram_tensor("yf_d", [1024, 2048], BF16).ap()
    wd["xm_d"] = nc.dram_tensor("xm_d", [1024, 512], F32).ap()
    wd["wo_bf_d"] = nc.dram_tensor("wo_bf_d", [2048, 1024], BF16).ap()
    al = Alloc(nc)
    p = Prog(nc)
    cs = declare_consts(nc, al, p)
    b = alloc_main_bufs(al)
    emit_load_x(p, cs, b, x_d, b["xres"][:])
    emit_l0_attn(p, al, cs, b, wd)
    emit_store_x(p, cs, b, b["xres"][:], out_d)
    p.finalize()
    al.close()
    return nc


def kernel_l0a(core_inputs):
    nc = build_l0a_only()
    cmaps = []
    consts = host_consts()
    for ci in core_inputs:
        w = host_l0a_weights(ci)
        m = {"x": np.ascontiguousarray(np.asarray(ci["x"], np.float32))}
        for nm, _ in L0A_WSPECS:
            m[nm] = w[nm]
        for k, v in consts.items():
            m[k] = v
        cmaps.append(m)
    res = run_bass_kernel_spmd(nc, cmaps, core_ids=list(range(len(cmaps))))
    return [np.asarray(r["out"]) for r in res.results]



def host_l0f_weights(inp):
    w = {}
    w["wqkv"] = np.ascontiguousarray(np.asarray(inp["L0_ffn_Wqkv"], np.float32))
    w["wo6"] = np.ascontiguousarray(np.asarray(inp["L0_ffn_Wo"], np.float32))
    W1 = np.asarray(inp["L0_ffn_W1"], np.float32)
    w["w1a"] = np.ascontiguousarray(W1[:, :, :2730])
    w["w1g"] = np.ascontiguousarray(W1[:, :, 2730:])
    w["w26"] = np.ascontiguousarray(np.asarray(inp["L0_ffn_W2"], np.float32))
    b1 = np.asarray(inp["L0_ffn_b1"], np.float32)
    pad = np.zeros((6, 2816 - 2730), np.float32)
    w["b1a"] = np.ascontiguousarray(np.concatenate(
        [b1[:, :2730], pad], 1).reshape(6, 22, 128).transpose(0, 2, 1).copy())
    w["b1g"] = np.ascontiguousarray(np.concatenate(
        [b1[:, 2730:], pad], 1).reshape(6, 22, 128).transpose(0, 2, 1).copy())
    w["b2r"] = np.ascontiguousarray(
        np.asarray(inp["L0_ffn_b2"], np.float32).reshape(6, 8, 128)
        .transpose(0, 2, 1).copy())
    w["lng"] = np.ascontiguousarray(
        np.asarray(inp["L0_ffn_ln_g"], np.float32).reshape(6, 8, 128)
        .transpose(0, 2, 1).copy())
    w["lnb"] = np.ascontiguousarray(
        np.asarray(inp["L0_ffn_ln_b"], np.float32).reshape(6, 8, 128)
        .transpose(0, 2, 1).copy())
    w["qscale"] = np.ascontiguousarray(np.asarray(inp["L0_ffn_q_scale"], np.float32))
    w["kscale"] = np.ascontiguousarray(np.asarray(inp["L0_ffn_k_scale"], np.float32))
    return w


def host_l0f_consts():
    c = {}
    DIM_HEAD, WINDOW, n = 64, 256, 2048
    half = np.arange(0, DIM_HEAD, 2, dtype=np.float32)
    inv_freq = 1.0 / 10000.0 ** (half / DIM_HEAD)
    t = np.arange(n, dtype=np.float32)
    freqs = t[:, None] * inv_freq[None, :]
    freqs = np.concatenate([freqs, freqs], -1)
    base_scale = (half + 0.4 * DIM_HEAD) / (1.4 * DIM_HEAD)
    power = (t - n // 2) / (WINDOW // 2)
    scale = base_scale[None, :] ** power[:, None]
    scale = np.concatenate([scale, scale], -1)
    cos, sin = np.cos(freqs), np.sin(freqs)
    sgn = np.concatenate([-np.ones(32, np.float32), np.ones(32, np.float32)])
    c["rt_csq"] = (8.0 * cos * scale).astype(np.float32)
    c["rt_snq"] = (8.0 * sin * scale * sgn[None, :]).astype(np.float32)
    c["rt_csk"] = (cos / scale).astype(np.float32)
    c["rt_snk"] = (sin / scale * sgn[None, :]).astype(np.float32)
    r = np.arange(128)
    j = np.arange(256)
    for s in (0, 1):
        qi = 128 * s + r[:, None]
        prev_dis = (qi > j[None, :]).astype(np.float32)
        cur_dis = (j[None, :] > qi).astype(np.float32)
        c[f"c_mw{s}"] = np.concatenate([prev_dis, cur_dis], 1)
    c["c_negbigb"] = (-1e30 * np.eye(128)).astype(np.float32)
    c["c_onesb"] = np.ones((128, 256), np.float32)
    return c


L0F_CONST_SPECS = [("c_mw0", (128, 512), BF16), ("c_mw1", (128, 512), BF16),
                   ("c_negbigb", (128, 128), BF16), ("c_onesb", (128, 256), BF16)]
L0F_TABLE_SPECS = [("rt_csq", (128, 1024)), ("rt_snq", (128, 1024)),
                   ("rt_csk", (128, 1024)), ("rt_snk", (128, 1024))]


def emit_l0_ffn(p, al, cs, b, wd, tag="l0f"):
    xi = b["xi"][:]
    b["cw_m"] = (b["xml"][:, 0:512], b["xml"][:, 512:1024])
    b["cw_nb"] = b["wms"][:, 0:128]
    b["cw_on"] = b["wms"][:, 128:384]
    p.dma("POOL", b["cw_m"][0], wd["c_mw0"], note=f"{tag}cm0")
    p.dma("POOL", b["cw_m"][1], wd["c_mw1"], note=f"{tag}cm1")
    p.dma("POOL", b["cw_nb"], wd["c_negbigb"], note=f"{tag}cnb")
    p.dma("POOL", b["cw_on"], wd["c_onesb"], note=f"{tag}con")
    for l in range(6):
        emit_l0f_attn(p, cs, b, wd, l, f"{tag}a{l}")
        emit_l0f_ffn(p, cs, b, wd, l, f"{tag}f{l}")


def emit_l0f_attn(p, cs, b, wd, l, tag):
    xi = b["xi"][:]
    qkv_d = wd["qkv_d"]
    v_d = wd["v_d"]
    q_d = wd["q_d"]
    k_d = wd["k_d"]
    a_d = wd["a_d"]
    wqkv3 = wd["wqkv"].rearrange("l (k p) j -> l p k j", p=128)[l]
    gscr = b["gscr"]
    pss = [b["ps0"], b["ps1"], b["ps2"], b["ps3"]]
    pb7 = b["pb7"]

    qsr = b["qsr"]
    p.dma("POOL", qsr[0:1, 0:64], wd["qscale"][l:l + 1, :], note=f"{tag}qs")
    p.dma("POOL", qsr[0:1, 64:128], wd["kscale"][l:l + 1, :], note=f"{tag}ks")
    p.op("PE", lambda e: e.matmul(b["ps6"][:, 0:128], cs["c_onesrow"][:],
                                  qsr[0:1, 0:128], start=True, stop=True),
         reads=[cs["c_onesrow"][:], qsr], writes=[b["ps6"][:]],
         note=f"{tag}qsb")
    p.op("DVE", lambda e: e.tensor_copy(b["qsb"][:], b["ps6"][:, 0:128]),
         reads=[b["ps6"][:]], writes=[b["qsb"][:]], note=f"{tag}qsc")
    for nm, sl_, tb in (("csq", (0, 64), "rt_csq"), ("snq", (0, 64), "rt_snq"),
                        ("csk", (64, 128), "rt_csk"), ("snk", (64, 128), "rt_snk")):
        dst = b["rt_" + nm]
        qsl = b["qsb"][:, sl_[0]:sl_[1]]
        p.dma("POOL", dst[:, :], wd[tb], note=f"{tag}rtl{nm}")
        p.op("DVE", lambda e, dst=dst, qsl=qsl: e.tensor_tensor(
            dst[:, :].rearrange("p (c w) -> p c w", w=64),
            dst[:, :].rearrange("p (c w) -> p c w", w=64),
            qsl.unsqueeze(1).broadcast_to((128, 16, 64)), ALU.mult),
            reads=[dst[:], b["qsb"][:]], writes=[dst[:]], note=f"{tag}rt{nm}")

    for ts in range(4):
        for ng in range(6):
            for k in range(8):
                sl = b["wqs"][:, (k % 2) * 512:(k % 2) * 512 + 512]
                p.dma("POOL", sl.bitcast(F32R),
                      wqkv3[:, k, ng * 512:ng * 512 + 512], note=f"{tag}wq")
                for t in range(4):
                    lh = xi_ap(xi, k, ts * 512 + t * 128, 128, shifted=True)
                    p.op("PE", lambda e, sl=sl, lh=lh, t=t, k=k: e.matmul(
                        pss[t][:], r32(lh), r32(sl), start=(k == 0),
                        stop=(k == 7)),
                        reads=[sl, xi], writes=[pss[t][:]], note=f"{tag}qkv")
            for t in range(4):
                st = (b["qst"], b["kst"], b["vst"])[ng // 2]
                dst = st[:, t * 1024 + (ng % 2) * 512: t * 1024 + (ng % 2) * 512 + 512]
                p.op("DVE", lambda e, dst=dst, t=t: e.tensor_copy(dst, pss[t][:]),
                     reads=[pss[t][:]], writes=[dst], note=f"{tag}qkve")
        for t in range(4):
            ct = ts * 4 + t
            p.dma("SP", v_d[ct * 128:ct * 128 + 128, :],
                  b["vst"][:, t * 1024:t * 1024 + 1024], note=f"{tag}vsp{ct}")
            for qk in range(2):
                st = (b["qst"], b["kst"])[qk]
                dd = (q_d, k_d)[qk]
                cst = (b["rt_csq"], b["rt_csk"])[qk]
                snt = (b["rt_snq"], b["rt_snk"])[qk]
                src = st[:, t * 1024:t * 1024 + 1024]
                emit_l0f_prep(p, cs, b, src, cst, snt, ct, dd, f"{tag}p{qk}{ct}")
    for w in range(8):
        emit_l0f_window(p, cs, b, wd, l, w, tag)
    wo3 = wd["wo6"].rearrange("l (k p) j -> l p k j", p=128)[l]
    for tc in range(4):
        for mg in range(2):
            for k in range(8):
                asl = b["wqs"].bitcast(BF16)[:, (k % 2) * 1024:(k % 2) * 1024 + 512]
                p.dma("SP", asl, a_d.rearrange("(k p) t -> p k t", p=128)[
                    :, k, tc * 512:tc * 512 + 512], note=f"{tag}wral")
                wsl = b["wos"][:, (k % 2) * 512:(k % 2) * 512 + 512]
                p.dma("POOL", wsl, wo3[:, k, mg * 512:mg * 512 + 512],
                      note=f"{tag}wrwo")
                for m in range(4):
                    lh = wsl[:, m * 128:m * 128 + 128]
                    p.op("PE", lambda e, lh=lh, asl=asl, m=m, k=k: e.matmul(
                        pss[m][:], lh, asl, start=(k == 0), stop=(k == 7)),
                        reads=[wsl, asl], writes=[pss[m][:]], note=f"{tag}wo")
            for m in range(4):
                mm = mg * 4 + m
                xsl = xi_ap(xi, mm, tc * 512, 512)
                p.op("DVE", lambda e, xsl=xsl, m=m: e.tensor_add(
                    w32(xsl), xsl, pss[m][:]),
                    reads=[xsl, pss[m][:]], writes=[xsl], note=f"{tag}wor")


def emit_l0f_prep(p, cs, b, src, cst, snt, ct, dd, tag):
    gscr = b["gscr"]
    sq = gscr[:, 0:1024]
    p.op("ACT", lambda e: e.activation(sq, src, AF.Square),
         reads=[src], writes=[sq], note=f"{tag}sq")
    ss = b["sst"][:, 0:16]
    p.op("DVE", lambda e: e.reduce_sum(
        ss.unsqueeze(2), sq.rearrange("p (h w) -> p h w", w=64), AX.X),
        reads=[sq], writes=[ss], note=f"{tag}ssum")
    p.op("ACT", lambda e: e.activation(ss, ss, AF.Sqrt),
         reads=[ss], writes=[ss], note=f"{tag}sqrt")
    p.op("DVE", lambda e: e.tensor_scalar_max(ss, ss, 1e-12),
         reads=[ss], writes=[ss], note=f"{tag}mx")
    p.op("DVE", lambda e: e.reciprocal(ss, ss),
         reads=[ss], writes=[ss], note=f"{tag}rc")
    qa = b["qa"]
    p.op("DVE", lambda e: e.tensor_tensor(
        qa[:].rearrange("p (h w) -> p h w", w=64),
        src.rearrange("p (h w) -> p h w", w=64),
        ss.unsqueeze(2).broadcast_to((128, 16, 64)), ALU.mult),
        reads=[src, ss], writes=[qa[:]], note=f"{tag}nrm")
    qb = b["qb"]
    ctab = cst[:, ct * 64:ct * 64 + 64].unsqueeze(1).broadcast_to((128, 16, 64))
    stab = snt[:, ct * 64:ct * 64 + 64].unsqueeze(1).broadcast_to((128, 16, 64))
    p.op("DVE", lambda e: e.tensor_tensor(
        qb[:].rearrange("p (h w) -> p h w", w=64),
        qa[:].rearrange("p (h w) -> p h w", w=64), ctab, ALU.mult),
        reads=[qa[:], cst[:]], writes=[qb[:]], note=f"{tag}cs")
    qsw = qa[:].rearrange("p (h u v) -> p h u v", u=2, v=32)[:, :, ::-1, :]
    p.op("DVE", lambda e: e.tensor_tensor(
        qa[:].rearrange("p (h u v) -> p h u v", u=2, v=32) if False else
        b["qc"][:].rearrange("p (h u v) -> p h u v", u=2, v=32),
        qsw, stab.rearrange("p h (u v) -> p h u v", v=32), ALU.mult),
        reads=[qa[:], snt[:]], writes=[b["qc"][:]], note=f"{tag}sn")
    p.op("DVE", lambda e: e.tensor_add(qb[:], qb[:], b["qc"][:]),
         reads=[qb[:], b["qc"][:]], writes=[qb[:]], note=f"{tag}add")
    pb7 = b["pb7"]
    for j in range(8):
        blk = qb[:, j * 128:j * 128 + 128]
        p.op("PE", lambda e, blk=blk, j=j: e.transpose(
            pb7[:, j * 128:j * 128 + 128], blk, cs["c_identb"][:]),
            reads=[blk, cs["c_identb"][:]], writes=[pb7[:]], note=f"{tag}T")
    fmst = b["fmst"]
    p.op("DVE", lambda e: e.tensor_copy(fmst[:], pb7[:]),
         reads=[pb7[:]], writes=[fmst[:]], note=f"{tag}Te")
    p.dma("SP", dd.rearrange("(m p) t -> p m t", p=128)[:, :, ct * 128:ct * 128 + 128],
          fmst[:].rearrange("p (m t) -> p m t", t=128), note=f"{tag}sp")


def emit_l0f_window(p, cs, b, wd, l, w, tag):
    q_d, k_d, v_d, a_d = wd["q_d"], wd["k_d"], wd["v_d"], wd["a_d"]
    pb7 = b["pb7"]
    p.dma("SP", b["qw"][:].rearrange("p (m t) -> p m t", t=256),
          q_d.rearrange("(m p) t -> p m t", p=128)[:, :, w * 256:w * 256 + 256],
          note=f"{tag}lq{w}")
    ksl = b["kring"][:, (w % 2) * 2048:(w % 2) * 2048 + 2048]
    p.dma("SP", ksl.rearrange("p (m t) -> p m t", t=256),
          k_d.rearrange("(m p) t -> p m t", p=128)[:, :, w * 256:w * 256 + 256],
          note=f"{tag}lk{w}")
    vsl = b["vring"][:, (w % 2) * 2048:(w % 2) * 2048 + 2048]
    p.dma("SP", vsl.rearrange("p (s t) -> p s t", t=1024),
          v_d.rearrange("(s p) c -> p s c", p=128).rearrange(
              "p (w s) c -> p w s c", s=2)[:, w, :, :],
          note=f"{tag}lv{w}")
    att = b["att_tm"]
    for sub in range(2):
        for h in range(16):
            psS = b["ps4"]
            mask = b["cw_m"][sub]
            p.op("PE", lambda e, mask=mask: e.matmul(
                psS[:], b["cw_nb"], mask, start=True, stop=False),
                reads=[b["cw_nb"], mask], writes=[psS[:]],
                note=f"{tag}mk")
            if w == 0:
                p.op("PE", lambda e: e.matmul(
                    psS[:, 0:256], b["cw_nb"], b["cw_on"], start=False,
                    stop=False),
                    reads=[b["cw_nb"], b["cw_on"]], writes=[psS[:]],
                    note=f"{tag}mk0")
            pc, half = h // 2, (h % 2) * 64
            qsl = b["qw"][:, :].rearrange("p (m t) -> p m t", t=256)[
                :, pc, sub * 128:sub * 128 + 128][half:half + 64, :]
            if w > 0:
                kprev = b["kring"][:, ((w - 1) % 2) * 2048 + pc * 256:
                                   ((w - 1) % 2) * 2048 + pc * 256 + 256][
                    half:half + 64, :]
                p.op("PE", lambda e, qsl=qsl, kprev=kprev: e.matmul(
                    psS[:, 0:256], qsl, kprev, start=False, stop=False),
                    reads=[qsl, b["kring"][:]], writes=[psS[:]], note=f"{tag}s0")
            kcur = ksl.rearrange("p (m t) -> p m t", t=256)[:, pc, :][
                half:half + 64, :]
            p.op("PE", lambda e, qsl=qsl, kcur=kcur: e.matmul(
                psS[:, 256:512], qsl, kcur, start=False, stop=True),
                reads=[qsl, b["kring"][:]], writes=[psS[:]], note=f"{tag}s1")
            nmx = b["sst"][:, 16:17]
            p.op("DVE", lambda e, nmx=nmx: e.reduce_max(
                nmx.unsqueeze(2), psS[:].unsqueeze(1), AX.X, negate=True),
                reads=[psS[:]], writes=[nmx], note=f"{tag}mx")
            pst = b["pstl"]
            lsum = b["sst"][:, 17:18]
            def _exp(e, nmx=nmx, pst=pst, lsum=lsum):
                return e.activation(pst[:], psS[:], AF.Exp,
                                    bias=nmx, scale=1.0, accum_out=lsum)
            p.op("ACT", _exp, reads=[psS[:], nmx], writes=[pst[:], lsum],
                 note=f"{tag}ex")
            p.op("DVE", lambda e, lsum=lsum: e.reciprocal(
                b["sst"][:, 18:19], lsum),
                reads=[lsum], writes=[b["sst"][:, 18:19]], note=f"{tag}rc")
            for bb in range(4):
                blk = pst[:, bb * 128:bb * 128 + 128]
                p.op("PE", lambda e, blk=blk, bb=bb: e.transpose(
                    pb7[:, bb * 128:bb * 128 + 128], blk, cs["c_identb"][:]),
                    reads=[blk, cs["c_identb"][:]], writes=[pb7[:]],
                    note=f"{tag}pT")
            p.op("DVE", lambda e: e.tensor_copy(b["ptsb"][:], pb7[:, 0:512]),
                 reads=[pb7[:]], writes=[b["ptsb"][:]], note=f"{tag}pTe")
            psAV = b["ps5"]
            b0 = 2 if w == 0 else 0
            for bb in range(b0, 4):
                lh = b["ptsb"][:, bb * 128:bb * 128 + 128]
                slot = ((w - 1) % 2) if bb < 2 else (w % 2)
                vv = b["vring"][:, slot * 2048 + (bb % 2) * 1024 + h * 64:
                                slot * 2048 + (bb % 2) * 1024 + h * 64 + 64]
                p.op("PE", lambda e, lh=lh, vv=vv, bb=bb, b0=b0: e.matmul(
                    psAV[:, 0:64], lh, vv, start=(bb == b0), stop=(bb == 3)),
                    reads=[lh, b["vring"][:]], writes=[psAV[:]], note=f"{tag}av")
            dst = att[:, sub * 1024 + h * 64:sub * 1024 + h * 64 + 64]
            p.op("DVE", lambda e, dst=dst: e.tensor_scalar(
                dst, psAV[:, 0:64], b["sst"][:, 18:19], None, ALU.mult),
                reads=[psAV[:], b["sst"][:, 18:19]], writes=[dst],
                note=f"{tag}ave")
    for sub in range(2):
        for j in range(8):
            blk = att[:, sub * 1024 + j * 128:sub * 1024 + j * 128 + 128]
            p.op("PE", lambda e, blk=blk, j=j: e.transpose(
                pb7[:, j * 128:j * 128 + 128], blk, cs["c_identb"][:]),
                reads=[blk, cs["c_identb"][:]], writes=[pb7[:]],
                note=f"{tag}aT")
        p.op("DVE", lambda e, sub=sub: e.tensor_copy(
            b["fmst"][:, sub * 1024:sub * 1024 + 1024] if False else b["fmst"][:],
            pb7[:]),
            reads=[pb7[:]], writes=[b["fmst"][:]], note=f"{tag}aTe")
        p.dma("SP", a_d.rearrange("(m p) t -> p m t", p=128)[
            :, :, w * 256 + sub * 128:w * 256 + sub * 128 + 128],
            b["fmst"][:].rearrange("p (m t) -> p m t", t=128),
            note=f"{tag}asp")


def emit_l0f_ffn(p, cs, b, wd, l, tag):
    xi = b["xi"][:]
    gscr = b["gscr"]
    pss = [b["ps0"], b["ps1"], b["ps2"], b["ps3"]]
    a_dd, g_dd = wd["a_dd"], wd["g_dd"]
    xfn = lambda pc, t0, n: xi_ap(xi, pc, t0, n, shifted=True)
    emit_ln_stats(p, cs, b, None, b["meanb"][:], b["rstdb"][:],
                  b["rstd_tm"][:], tag + "ln", xfn=xfn)
    lngr = b["lngr"]
    p.dma("POOL", lngr[:, 0:8], wd["lng"][l], note=f"{tag}lg")
    p.dma("POOL", lngr[:, 8:16], wd["lnb"][l], note=f"{tag}lb")
    for pc in range(8):
        for tc in range(4):
            xs = xi_ap(xi, pc, tc * 512, 512, shifted=True)
            t_ = gscr[:, 0:512]
            mb = b["meanb"][:, tc * 512:tc * 512 + 512]
            rb = b["rstdb"][:, tc * 512:tc * 512 + 512]
            dst = xln_sl(b, pc, tc * 512, 512)
            p.op("DVE", lambda e, xs=xs, t_=t_, mb=mb: e.tensor_sub(t_, xs, mb),
                 reads=[xs, mb], writes=[t_], note=f"{tag}x1")
            p.op("DVE", lambda e, t_=t_, rb=rb: e.tensor_mul(t_, t_, rb),
                 reads=[t_, rb], writes=[t_], note=f"{tag}x2")
            p.op("DVE", lambda e, t_=t_, dst=dst, pc=pc: e.tensor_scalar(
                dst, t_, lngr[:, pc:pc + 1], lngr[:, 8 + pc:9 + pc],
                ALU.mult, ALU.add),
                reads=[t_, lngr[:]], writes=[dst], note=f"{tag}x3")
    for part, wname, dd in ((0, "w1a", a_dd), (1, "w1g", g_dd)):
        w3 = wd[wname].rearrange("l (k p) j -> l p k j", p=128)[l]
        for m in range(22):
            mw = 128 if m < 21 else 42
            sl = b["cscr"][:, (m % 2) * 2048:(m % 2) * 2048 + 2048]
            sl3 = sl.rearrange("p (k w) -> p k w", k=8)[:, :, 0:mw]
            p.dma("POOL", sl3, w3[:, :, m * 128:m * 128 + mw],
                  note=f"{tag}w1l{part}")
            for tc in range(4):
                ps = pss[tc]
                for k in range(8):
                    wk = sl[:, k * 256:k * 256 + mw]
                    xsl = xln_sl(b, k, tc * 512, 512)
                    p.op("PE", lambda e, wk=wk, xsl=xsl, ps=ps, k=k, mw=mw:
                         e.matmul(ps[0:mw, :], wk, xsl, start=(k == 0),
                                  stop=(k == 7)),
                         reads=[wk, xsl], writes=[ps[:]], note=f"{tag}w1m")
                stg = b["zstg"][:, tc * 512:tc * 512 + 512]
                p.op("DVE", lambda e, ps=ps, stg=stg, mw=mw: e.tensor_copy(
                    stg[0:mw, :], ps[0:mw, :]),
                    reads=[ps[:]], writes=[stg], note=f"{tag}w1e")
            p.dma("SP", dd[m * 128:m * 128 + mw, :], b["zstg"][0:mw, :],
                  note=f"{tag}hgo{part}")
    b1s = b["lngr"]
    p.dma("POOL", b1s[:, 16:38], wd["b1a"][l], note=f"{tag}b1a")
    p.dma("POOL", b1s[:, 38:60], wd["b1g"][l], note=f"{tag}b1g")
    p.dma("POOL", b["b2s"][:], wd["b2r"][l], note=f"{tag}b2")
    for tc in range(4):
        for k in range(22):
            kw = 128 if k < 21 else 42
            asl = b["qa"][:]
            p.dma("SP", asl[0:kw, 0:512],
                  a_dd[k * 128:k * 128 + kw, tc * 512:tc * 512 + 512],
                  note=f"{tag}pla")
            p.dma("SP", asl[0:kw, 512:1024],
                  g_dd[k * 128:k * 128 + kw, tc * 512:tc * 512 + 512],
                  note=f"{tag}plg")
            gl = b["qb"][:, 0:512]
            p.op("ACT", lambda e, gl=gl, asl=asl, k=k, kw=kw: e.activation(
                gl[0:kw, :], asl[0:kw, 512:1024], AF.Gelu,
                bias=b1s[0:kw, 38 + k:39 + k]),
                reads=[asl, b1s[:]], writes=[gl], note=f"{tag}gel")
            dst = prod_sl(b, k)
            p.op("DVE", lambda e, dst=dst, asl=asl, gl=gl, k=k, kw=kw:
                 e.scalar_tensor_tensor(dst[0:kw, :], asl[0:kw, 0:512],
                                        b1s[0:kw, 16 + k:17 + k], gl[0:kw, :],
                                        ALU.add, ALU.mult),
                 reads=[asl, gl, b1s[:]], writes=[dst], note=f"{tag}prd")
        for mg in range(2):
            for k in range(22):
                kw = 128 if k < 21 else 42
                wsl = b["wos"][:, (k % 2) * 512:(k % 2) * 512 + 512]
                p.dma("POOL", wsl[0:kw, :],
                      wd["w26"][l][k * 128:k * 128 + kw,
                                   mg * 512:mg * 512 + 512],
                      note=f"{tag}w2l")
                prk = prod_sl(b, k)
                for m in range(4):
                    lh = wsl[0:kw, m * 128:m * 128 + 128]
                    p.op("PE", lambda e, lh=lh, prk=prk, m=m, k=k, kw=kw:
                         e.matmul(pss[m][:], lh, prk[0:kw, :],
                                  start=(k == 0), stop=(k == 21)),
                         reads=[wsl, prk], writes=[pss[m][:]], note=f"{tag}w2m")
            for m in range(4):
                mm = mg * 4 + m
                xsl = xi_ap(xi, mm, tc * 512, 512)
                p.op("DVE", lambda e, xsl=xsl, m=m, mm=mm:
                     e.scalar_tensor_tensor(w32(xsl), pss[m][:],
                                            b["b2s"][:, mm:mm + 1], xsl,
                                            ALU.add, ALU.add),
                     reads=[pss[m][:], b["b2s"][:], xsl], writes=[xsl],
                     note=f"{tag}w2r")


def alloc_l0f_bufs(al, b):
    b["xi"] = b["xres"]
    b["qst"] = b["zz"]
    b["kst"] = b["ygT"][:, 0:4096]
    b["vst"] = b["ygT"][:, 4096:8192]
    b["wqs"] = b["wim"][:, 0:1024]
    xbcb = b["xbc"]
    b["rt_csq"] = xbcb[:, 0:1024]
    b["rt_snq"] = xbcb[:, 1024:2048]
    b["rt_csk"] = xbcb[:, 2048:3072]
    b["rt_snk"] = xbcb[:, 3072:4096]
    b["qa"] = xbcb[:, 4096:5120]
    b["qb"] = xbcb[:, 5120:6144]
    b["qc"] = xbcb[:, 6144:7168]
    b["fmst"] = xbcb[:, 7168:8192]
    b["qw"] = b["dbt"]
    b["kring"] = b["cscr"][:, 0:4096]
    b["vring"] = b["u8k"][:].bitcast(BF16)
    b["att_tm"] = b["hst"]
    b["pstl"] = b["xT"][:, 0:512]
    b["ptsb"] = b["xT"][:, 512:1024]
    b["qsr"] = b["mt"][:].bitcast(F32)[0:1, 0:128]
    b["qsb"] = b["egr"][:, 0:128]
    b["sst"] = b["tmv"][:, 0:20]
    b["lngr"] = b["tmv"][:, 20:84]
    b["b2s"] = b["tmv"][:, 84:92]


def xln_sl(b, pc, t0, n):
    if pc < 4:
        return b["ygT"][:, pc * 2048 + t0:pc * 2048 + t0 + n]
    return b["xbc"][:, (pc - 4) * 2048 + t0:(pc - 4) * 2048 + t0 + n]


def prod_sl(b, k, n=512):
    if k < 8:
        return b["zz"][:, k * 512:k * 512 + n]
    if k < 17:
        return b["cscr"][:, (k - 8) * 512:(k - 8) * 512 + n]
    return b["u8k"][:].bitcast(BF16)[:, (k - 17) * 512:(k - 17) * 512 + n]



def emit_l0_ffn_block(p, al, cs, b, wd, tag="l0f"):
    xres = b["xres"][:]
    xres_d = wd["xres_d"]
    gscr = b["gscr"]
    for pc in range(8):
        p.dma("SP", xres_d[pc * 128:pc * 128 + 128, :]
              if False else xres_d.rearrange("(c p) t -> p c t", p=128)[:, pc, :],
              xi_ap(xres, pc, 0, 2048), note=f"{tag}sp{pc}")
    emit_ln_stats(p, cs, b, xres, b["meanb"][:], b["rstdb"][:],
                  b["rstd_tm"][:], tag + "ln0")
    for pc in range(8):
        for tc in range(4):
            xs = xi_ap(xres, pc, tc * 512, 512)
            t_ = gscr[:, 0:512]
            mb = b["meanb"][:, tc * 512:tc * 512 + 512]
            rb = b["rstdb"][:, tc * 512:tc * 512 + 512]
            p.op("DVE", lambda e, xs=xs, t_=t_, mb=mb: e.tensor_sub(t_, xs, mb),
                 reads=[xs, mb], writes=[t_], note=f"{tag}l1")
            p.op("DVE", lambda e, xs=xs, t_=t_, rb=rb: e.tensor_mul(
                w32(xs), t_, rb),
                reads=[t_, rb], writes=[xs], note=f"{tag}l2")
    emit_l0_ffn(p, al, cs, b, wd, tag)
    for pc in range(8):
        for tc in range(4):
            ld = gscr[:, 512:1024]
            p.dma("SP", ld, xres_d.rearrange("(c p) t -> p c t", p=128)[
                :, pc, tc * 512:tc * 512 + 512], note=f"{tag}rl")
            xs = xi_ap(xres, pc, tc * 512, 512)
            p.op("DVE", lambda e, xs=xs, ld=ld: e.tensor_add(w32(xs), xs, ld),
                 reads=[xs, ld], writes=[xs], note=f"{tag}ra")


L0F_WSPECS = [("wqkv", (6, 1024, 3072)), ("wo6", (6, 1024, 1024)),
              ("w1a", (6, 1024, 2730)), ("w1g", (6, 1024, 2730)),
              ("w26", (6, 2730, 1024)), ("b1a", (6, 128, 22)),
              ("b1g", (6, 128, 22)), ("b2r", (6, 128, 8)),
              ("lng", (6, 128, 8)), ("lnb", (6, 128, 8)),
              ("qscale", (6, 64)), ("kscale", (6, 64))]


def build_l0f_only():
    nc = bass.Bass("TRN2", target_bir_lowering=False, debug=False)
    x_d = nc.dram_tensor("x", [2048, 1024], F32, kind="ExternalInput").ap()
    out_d = nc.dram_tensor("out", [2048, 1024], F32, kind="ExternalOutput").ap()
    wd = {}
    for nm, shape in L0F_WSPECS:
        wd[nm] = nc.dram_tensor(nm, list(shape), F32, kind="ExternalInput").ap()
    for nm, shape, dt_ in L0F_CONST_SPECS:
        wd[nm] = nc.dram_tensor(nm, list(shape), F32, kind="ExternalInput").ap()
    for nm, shape in L0F_TABLE_SPECS:
        wd[nm] = nc.dram_tensor(nm, list(shape), F32, kind="ExternalInput").ap()
    wd["qkv_d"] = nc.dram_tensor("qkv_d", [3072, 2048], BF16).ap()
    wd["v_d"] = nc.dram_tensor("v_d", [2048, 1024], BF16).ap()
    wd["q_d"] = nc.dram_tensor("q_d", [1024, 2048], BF16).ap()
    wd["k_d"] = nc.dram_tensor("k_d", [1024, 2048], BF16).ap()
    wd["a_d"] = nc.dram_tensor("a_d", [1024, 2048], BF16).ap()
    wd["a_dd"] = nc.dram_tensor("a_dd", [2816, 2048], BF16).ap()
    wd["g_dd"] = nc.dram_tensor("g_dd", [2816, 2048], BF16).ap()
    wd["xres_d"] = nc.dram_tensor("xres_d", [1024, 2048], F32).ap()
    al = Alloc(nc)
    p = Prog(nc)
    cs = declare_consts(nc, al, p)
    b = alloc_main_bufs(al)
    alloc_l0f_bufs(al, b)
    emit_load_x(p, cs, b, x_d, b["xres"][:])
    emit_l0_ffn_block(p, al, cs, b, wd)
    emit_store_x(p, cs, b, b["xres"][:], out_d)
    p.finalize()
    al.close()
    return nc


def host_l0f_const_stage(c):
    out = {}
    for nm, shape, dt_ in L0F_CONST_SPECS:
        out[nm] = np.ascontiguousarray(c[nm].astype(np.float32))
    for nm, shape in L0F_TABLE_SPECS:
        v = c[nm].reshape(16, 128, 64).transpose(1, 0, 2).reshape(128, 1024)
        out[nm] = np.ascontiguousarray(v.astype(np.float32))
    return out


def kernel_l0f(core_inputs):
    nc = build_l0f_only()
    consts = host_consts()
    fcon = host_l0f_const_stage(host_l0f_consts())
    cmaps = []
    for ci in core_inputs:
        w = host_l0f_weights(ci)
        m = {"x": np.ascontiguousarray(np.asarray(ci["x"], np.float32))}
        for nm, _ in L0F_WSPECS:
            m[nm] = w[nm]
        m.update(consts)
        m.update(fcon)
        cmaps.append(m)
    res = run_bass_kernel_spmd(nc, cmaps, core_ids=list(range(len(cmaps))))
    return [np.asarray(r["out"]) for r in res.results]



def host_l1a_weights(inp):
    w = {}
    w["wq_t"] = np.ascontiguousarray(np.asarray(inp["L1_attn_Wq"], np.float32).T)
    w["wk_t"] = np.ascontiguousarray(np.asarray(inp["L1_attn_Wk"], np.float32).T)
    w["wv_t"] = np.ascontiguousarray(np.asarray(inp["L1_attn_Wv"], np.float32).T)
    w["wg_t"] = np.ascontiguousarray(np.asarray(inp["L1_attn_Wg"], np.float32).T)
    w["wgk1_t"] = np.ascontiguousarray(np.asarray(inp["L1_attn_Wgk1"], np.float32).T)
    w["wgk2_t"] = np.ascontiguousarray(np.asarray(inp["L1_attn_Wgk2"], np.float32).T)
    w["wout_t"] = np.ascontiguousarray(np.asarray(inp["L1_attn_Wout"], np.float32).T)
    w["bgk2c"] = np.asarray(inp["L1_attn_bgk2"], np.float32).reshape(8, 128).T.copy()
    w["rmsw"] = np.asarray(inp["L1_attn_rms_w"], np.float32).reshape(1, 512).copy()
    return w


L1A_WSPECS = [("wq_t", (1024, 1024)), ("wk_t", (1024, 1024)),
              ("wv_t", (1024, 2048)), ("wg_t", (1024, 2048)),
              ("wgk1_t", (1024, 32)), ("wgk2_t", (32, 1024)),
              ("wout_t", (2048, 1024)), ("bgk2c", (128, 8)), ("rmsw", (1, 512))]


def emit_proj_fm(p, cs, b, x, w_d, out_d, ncols, tag, evac_rstd=True):
    negones = cs["c_negones"]
    wim, zstg, nws = b["wim"], b["zstg"], b["nws"]
    nm = (ncols + 127) // 128
    for m in range(nm):
        mw = min(128, ncols - m * 128)
        sl = wim[:, (m % 2) * 1024:(m % 2) * 1024 + 1024]
        sl3 = sl.rearrange("p (k w) -> p k w", k=8)[:, :, 0:mw]
        p.dma("POOL", sl3.bitcast(F32R), w_d.rearrange("(k p) j -> p k j",
              p=128)[:, :, m * 128:m * 128 + mw], note=f"{tag}wl{m}")
        for k in range(8):
            wk = sl[:, k * 128:k * 128 + mw]
            p.op("PE", lambda e, wk=wk, k=k, mw=mw: e.matmul(
                b["ps4"][0:1, 0:mw], negones[:], r32(wk),
                start=(k == 0), stop=(k == 7)),
                reads=[negones[:], wk], writes=[b["ps4"][:]], note=f"{tag}ws{m}")
        p.op("DVE", lambda e, mw=mw: e.tensor_copy(w32(nws[0:1, 0:mw]),
                                                   b["ps4"][0:1, 0:mw]),
             reads=[b["ps4"][:]], writes=[nws[:]], note=f"{tag}wc{m}")
        for tc in range(4):
            ps = b[f"ps{tc}"]
            for k in range(8):
                wk = sl[:, k * 128:k * 128 + mw]
                xs = xi_ap(x, k, tc * 512, 512)
                p.op("PE", lambda e, wk=wk, xs=xs, ps=ps, k=k, mw=mw: e.matmul(
                    ps[0:mw, :], r32(wk), r32(xs), start=(k == 0), stop=False),
                    reads=[wk, xs], writes=[ps[:]], note=f"{tag}mm{m}")
            mrr = b["meanb"][0:1, tc * 512:tc * 512 + 512]
            p.op("PE", lambda e, mrr=mrr, ps=ps, mw=mw: e.matmul(
                ps[0:mw, :], r32(nws[0:1, 0:mw]), r32(mrr), start=False,
                stop=True),
                reads=[nws[:], mrr], writes=[ps[:]], note=f"{tag}r1{m}")
            stg = zstg[:, tc * 512:tc * 512 + 512]
            rs = b["rstdb"][:, tc * 512:tc * 512 + 512]
            p.op("DVE", lambda e, ps=ps, stg=stg, rs=rs, mw=mw: e.tensor_mul(
                stg[0:mw, :], ps[0:mw, :], rs[0:mw, :]),
                reads=[ps[:], rs], writes=[stg], note=f"{tag}ev{m}")
        p.dma("SP", out_d[m * 128:m * 128 + mw, :], zstg[0:mw, :],
              note=f"{tag}o{m}")


def emit_proj_tm(p, cs, b, x, w_d, out_d, ncols, tag, silu=False):
    negones = cs["c_negones"]
    pss = [b["ps0"], b["ps1"], b["ps2"], b["ps3"]]
    w3 = w_d.rearrange("(k p) j -> p k j", p=128)
    nn = ncols // 512
    for ts in range(4):
        for ng in range(nn):
            for k in range(8):
                sl = b["wim"][:, (k % 2) * 512:(k % 2) * 512 + 512]
                p.dma("POOL", sl.bitcast(F32R),
                      w3[:, k, ng * 512:ng * 512 + 512], note=f"{tag}wl")
                p.op("PE", lambda e, sl=sl, k=k: e.matmul(
                    b["ps4"][0:1, :], negones[:], r32(sl),
                    start=(k == 0), stop=(k == 7)),
                    reads=[negones[:], sl], writes=[b["ps4"][:]],
                    note=f"{tag}ws")
                for t in range(4):
                    lh = xi_ap(x, k, ts * 512 + t * 128, 128)
                    p.op("PE", lambda e, sl=sl, lh=lh, t=t, k=k: e.matmul(
                        pss[t][:], r32(lh), r32(sl), start=(k == 0),
                        stop=False),
                        reads=[sl, lh], writes=[pss[t][:]], note=f"{tag}mm")
            p.op("DVE", lambda e: e.tensor_copy(w32(b["nws"][0:1, :]),
                                                b["ps4"][0:1, :]),
                 reads=[b["ps4"][:]], writes=[b["nws"][:]], note=f"{tag}wc")
            for t in range(4):
                mrr = b["meanb"][0:1, ts * 512 + t * 128:ts * 512 + t * 128 + 128]
                p.op("PE", lambda e, mrr=mrr, t=t: e.matmul(
                    pss[t][:], r32(mrr), r32(b["nws"][0:1, :]), start=False,
                    stop=True),
                    reads=[b["nws"][:], mrr], writes=[pss[t][:]],
                    note=f"{tag}r1")
                ct = ts * 4 + t
                stg = b["zstg"][:, t * 512:t * 512 + 512]
                rcol = b["rstd_tm"][:, ct:ct + 1]
                if silu:
                    p.op("ACT", lambda e, stg=stg, rcol=rcol, t=t: e.activation(
                        stg, pss[t][:], AF.Silu, scale=rcol),
                        reads=[pss[t][:], b["rstd_tm"][:]], writes=[stg],
                        note=f"{tag}evs")
                else:
                    p.op("DVE", lambda e, stg=stg, rcol=rcol, t=t:
                         e.tensor_scalar(stg, pss[t][:], rcol, None, ALU.mult),
                         reads=[pss[t][:], b["rstd_tm"][:]], writes=[stg],
                         note=f"{tag}ev")
            for t in range(4):
                p.dma("SP", out_d[ts * 512 + t * 128:ts * 512 + t * 128 + 128,
                                  ng * 512:ng * 512 + 512],
                      b["zstg"][:, t * 512:t * 512 + 512], note=f"{tag}o")


def emit_l1_attn(p, al, cs, b, wd, tag="l1a"):
    xres = b["xres"][:]
    gscr = b["gscr"]
    pss = [b["ps0"], b["ps1"], b["ps2"], b["ps3"]]
    pb7 = b["pb7"]
    emit_ln_stats(p, cs, b, xres, b["meanb"][:], b["rstdb"][:],
                  b["rstd_tm"][:], tag + "ln")
    emit_proj_fm(p, cs, b, xres, wd["wq_t"], wd["q_dd"], 1024, tag + "q")
    emit_proj_fm(p, cs, b, xres, wd["wk_t"], wd["k_dd"], 1024, tag + "k")
    emit_proj_fm(p, cs, b, xres, wd["wgk1_t"], wd["t1_dd"], 32, tag + "t1")
    emit_proj_tm(p, cs, b, xres, wd["wv_t"], wd["v_dd"], 2048, tag + "v")
    emit_proj_tm(p, cs, b, xres, wd["wg_t"], wd["g_dd"], 2048, tag + "g",
                 silu=True)
    t1s = b["dtr"]
    p.dma("SP", t1s[:], wd["t1_dd"][0:32, :], note=f"{tag}t1l")
    wg2 = b["qsb"][:].bitcast(BF16)[0:32, 0:128]
    bgk = b["b2s"]
    p.dma("POOL", bgk[:], wd["bgk2c"], note=f"{tag}bgk")
    gscr = b["gscr"]
    for pc in range(8):
        p.dma("POOL", wg2, wd["wgk2_t"][:, pc * 128:pc * 128 + 128],
              note=f"{tag}w2l{pc}")
        for tc in range(4):
            p.op("PE", lambda e, tc=tc: e.matmul(
                b["ps4"][:], wg2, t1s[:, tc * 512:tc * 512 + 512],
                start=True, stop=True),
                reads=[wg2, t1s[:]], writes=[b["ps4"][:]], note=f"{tag}gm")
            zt = gscr[:, 0:512]
            p.op("DVE", lambda e, zt=zt, pc=pc: e.tensor_scalar(
                zt, b["ps4"][:], bgk[:, pc:pc + 1], None, ALU.add),
                reads=[b["ps4"][:], bgk[:]], writes=[zt], note=f"{tag}gz")
            t_ = gscr[:, 512:1024]
            p.op("DVE", lambda e, zt=zt, t_=t_: e.tensor_scalar(
                t_, zt, -1.0, 0.0, ALU.mult, ALU.max),
                reads=[zt], writes=[t_], note=f"{tag}gr")
            p.op("ACT", lambda e, zt=zt: e.activation(zt, zt, AF.Abs),
                 reads=[zt], writes=[zt], note=f"{tag}ga")
            p.op("ACT", lambda e, zt=zt: e.activation(zt, zt, AF.Exp,
                                                      scale=-1.0),
                 reads=[zt], writes=[zt], note=f"{tag}ge")
            p.op("ACT", lambda e, zt=zt: e.activation(zt, zt, AF.Ln, bias=1.0),
                 reads=[zt, cs["c_fvals"][:]], writes=[zt], note=f"{tag}gl")
            p.op("DVE", lambda e, zt=zt, t_=t_: e.tensor_add(zt, zt, t_),
                 reads=[zt, t_], writes=[zt], note=f"{tag}gs")
            bs = gscr[:, 1024:1536]
            p.op("DVE", lambda e, zt=zt, bs=bs: e.tensor_scalar_mul(
                bs, zt, -1.0 / 32.0),
                reads=[zt], writes=[bs], note=f"{tag}gn")
            for j in range(4):
                s = slice(1024 + j * 128, 1024 + j * 128 + 128)
                p.op("DVE", lambda e, s=s: e.tensor_tensor_scan(
                    gscr[:, s], gscr[:, s], gscr[:, s], 0.0, ALU.add,
                    ALU.bypass),
                    reads=[gscr[:, s]], writes=[gscr[:, s]], note=f"{tag}bc")
            p.dma("SP", wd["bfm_d"][pc * 128:pc * 128 + 128,
                                    tc * 512:tc * 512 + 512], bs,
                  note=f"{tag}bo{pc}")
    rwb = b["rwb"]
    for half in range(4):
        p.dma("SP", b["qsr"][0:1, 0:128],
              wd["rmsw"][0:1, half * 128:half * 128 + 128], note=f"{tag}rw{half}")
        p.op("PE", lambda e, half=half: e.matmul(
            b["ps5"][:, 0:128], cs["c_onesrow"][:], b["qsr"][0:1, 0:128],
            start=True, stop=True),
            reads=[cs["c_onesrow"][:], b["qsr"]], writes=[b["ps5"][:]],
            note=f"{tag}rwm")
        p.op("DVE", lambda e, half=half: e.tensor_copy(
            rwb[:, half * 128:half * 128 + 128], b["ps5"][:, 0:128]),
            reads=[b["ps5"][:]], writes=[rwb[:]], note=f"{tag}rwc")
    p.op("DVE", lambda e: e.memset(b["Sst"][:], 0.0),
         reads=[], writes=[b["Sst"][:]], note=f"{tag}S0")
    for ccg in range(8):
        for ci2 in range(2):
            cc = ccg * 2 + ci2
            emit_l1a_chunk(p, cs, b, wd, cc, tag)
        emit_l1a_out2(p, cs, b, wd, ccg, tag)


def emit_l1a_chunk(p, cs, b, wd, cc, tag):
    gscr = b["gscr"]
    pb7 = b["pb7"]
    csl = slice(cc * 128, cc * 128 + 128)
    bfml = b["bfml"][:, (cc % 2) * 1024:(cc % 2) * 1024 + 1024]
    p.dma("SP", bfml.rearrange("p (m w) -> p m w", w=128),
          wd["bfm_d"].rearrange("(m p) t -> p m t", p=128)[:, :, csl],
          note=f"{tag}lb{cc}")
    qld = b["qld"][:, (cc % 2) * 1024:(cc % 2) * 1024 + 1024]
    kld = b["kld"][:, (cc % 2) * 1024:(cc % 2) * 1024 + 1024]
    p.dma("SP", qld.rearrange("p (m w) -> p m w", w=128),
          wd["q_dd"].rearrange("(m p) t -> p m t", p=128)[:, :, csl],
          note=f"{tag}lq{cc}")
    p.dma("SP", kld.rearrange("p (m w) -> p m w", w=128),
          wd["k_dd"].rearrange("(m p) t -> p m t", p=128)[:, :, csl],
          note=f"{tag}lk{cc}")
    vld = b["vld"][:, (cc % 2) * 2048:(cc % 2) * 2048 + 2048]
    gld = b["gld"][:, (cc % 2) * 2048:(cc % 2) * 2048 + 2048]
    p.dma("SP", vld, wd["v_dd"][csl, :], note=f"{tag}lv{cc}")
    p.dma("SP", gld, wd["g_dd"][csl, :], note=f"{tag}lg{cc}")
    eb, enb, ebml = b["ebq"], b["enb"], b["ebml"]
    for pc in range(8):
        bs = bfml[:, pc * 128:pc * 128 + 128]
        p.op("ACT", lambda e, bs=bs, pc=pc: e.activation(
            eb[:, pc * 128:pc * 128 + 128], bs, AF.Exp, bias=-LN16),
            reads=[bs, cs["c_fvals"][:]], writes=[eb[:]], note=f"{tag}eb")
        p.op("ACT", lambda e, bs=bs, pc=pc: e.activation(
            enb[:, pc * 128:pc * 128 + 128], bs, AF.Exp, scale=-1.0),
            reads=[bs], writes=[enb[:]], note=f"{tag}en")
        ecol = bfml[:, pc * 128 + 127:pc * 128 + 128]
        t_ = gscr[:, 0:128]
        p.op("DVE", lambda e, bs=bs, ecol=ecol, t_=t_: e.tensor_scalar(
            t_, bs, ecol, None, ALU.subtract),
            reads=[bs, b["bfml"][:]], writes=[t_], note=f"{tag}em1")
        p.op("ACT", lambda e, t_=t_, pc=pc: e.activation(
            ebml[:, pc * 128:pc * 128 + 128], t_, AF.Exp, scale=-1.0),
            reads=[t_], writes=[ebml[:]], note=f"{tag}em2")
    qt, kt, kdf = b["qt"], b["kt"], b["kdf"]
    p.op("DVE", lambda e: e.tensor_mul(qt[:], qld, eb[:]),
         reads=[qld, eb[:]], writes=[qt[:]], note=f"{tag}qt")
    p.op("DVE", lambda e: e.tensor_mul(kt[:], kld, enb[:]),
         reads=[kld, enb[:]], writes=[kt[:]], note=f"{tag}kt")
    p.op("DVE", lambda e: e.tensor_mul(kdf[:], kld, ebml[:]),
         reads=[kld, ebml[:]], writes=[kdf[:]], note=f"{tag}kd")
    for j in range(8):
        p.op("PE", lambda e, j=j: e.transpose(
            pb7[:, j * 128:j * 128 + 128], kdf[:, j * 128:j * 128 + 128],
            cs["c_identb"][:]),
            reads=[kdf[:], cs["c_identb"][:]], writes=[pb7[:]],
            note=f"{tag}kT")
    p.op("DVE", lambda e: e.tensor_copy(b["kdt"][:], pb7[:]),
         reads=[pb7[:]], writes=[b["kdt"][:]], note=f"{tag}kTe")
    ebl = b["sst"][:, 0:8]
    ecols = bfml.rearrange("p (m w) -> p m w", w=128)[:, :, 127]
    p.op("ACT", lambda e: e.activation(ebl, ecols, AF.Exp),
         reads=[b["bfml"][:]], writes=[b["sst"][:, 0:8]], note=f"{tag}ebl")
    ps4, ps5 = b["ps4"], b["ps5"]
    for h in range(4):
        hsl = slice(h * 256, h * 256 + 256)
        vsl = vld[:, h * 512:h * 512 + 512]
        for dc in range(2):
            lh = kt[:, h * 256 + dc * 128:h * 256 + dc * 128 + 128]
            rh = qt[:, h * 256 + dc * 128:h * 256 + dc * 128 + 128]
            p.op("PE", lambda e, lh=lh, rh=rh, dc=dc: e.matmul(
                ps4[:, 0:128], lh, rh, start=(dc == 0), stop=(dc == 1)),
                reads=[lh, rh], writes=[ps4[:]], note=f"{tag}A{h}")
        amt = b["amt"]
        p.op("DVE", lambda e: e.tensor_tensor(amt[:], ps4[:, 0:128],
                                              cs["c_uppib"][:], ALU.mult),
             reads=[ps4[:], cs["c_uppib"][:]], writes=[amt[:]],
             note=f"{tag}Am{h}")
        for dc in range(2):
            lh = qt[:, h * 256 + dc * 128:h * 256 + dc * 128 + 128]
            Ssl = b["Sst"][:, (h * 2 + dc) * 512:(h * 2 + dc) * 512 + 512]
            p.op("PE", lambda e, lh=lh, Ssl=Ssl, dc=dc: e.matmul(
                ps5[:], lh, Ssl, start=(dc == 0), stop=False),
                reads=[lh, Ssl], writes=[ps5[:]], note=f"{tag}oi{h}")
        p.op("PE", lambda e, vsl=vsl: e.matmul(ps5[:], amt[:], vsl,
                                               start=False, stop=True),
             reads=[amt[:], vld], writes=[ps5[:]], note=f"{tag}oa{h}")
        for dc in range(2):
            lh = b["kdt"][:, h * 256 + dc * 128:h * 256 + dc * 128 + 128]
            p.op("PE", lambda e, lh=lh, vsl=vsl, dc=dc: e.matmul(
                b[f"ps{dc}"][:], lh, vsl, start=True, stop=True),
                reads=[lh, vld], writes=[b[f"ps{dc}"][:]], note=f"{tag}su{h}")
        for dc in range(2):
            Ssl = b["Sst"][:, (h * 2 + dc) * 512:(h * 2 + dc) * 512 + 512]
            ebcol = b["sst"][:, h * 2 + dc:h * 2 + dc + 1]
            p.op("DVE", lambda e, Ssl=Ssl, ebcol=ebcol, dc=dc:
                 e.scalar_tensor_tensor(Ssl, Ssl, ebcol, b[f"ps{dc}"][:],
                                        ALU.mult, ALU.add),
                 reads=[Ssl, b["sst"][:, 0:8], b[f"ps{dc}"][:]], writes=[Ssl],
                 note=f"{tag}sd{h}")
        sq = gscr[:, 0:512]
        p.op("ACT", lambda e, sq=sq: e.activation(sq, ps5[:], AF.Square),
             reads=[ps5[:]], writes=[sq], note=f"{tag}r1{h}")
        ms = b["sst"][:, 8:9]
        p.op("DVE", lambda e, sq=sq, ms=ms: e.reduce_sum(
            ms.unsqueeze(2), sq.unsqueeze(1), AX.X),
            reads=[sq], writes=[b["sst"][:, 8:9]], note=f"{tag}r2{h}")
        p.op("DVE", lambda e, ms=ms: e.tensor_scalar(
            ms, ms, 1.0 / 512.0, 1e-5, ALU.mult, ALU.add),
            reads=[ms], writes=[ms], note=f"{tag}r3{h}")
        p.op("ACT", lambda e, ms=ms: e.activation(ms, ms, AF.Ln),
             reads=[ms], writes=[ms], note=f"{tag}r4{h}")
        p.op("ACT", lambda e, ms=ms: e.activation(ms, ms, AF.Exp, scale=-0.5),
             reads=[ms], writes=[ms], note=f"{tag}r5{h}")
        on = gscr[:, 512:1024]
        p.op("DVE", lambda e, on=on, ms=ms: e.scalar_tensor_tensor(
            on, ps5[:], ms, b["rwb"][:], ALU.mult, ALU.mult),
            reads=[ps5[:], ms, b["rwb"][:]], writes=[on], note=f"{tag}r6{h}")
        dst = b["ogt"][:, h * 512:h * 512 + 512]
        gsl = gld[:, h * 512:h * 512 + 512]
        p.op("DVE", lambda e, dst=dst, on=on, gsl=gsl: e.tensor_tensor(
            dst, on, gsl, ALU.mult),
            reads=[on, gld], writes=[dst], note=f"{tag}r7{h}")
    for g8 in range(2):
        for j in range(8):
            kk = g8 * 8 + j
            src = b["ogt"][:, kk * 128:kk * 128 + 128]
            p.op("PE", lambda e, src=src, j=j: e.transpose(
                pb7[:, j * 128:j * 128 + 128], src, cs["c_identb"][:]),
                reads=[src, cs["c_identb"][:]], writes=[pb7[:]],
                note=f"{tag}oT{cc}")
        dst2 = b["ogT2"][:, :].rearrange("p (k c w) -> p k c w", c=2, w=128)[
            :, g8 * 8:g8 * 8 + 8, cc % 2, :]
        p.op("DVE", lambda e, dst2=dst2: e.tensor_copy(
            dst2, pb7[:, 0:1024].rearrange("p (j w) -> p j w", w=128)),
            reads=[pb7[:]], writes=[b["ogT2"][:]], note=f"{tag}oTe{cc}")


def emit_l1a_out2(p, cs, b, wd, ccg, tag):
    xres = b["xres"][:]
    pss = [b["ps0"], b["ps1"], b["ps2"], b["ps3"]]
    wo3 = wd["wout_bf_d"].rearrange("(k p) j -> p k j", p=128)
    s0 = ccg * 256
    for mg in range(2):
        for k in range(16):
            wsl = b["wos"][:, (k % 2) * 512:(k % 2) * 512 + 512]
            p.dma("POOL", wsl, wo3[:, k, mg * 512:mg * 512 + 512],
                  note=f"{tag}wol{ccg}")
            rhs = b["ogT2"][:, k * 256:k * 256 + 256]
            for m in range(4):
                lh = wsl[:, m * 128:m * 128 + 128]
                p.op("PE", lambda e, lh=lh, rhs=rhs, m=m, k=k: e.matmul(
                    pss[m][:, 0:256], lh, rhs, start=(k == 0), stop=(k == 15)),
                    reads=[wsl, rhs], writes=[pss[m][:]], note=f"{tag}wo{ccg}")
        for m in range(4):
            mm = mg * 4 + m
            xsl = xi_ap(xres, mm, s0, 256)
            p.op("DVE", lambda e, xsl=xsl, m=m: e.tensor_add(
                w32(xsl), xsl, pss[m][:, 0:256]),
                reads=[xsl, pss[m][:]], writes=[xsl], note=f"{tag}wr{ccg}")


def emit_wout_precast(p, b, w_d, w_bf_d, tag):
    stg = b["wos"]
    for i in range(16):
        sl = stg[:, (i % 2) * 1024:(i % 2) * 1024 + 1024]
        p.dma("POOL", sl, w_d[i * 128:i * 128 + 128, :], note=f"{tag}c{i}")
        p.dma("SP", w_bf_d[i * 128:i * 128 + 128, :], sl, note=f"{tag}s{i}")


def alloc_l1a_bufs(b):
    b["bfml"] = b["wim"]
    b["qld"] = b["dbt"]
    b["kld"] = b["zz"][:, 0:2048]
    b["vld"] = b["ygT"][:, 0:4096]
    b["gld"] = b["ygT"][:, 4096:8192]
    xb = b["xbc"]
    b["ebq"] = xb[:, 0:1024]
    b["enb"] = xb[:, 1024:2048]
    b["ebml"] = xb[:, 2048:3072]
    b["qt"] = xb[:, 3072:4096]
    b["kt"] = xb[:, 4096:5120]
    b["kdf"] = xb[:, 5120:6144]
    b["kdt"] = xb[:, 6144:7168]
    b["amt"] = xb[:, 7168:7296]
    b["Sst"] = b["cscr"][:, 0:4096]
    b["ogt"] = b["hst"]
    b["ogT2"] = b["u8k"][:].bitcast(BF16)
    b["rwb"] = b["xml"][:].bitcast(F32)


def build_l1a_only():
    nc = bass.Bass("TRN2", target_bir_lowering=False, debug=False)
    x_d = nc.dram_tensor("x", [2048, 1024], F32, kind="ExternalInput").ap()
    out_d = nc.dram_tensor("out", [2048, 1024], F32, kind="ExternalOutput").ap()
    wd = {}
    for nm, shape in L1A_WSPECS:
        wd[nm] = nc.dram_tensor(nm, list(shape), F32, kind="ExternalInput").ap()
    wd["q_dd"] = nc.dram_tensor("q_dd", [1024, 2048], BF16).ap()
    wd["k_dd"] = nc.dram_tensor("k_dd", [1024, 2048], BF16).ap()
    wd["t1_dd"] = nc.dram_tensor("t1_dd", [128, 2048], BF16).ap()
    wd["v_dd"] = nc.dram_tensor("v_dd", [2048, 2048], BF16).ap()
    wd["g_dd"] = nc.dram_tensor("g_dd", [2048, 2048], BF16).ap()
    wd["bfm_d"] = nc.dram_tensor("bfm_d", [1024, 2048], F32).ap()
    wd["wout_bf_d"] = nc.dram_tensor("wout_bf_d", [2048, 1024], BF16).ap()
    al = Alloc(nc)
    p = Prog(nc)
    cs = declare_consts(nc, al, p)
    b = alloc_main_bufs(al)
    alloc_l0f_bufs(al, b)
    alloc_l1a_bufs(b)
    emit_load_x(p, cs, b, x_d, b["xres"][:])
    emit_wout_precast(p, b, wd["wout_t"], wd["wout_bf_d"], "wpc")
    emit_l1_attn(p, al, cs, b, wd)
    emit_store_x(p, cs, b, b["xres"][:], out_d)
    p.finalize()
    al.close()
    return nc


def kernel_l1a(core_inputs):
    nc = build_l1a_only()
    consts = host_consts()
    cmaps = []
    for ci in core_inputs:
        w = host_l1a_weights(ci)
        m = {"x": np.ascontiguousarray(np.asarray(ci["x"], np.float32))}
        for nm, _ in L1A_WSPECS:
            m[nm] = w[nm]
        m.update(consts)
        cmaps.append(m)
    res = run_bass_kernel_spmd(nc, cmaps, core_ids=list(range(len(cmaps))))
    return [np.asarray(r["out"]) for r in res.results]



MOE_C = 5120


def host_l1f_weights(inp):
    w = {}
    w["wg8"] = np.ascontiguousarray(np.asarray(inp["L1_ffn_wg"], np.float32))
    fc1 = np.asarray(inp["L1_ffn_fc1_w"], np.float32)
    w["fc1t"] = np.ascontiguousarray(fc1.transpose(0, 2, 1))
    w["fc2w"] = np.ascontiguousarray(np.asarray(inp["L1_ffn_fc2_w"], np.float32))
    w["b1c"] = np.ascontiguousarray(
        np.asarray(inp["L1_ffn_fc1_b"], np.float32).reshape(8, 32, 128)
        .transpose(0, 2, 1).copy())
    w["b2m"] = np.ascontiguousarray(np.asarray(inp["L1_ffn_fc2_b"], np.float32))
    w["lngam"] = np.asarray(inp["L1_ffn_ln_gamma"], np.float32).reshape(1, 1024).copy()
    w["lnbet"] = np.asarray(inp["L1_ffn_ln_beta"], np.float32).reshape(1, 1024).copy()
    return w


L1F_WSPECS = [("wg8", (1024, 8)), ("fc1t", (8, 1024, 4096)),
              ("fc2w", (8, 4096, 1024)), ("b1c", (8, 128, 32)),
              ("b2m", (8, 1024)), ("lngam", (1, 1024)), ("lnbet", (1, 1024))]


def emit_l1f_routing(p, al, cs, b, wd, tag="l1r"):
    xres = b["xres"][:]
    gscr = b["gscr"]
    emit_ln_stats(p, cs, b, xres, b["meanb"][:], b["rstdb"][:],
                  b["rstd_tm"][:], tag + "ln")
    for tc in range(4):
        for c4 in range(4):
            cc = tc * 4 + c4
            sl = b["meanb"][:, cc * 128:cc * 128 + 128]
            p.op("PE", lambda e, sl=sl, c4=c4: e.transpose(
                b["ps6"][:, c4 * 128:c4 * 128 + 128], sl, cs["c_ident"][:]),
                reads=[sl, cs["c_ident"][:]], writes=[b["ps6"][:]],
                note=f"{tag}mrT")
        for c4 in range(4):
            cc = tc * 4 + c4
            p.op("DVE", lambda e, cc=cc, c4=c4: e.tensor_copy(
                b["mrtm"][:, cc:cc + 1], b["ps6"][:, c4 * 128:c4 * 128 + 1]),
                reads=[b["ps6"][:]], writes=[b["mrtm"][:, cc:cc + 1]],
                note=f"{tag}mrC")
    p.dma("SP", wd["rstdtm_d"], b["rstd_tm"][:], note=f"{tag}rts")
    p.dma("SP", wd["mrtm_d"], b["mrtm"][:], note=f"{tag}mrs")
    for pc in range(8):
        xs = xi_ap(xres, pc, 0, 2048)
        p.dma("SP", wd["xres_d"][pc * 128:pc * 128 + 128, :], xs,
              note=f"{tag}xs{pc}")
        tv = gscr[:, 0:2048]
        hb = b["zz"][:, 0:2048]
        p.op("DVE", lambda e, xs=xs, tv=tv: e.tensor_sub(
            tv, xs, b["meanb"][:]),
            reads=[xs, b["meanb"][:]], writes=[tv], note=f"{tag}hs{pc}")
        p.op("DVE", lambda e, tv=tv, hb=hb: e.tensor_mul(
            hb, tv, b["rstdb"][:]),
            reads=[tv, b["rstdb"][:]], writes=[hb], note=f"{tag}hm{pc}")
        p.dma("SP", wd["hid_d"][pc * 128:pc * 128 + 128, :], hb,
              note=f"{tag}hw{pc}")
    wgs = gscr[:, 1984:2048]
    p.dma("SP", wgs.rearrange("p (k j) -> p k j", j=8),
          wd["wg8"].rearrange("(k p) j -> p k j", p=128), note=f"{tag}wgl")
    for k in range(8):
        sl = wgs[:, k * 8:k * 8 + 8]
        p.op("PE", lambda e, sl=sl, k=k: e.matmul(
            b["ps4"][0:1, 0:8], cs["c_ones"][:, 0:1], sl,
            start=(k == 0), stop=(k == 7)),
            reads=[cs["c_ones"][:], sl], writes=[b["ps4"][:]],
            note=f"{tag}ws")
    p.op("DVE", lambda e: e.tensor_scalar_mul(b["nwsr"],
                                              b["ps4"][0:1, 0:8], -1.0),
         reads=[b["ps4"][:]], writes=[b["nwsr"]], note=f"{tag}wc")
    off = b["moff"][0:1, 0:16]
    p.op("DVE", lambda e: e.memset(off, 0.0), reads=[], writes=[off],
         note=f"{tag}o0")
    for ct in range(16):
        t0 = ct * 128
        ps = b["ps5"]
        for k in range(8):
            lh = xi_ap(xres, k, t0, 128)
            sl = wgs[:, k * 8:k * 8 + 8]
            p.op("PE", lambda e, lh=lh, sl=sl, k=k: e.matmul(
                ps[:, 0:8], lh, sl, start=(k == 0), stop=False),
                reads=[lh, sl], writes=[ps[:]], note=f"{tag}sm")
        mrr = b["meanb"][0:1, t0:t0 + 128]
        p.op("PE", lambda e, mrr=mrr: e.matmul(
            ps[:, 0:8], mrr, b["nwsr"], start=False, stop=True),
            reads=[mrr, b["nwsr"]], writes=[ps[:]], note=f"{tag}r1")
        sc = gscr[:, 0:8]
        rcol = b["rstd_tm"][:, ct:ct + 1]
        p.op("DVE", lambda e, sc=sc, rcol=rcol: e.tensor_scalar(
            sc, b["ps5"][:, 0:8], rcol, None, ALU.mult),
            reads=[b["ps5"][:], b["rstd_tm"][:]], writes=[sc], note=f"{tag}sc")
        mx = gscr[:, 8:9]
        p.op("DVE", lambda e, sc=sc, mx=mx: e.reduce_max(
            mx.unsqueeze(2), sc.unsqueeze(1), AX.X, negate=True),
            reads=[sc], writes=[mx], note=f"{tag}mx")
        lsum = gscr[:, 9:10]
        p.op("ACT", lambda e, sc=sc, mx=mx, lsum=lsum: e.activation(
            sc, sc, AF.Exp, bias=mx, accum_out=lsum),
            reads=[sc, mx], writes=[sc, lsum], note=f"{tag}ex")
        rl = gscr[:, 10:11]
        p.op("DVE", lambda e, lsum=lsum, rl=rl: e.reciprocal(rl, lsum),
             reads=[lsum], writes=[rl], note=f"{tag}rc")
        p.op("DVE", lambda e, sc=sc, rl=rl: e.tensor_scalar(
            sc, sc, rl, None, ALU.mult),
            reads=[sc, rl], writes=[sc], note=f"{tag}nm")
        srt = gscr[:, 16:24]
        p.op("DVE", lambda e, sc=sc, srt=srt: e.max(srt, sc),
             reads=[sc], writes=[srt], note=f"{tag}srt")
        gs = gscr[:, 32:33]
        p.op("DVE", lambda e, srt=srt, gs=gs: e.tensor_tensor(
            gs, srt[:, 0:1], srt[:, 1:2], ALU.add),
            reads=[srt], writes=[gs], note=f"{tag}gs")
        p.op("DVE", lambda e, gs=gs: e.tensor_scalar_max(gs, gs,
                                                         1.1920929e-07),
             reads=[gs], writes=[gs], note=f"{tag}gc")
        p.op("DVE", lambda e, gs=gs: e.reciprocal(gs, gs),
             reads=[gs], writes=[gs], note=f"{tag}gr")
        rt = b["route"][:, ct * 24:ct * 24 + 24]
        p.op("DVE", lambda e, srt=srt, gs=gs, rt=rt: e.tensor_scalar(
            rt[:, 0:2], srt[:, 0:2], gs, None, ALU.mult),
            reads=[srt, gs], writes=[b["route"][:]], note=f"{tag}gn")
        for j in range(2):
            oh = rt[:, 4 + j * 8:12 + j * 8]
            p.op("DVE", lambda e, oh=oh, sc=sc, srt=srt, j=j: e.tensor_tensor(
                oh, sc, srt[:, j:j + 1].broadcast_to((128, 8)), ALU.is_equal),
                reads=[sc, srt], writes=[b["route"][:]], note=f"{tag}oh{j}")
            ps2_ = b["ps6"]
            p.op("PE", lambda e, oh=oh, j=j: e.matmul(
                ps2_[:, 0:8], cs["c_upp01"][:], oh, start=True, stop=False),
                reads=[cs["c_upp01"][:], oh], writes=[ps2_[:]],
                note=f"{tag}rk{j}")
            p.op("PE", lambda e, j=j: e.matmul(
                ps2_[:, 0:8], cs["c_onesrow"][:], off[0:1, j * 8:j * 8 + 8],
                start=False, stop=True),
                reads=[cs["c_onesrow"][:], off], writes=[ps2_[:]],
                note=f"{tag}ro{j}")
            tmp8 = gscr[:, 64 + j * 8:72 + j * 8]
            p.op("DVE", lambda e, oh=oh, tmp8=tmp8: e.tensor_tensor(
                tmp8, ps2_[:, 0:8], oh, ALU.mult),
                reads=[ps2_[:], oh], writes=[tmp8], note=f"{tag}rm{j}")
            p.op("DVE", lambda e, tmp8=tmp8, rt=rt, j=j: e.reduce_sum(
                rt[:, 2 + j:3 + j].unsqueeze(2), tmp8.unsqueeze(1), AX.X),
                reads=[tmp8], writes=[b["route"][:]], note=f"{tag}rr{j}")
            p.op("PE", lambda e, oh=oh, j=j: e.matmul(
                b["ps4"][0:1, 0:8], cs["c_ones"][:, 0:1], oh, start=True,
                stop=True),
                reads=[cs["c_ones"][:], oh], writes=[b["ps4"][:]],
                note=f"{tag}cs{j}")
            p.op("DVE", lambda e, j=j: e.tensor_add(
                off[0:1, j * 8:j * 8 + 8], off[0:1, j * 8:j * 8 + 8],
                b["ps4"][0:1, 0:8]),
                reads=[off, b["ps4"][:]], writes=[off], note=f"{tag}oa{j}")
    for ct in range(16):
        p.dma("SP", wd["route_d"][ct * 128:ct * 128 + 128, :],
              b["route"][:, ct * 24:ct * 24 + 24], note=f"{tag}rs{ct}")
    p.dma("SP", wd["cnt_d"], off, note=f"{tag}cn")


def alloc_l1f_bufs(al, b):
    xbf = b["xbc"][:].bitcast(F32)
    b["route"] = xbf[:, 0:384]
    b["mrtm"] = xbf[:, 384:400]
    b["moff"] = xbf[0:1, 400:416]
    b["nwsr"] = xbf[0:1, 416:424]


def host_l1f_split(inp):
    w = {}
    fc1 = np.asarray(inp["L1_ffn_fc1_w"], np.float32)
    fc2 = np.asarray(inp["L1_ffn_fc2_w"], np.float32)
    b1 = np.asarray(inp["L1_ffn_fc1_b"], np.float32)
    for e in range(8):
        w[f"fc1t{e}"] = np.ascontiguousarray(fc1[e].T)
        w[f"fc2{e}"] = np.ascontiguousarray(fc2[e])
        w[f"b1{e}"] = np.ascontiguousarray(
            b1[e].reshape(32, 128).T.copy())
    w["b2m"] = np.ascontiguousarray(np.asarray(inp["L1_ffn_fc2_b"],
                                               np.float32))
    w["lngam"] = np.asarray(inp["L1_ffn_ln_gamma"],
                            np.float32).reshape(1, 1024).copy()
    w["lnbet"] = np.asarray(inp["L1_ffn_ln_beta"],
                            np.float32).reshape(1, 1024).copy()
    return w


L1F_B_WSPECS = ([(f"fc1t{e}", (1024, 4096)) for e in range(8)]
                + [(f"fc2{e}", (4096, 1024)) for e in range(8)]
                + [(f"b1{e}", (128, 32)) for e in range(8)]
                + [("b2m", (8, 1024)), ("lngam", (1, 1024)),
                   ("lnbet", (1, 1024))])


def emit_l1f_geff(p, cs, b, wd, C, tag="gf"):
    allc, cmk = b["allc"], b["cmk"]
    p.dma("SP", allc[:], wd["allcnt"], note=f"{tag}ac")
    p.dma("SP", cmk[:], wd["cmask"], note=f"{tag}cm")
    p.dma("SP", b["rstm"][:], wd["rstdtm_d"], note=f"{tag}rt")
    p.dma("SP", b["mrtm"][:], wd["mrtm_d"], note=f"{tag}mt")
    p.dma("SP", b["routeb"][:].rearrange("p (c j) -> p c j", j=24),
          wd["route_d"].rearrange("(c p) j -> p c j", p=128), note=f"{tag}rl")
    mall = b["prows"][0:8, 16:32]
    p.op("DVE", lambda e: e.tensor_scalar(mall, allc[:], cmk[:, 0:1], None,
                                          ALU.mult),
         reads=[allc[:], cmk[:]], writes=[b["prows"][:]], note=f"{tag}msk")
    ps0, ps1 = b["psB"][0], b["psB"][1]
    p.op("PE", lambda e: e.matmul(ps0[0:1, 0:16], cs["c_ones"][0:8, 0:1],
                                  mall, start=True, stop=True),
         reads=[cs["c_ones"][:], mall], writes=[ps0[:]], note=f"{tag}pf")
    p.op("PE", lambda e: e.matmul(ps1[0:1, 0:16], cs["c_ones"][0:8, 0:1],
                                  allc[:], start=True, stop=True),
         reads=[cs["c_ones"][:], allc[:]], writes=[ps1[:]], note=f"{tag}tt")
    prow = b["prows"][0:1, 0:16]
    p.op("DVE", lambda e: e.tensor_copy(prow, ps0[0:1, 0:16]),
         reads=[ps0[:]], writes=[b["prows"][:]], note=f"{tag}pc")
    p1e = b["prows"][0:1, 32:40]
    p.op("DVE", lambda e: e.tensor_add(p1e, prow[0:1, 8:16], ps1[0:1, 0:8]),
         reads=[b["prows"][:], ps1[:]], writes=[b["prows"][:]],
         note=f"{tag}p1")
    for i, row in enumerate([prow[0:1, 0:8], p1e]):
        ps = b["psB"][i]
        p.op("PE", lambda e, row=row, ps=ps: e.matmul(
            ps[:, 0:8], cs["c_onesrow"][:], row, start=True, stop=True),
            reads=[cs["c_onesrow"][:], b["prows"][:]], writes=[ps[:]],
            note=f"{tag}bc{i}")
    p0t, p1t = b["p0t"], b["p1t"]
    p.op("DVE", lambda e: e.tensor_copy(p0t[:], b["psB"][0][:, 0:8]),
         reads=[b["psB"][0][:]], writes=[p0t[:]], note=f"{tag}b0")
    p.op("DVE", lambda e: e.tensor_copy(p1t[:], b["psB"][1][:, 0:8]),
         reads=[b["psB"][1][:]], writes=[p1t[:]], note=f"{tag}b1")
    scr = b["scr32"]
    for ct in range(16):
        rsl = b["routeb"][:, ct * 24:ct * 24 + 24]
        gsl = b["geffb"][:, ct * 8:ct * 8 + 8]
        for j, pt in ((0, p0t), (1, p1t)):
            oh = rsl[:, 4 + j * 8:12 + j * 8]
            tmp = scr[:, j * 8:j * 8 + 8]
            p.op("DVE", lambda e, oh=oh, pt=pt, tmp=tmp: e.tensor_tensor(
                tmp, oh, pt[:], ALU.mult),
                reads=[b["routeb"][:], pt[:]], writes=[b["scr32"][:]],
                note=f"{tag}pm{j}")
            loc = scr[:, 16 + j:17 + j]
            p.op("DVE", lambda e, tmp=tmp, loc=loc: e.reduce_sum(
                loc.unsqueeze(2), tmp.unsqueeze(1), AX.X),
                reads=[b["scr32"][:]], writes=[b["scr32"][:]],
                note=f"{tag}ps{j}")
            p.op("DVE", lambda e, loc=loc, rsl=rsl, j=j: e.tensor_add(
                loc, loc, rsl[:, 2 + j:3 + j]),
                reads=[b["scr32"][:], b["routeb"][:]], writes=[b["scr32"][:]],
                note=f"{tag}pl{j}")
            kp = scr[:, 18 + j:19 + j]
            p.op("DVE", lambda e, loc=loc, kp=kp: e.tensor_scalar(
                kp, loc, float(C), None, ALU.is_lt),
                reads=[b["scr32"][:]], writes=[b["scr32"][:]],
                note=f"{tag}kp{j}")
            cj = scr[:, 20 + j * 8:28 + j * 8]
            p.op("DVE", lambda e, oh=oh, rsl=rsl, kp=kp, cj=cj, j=j:
                 e.tensor_scalar(cj, oh, rsl[:, j:j + 1], kp, ALU.mult,
                                 ALU.mult),
                 reads=[b["routeb"][:], b["scr32"][:]],
                 writes=[b["scr32"][:]], note=f"{tag}cj{j}")
        p.op("DVE", lambda e, gsl=gsl: e.tensor_add(
            gsl, scr[:, 20:28], scr[:, 28:36]),
            reads=[b["scr32"][:]], writes=[b["geffb"][:]], note=f"{tag}ge")


def emit_l1f_experts(p, cs, b, wd, tag="ex"):
    hid3 = b["hid3"][:].rearrange("p (k t) -> p k t", t=2048)
    p.dma("SP", hid3, wd["hid_d"].rearrange("(k p) t -> p k t", p=128),
          note=f"{tag}hl")
    for nm, row in (("gamt", wd["lngam"]), ("bett", wd["lnbet"])):
        rsb = b["prows"][0:1, 48 + (0 if nm == "gamt" else 1) * 1024:
                         48 + (1 if nm == "gamt" else 2) * 1024]
        p.dma("SP", rsb, row, note=f"{tag}{nm}l")
        for ng in range(2):
            ps = b["psB"][2 + ng]
            p.op("PE", lambda e, rsb=rsb, ng=ng, ps=ps: e.matmul(
                ps[:, 0:512], cs["c_onesrow"][:],
                rsb[0:1, ng * 512:ng * 512 + 512], start=True, stop=True),
                reads=[cs["c_onesrow"][:], b["prows"][:]], writes=[ps[:]],
                note=f"{tag}{nm}b")
        for ng in range(2):
            sl = b[nm][:, ng * 512:ng * 512 + 512]
            p.op("DVE", lambda e, sl=sl, ng=ng: e.tensor_copy(
                sl, b["psB"][2 + ng][:, 0:512]),
                reads=[b["psB"][2 + ng][:]], writes=[b[nm][:]],
                note=f"{tag}{nm}c")
    p.dma("SP", b["b2s"][:], wd["b2m"], note=f"{tag}b2l")
    for tg in range(2):
        tgo = tg * 1024
        p.op("DVE", lambda e: e.memset(b["acc"][:], 0.0),
             reads=[], writes=[b["acc"][:]], note=f"{tag}a0{tg}")
        for e_ in range(8):
            p.dma("SP", b["b1sb"][:], wd[f"b1{e_}"], note=f"{tag}b1{tg}{e_}")
            for m in range(32):
                wsl = b["w1s"][:, (m % 2) * 1024:(m % 2) * 1024 + 1024]
                wsl3 = wsl.rearrange("p (k j) -> p k j", j=128)
                p.dma("POOL", wsl3,
                      wd[f"fc1t{e_}"][:, m * 128:m * 128 + 128].rearrange(
                          "(k p) j -> p k j", p=128), note=f"{tag}w1{m}")
                pa = [b["psB"][2 * (m % 2)], b["psB"][2 * (m % 2) + 1]]
                for k in range(8):
                    lh = wsl3[:, k, :]
                    for tb in range(2):
                        rh = hid3[:, k, tgo + tb * 512:tgo + tb * 512 + 512]
                        ps = pa[tb]
                        p.op("PE", lambda e, lh=lh, rh=rh, ps=ps, k=k:
                             e.matmul(ps[:, 0:512], lh, rh, start=(k == 0),
                                      stop=(k == 7)),
                             reads=[wsl, b["hid3"][:]],
                             writes=[ps[:]], note=f"{tag}f1")
                for tb in range(2):
                    hsl = b["htg"][:, m * 1024 + tb * 512:
                                   m * 1024 + tb * 512 + 512]
                    ps = pa[tb]
                    p.op("ACT", lambda e, hsl=hsl, ps=ps, m=m: e.activation(
                        hsl, ps[:, 0:512], AF.Silu,
                        bias=b["b1sb"][:, m:m + 1]),
                        reads=[ps[:], b["b1sb"][:]], writes=[hsl],
                        note=f"{tag}si")
            for ng in range(2):
                for kk in range(32):
                    w2 = b["w2s"][:, (kk % 2) * 512:(kk % 2) * 512 + 512]
                    p.dma("POOL", w2,
                          wd[f"fc2{e_}"][kk * 128:kk * 128 + 128,
                                         ng * 512:ng * 512 + 512],
                          note=f"{tag}w2{ng}{kk}")
                    for tq in range(8):
                        lh = b["htg"][:, kk * 1024 + tq * 128:
                                      kk * 1024 + tq * 128 + 128]
                        ps = b["psB"][tq]
                        p.op("PE", lambda e, lh=lh, w2=w2, ps=ps, kk=kk:
                             e.matmul(ps[:, 0:512], lh, w2, start=(kk == 0),
                                      stop=(kk == 31)),
                             reads=[lh, w2], writes=[ps[:]],
                             note=f"{tag}f2")
                for tq in range(8):
                    asl = b["acc"][:, tq * 1024 + ng * 512:
                                   tq * 1024 + ng * 512 + 512]
                    gcol = b["geffb"][:, (tg * 8 + tq) * 8 + e_:
                                      (tg * 8 + tq) * 8 + e_ + 1]
                    ps = b["psB"][tq]
                    p.op("DVE", lambda e, asl=asl, gcol=gcol, ps=ps:
                         e.scalar_tensor_tensor(asl, ps[:, 0:512], gcol, asl,
                                                ALU.mult, ALU.add),
                         reads=[ps[:], b["geffb"][:], asl],
                         writes=[b["acc"][:]], note=f"{tag}cb")
        for tq in range(8):
            ct = tg * 8 + tq
            gsl = b["geffb"][:, ct * 8:ct * 8 + 8]
            ps0 = b["psB"][0]
            p.op("PE", lambda e, gsl=gsl: e.transpose(
                ps0[0:8, 0:128], gsl, cs["c_ident"][:]),
                reads=[b["geffb"][:], cs["c_ident"][:]], writes=[ps0[:]],
                note=f"{tag}gT{tq}")
            p.op("DVE", lambda e: e.tensor_copy(b["gtr"][:], ps0[0:8, 0:128]),
                 reads=[ps0[:]], writes=[b["gtr"][:]], note=f"{tag}gc{tq}")
            for ng in range(2):
                ps1 = b["psB"][1]
                p.op("PE", lambda e, ng=ng: e.matmul(
                    ps1[:, 0:512], b["gtr"][:],
                    b["b2s"][:, ng * 512:ng * 512 + 512], start=True,
                    stop=True),
                    reads=[b["gtr"][:], b["b2s"][:]], writes=[ps1[:]],
                    note=f"{tag}b2m{tq}")
                asl = b["acc"][:, tq * 1024 + ng * 512:
                               tq * 1024 + ng * 512 + 512]
                p.op("DVE", lambda e, asl=asl: e.tensor_add(
                    asl, asl, ps1[:, 0:512]),
                    reads=[asl, ps1[:]], writes=[b["acc"][:]],
                    note=f"{tag}b2a{tq}")
        for tq in range(8):
            ct = tg * 8 + tq
            t0 = ct * 128
            xld3 = b["xld"][:].rearrange("p (k t) -> p k t", t=128)
            p.dma("SP", xld3,
                  wd["xres_d"][:, t0:t0 + 128].rearrange("(k p) t -> p k t",
                                                         p=128),
                  note=f"{tag}xl{ct}")
            for k in range(8):
                ps = b["psB"][2 + k // 4]
                p.op("PE", lambda e, k=k, ps=ps: e.transpose(
                    ps[:, (k % 4) * 128:(k % 4) * 128 + 128], xld3[:, k, :],
                    cs["c_ident"][:]),
                    reads=[b["xld"][:], cs["c_ident"][:]], writes=[ps[:]],
                    note=f"{tag}xT{ct}")
            for hk in range(2):
                p.op("DVE", lambda e, hk=hk: e.tensor_copy(
                    b["xrt"][:, hk * 512:hk * 512 + 512],
                    b["psB"][2 + hk][:, 0:512]),
                    reads=[b["psB"][2 + hk][:]], writes=[b["xrt"][:]],
                    note=f"{tag}xc{ct}")
            rcol = b["rstm"][:, ct:ct + 1]
            mcol = b["mrtm"][:, ct:ct + 1]
            p.op("DVE", lambda e, rcol=rcol, mcol=mcol: e.tensor_scalar(
                b["ht"][:], b["xrt"][:], rcol, mcol, ALU.mult, ALU.subtract),
                reads=[b["xrt"][:], b["rstm"][:], b["mrtm"][:]],
                writes=[b["ht"][:]], note=f"{tag}hid{ct}")
            asl = b["acc"][:, tq * 1024:tq * 1024 + 1024]
            p.op("DVE", lambda e, asl=asl: e.tensor_add(b["ht"][:], b["ht"][:],
                                                        asl),
                 reads=[b["ht"][:], asl], writes=[b["ht"][:]],
                 note=f"{tag}res{ct}")
            sm = b["scr32"][:, 0:1]
            sq = b["scr32"][:, 1:2]
            p.op("DVE", lambda e, sm=sm: e.reduce_sum(
                sm.unsqueeze(2), b["ht"][:].unsqueeze(1), AX.X),
                reads=[b["ht"][:]], writes=[b["scr32"][:]],
                note=f"{tag}sm{ct}")
            p.op("ACT", lambda e, sq=sq: e.activation(
                b["sqb"][:], b["ht"][:], AF.Square, accum_out=sq),
                reads=[b["ht"][:]], writes=[b["sqb"][:], b["scr32"][:]],
                note=f"{tag}sq{ct}")
            mu = b["scr32"][:, 2:3]
            p.op("DVE", lambda e, sm=sm, mu=mu: e.tensor_scalar_mul(
                mu, sm, 1.0 / 1024.0),
                reads=[b["scr32"][:]], writes=[b["scr32"][:]],
                note=f"{tag}mu{ct}")
            msq = b["scr32"][:, 3:4]
            p.op("ACT", lambda e, mu=mu, msq=msq: e.activation(
                msq, mu, AF.Square),
                reads=[b["scr32"][:]], writes=[b["scr32"][:]],
                note=f"{tag}ms{ct}")
            var = b["scr32"][:, 4:5]
            p.op("DVE", lambda e, sq=sq, msq=msq, var=var:
                 e.scalar_tensor_tensor(var, sq, 1.0 / 1024.0, msq, ALU.mult,
                                        ALU.subtract),
                 reads=[b["scr32"][:]], writes=[b["scr32"][:]],
                 note=f"{tag}va{ct}")
            p.op("ACT", lambda e, var=var: e.activation(var, var, AF.Ln,
                                                        bias=1e-5),
                 reads=[b["scr32"][:], cs["c_fvals"][:]],
                 writes=[b["scr32"][:]], note=f"{tag}lv{ct}")
            r2 = b["scr32"][:, 5:6]
            p.op("ACT", lambda e, var=var, r2=r2: e.activation(
                r2, var, AF.Exp, scale=-0.5),
                reads=[b["scr32"][:]], writes=[b["scr32"][:]],
                note=f"{tag}r2{ct}")
            mur = b["scr32"][:, 6:7]
            p.op("DVE", lambda e, mu=mu, r2=r2, mur=mur: e.tensor_mul(
                mur, mu, r2),
                reads=[b["scr32"][:]], writes=[b["scr32"][:]],
                note=f"{tag}mr{ct}")
            p.op("DVE", lambda e, r2=r2, mur=mur: e.tensor_scalar(
                b["sqb"][:], b["ht"][:], r2, mur, ALU.mult, ALU.subtract),
                reads=[b["ht"][:], b["scr32"][:]], writes=[b["sqb"][:]],
                note=f"{tag}nm{ct}")
            p.op("DVE", lambda e: e.tensor_mul(b["sqb"][:], b["sqb"][:],
                                               b["gamt"][:]),
                 reads=[b["sqb"][:], b["gamt"][:]], writes=[b["sqb"][:]],
                 note=f"{tag}ga{ct}")
            p.op("DVE", lambda e: e.tensor_add(b["xrt"][:], b["xrt"][:],
                                               b["bett"][:]),
                 reads=[b["xrt"][:], b["bett"][:]], writes=[b["xrt"][:]],
                 note=f"{tag}be{ct}")
            p.op("DVE", lambda e: e.tensor_add(b["ob"][:], b["sqb"][:],
                                               b["xrt"][:]),
                 reads=[b["sqb"][:], b["xrt"][:]], writes=[b["ob"][:]],
                 note=f"{tag}ou{ct}")
            p.dma("SP", wd["out_d"][t0:t0 + 128, :], b["ob"][:],
                  note=f"{tag}os{ct}")


def build_l1f_b(C):
    nc = bass.Bass("TRN2", target_bir_lowering=False, debug=False)
    wd = {}
    for nm, shape in L1F_B_WSPECS:
        wd[nm] = nc.dram_tensor(nm, list(shape), F32,
                                kind="ExternalInput").ap()
    for nm, shape, dt_ in [("xres_d", (1024, 2048), F32),
                           ("hid_d", (1024, 2048), BF16),
                           ("route_d", (2048, 24), F32),
                           ("allcnt", (8, 16), F32), ("cmask", (8, 1), F32),
                           ("rstdtm_d", (128, 16), F32),
                           ("mrtm_d", (128, 16), F32)]:
        wd[nm] = nc.dram_tensor(nm, list(shape), dt_,
                                kind="ExternalInput").ap()
    wd["out_d"] = nc.dram_tensor("out", [2048, 1024], F32,
                                 kind="ExternalOutput").ap()
    al = Alloc(nc)
    p = Prog(nc)
    cs = declare_consts(nc, al, p)
    b = {}
    b["hid3"] = al.sbuf("hid3", (128, 16384), BF16)
    b["htg"] = al.sbuf("htg", (128, 32768), BF16)
    b["acc"] = al.sbuf("acc", (128, 8192), F32)
    b["w1s"] = al.sbuf("w1s", (128, 2048), BF16)
    b["w2s"] = al.sbuf("w2s", (128, 1024), BF16)
    b["b1sb"] = al.sbuf("b1sb", (128, 32), F32)
    b["routeb"] = al.sbuf("routeb", (128, 384), F32)
    b["geffb"] = al.sbuf("geffb", (128, 128), F32)
    b["allc"] = al.sbuf("allc", (8, 16), F32)
    b["cmk"] = al.sbuf("cmk", (8, 1), F32)
    b["p0t"] = al.sbuf("p0t", (128, 8), F32)
    b["p1t"] = al.sbuf("p1t", (128, 8), F32)
    b["prows"] = al.sbuf("prows", (128, 2096), F32)
    b["rstm"] = al.sbuf("rstm", (128, 16), F32)
    b["mrtm"] = al.sbuf("mrtm", (128, 16), F32)
    b["gamt"] = al.sbuf("gamt", (128, 1024), F32)
    b["bett"] = al.sbuf("bett", (128, 1024), F32)
    b["b2s"] = al.sbuf("b2s", (8, 1024), F32)
    b["gtr"] = al.sbuf("gtr", (8, 128), F32)
    b["xld"] = al.sbuf("xld", (128, 1024), F32)
    b["xrt"] = al.sbuf("xrt", (128, 1024), F32)
    b["ht"] = al.sbuf("ht", (128, 1024), F32)
    b["sqb"] = al.sbuf("sqb", (128, 1024), F32)
    b["ob"] = al.sbuf("ob", (128, 1024), F32)
    b["scr32"] = al.sbuf("scr32", (128, 40), F32)
    b["psB"] = [al.psum(f"psB{i}", (128, 512), F32) for i in range(8)]
    emit_l1f_geff(p, cs, b, wd, C)
    emit_l1f_experts(p, cs, b, wd)
    p.finalize()
    al.close()
    return nc


def build_l1f_a_only():
    nc = bass.Bass("TRN2", target_bir_lowering=False, debug=False)
    x_d = nc.dram_tensor("x", [2048, 1024], F32, kind="ExternalInput").ap()
    wd = {"wg8": nc.dram_tensor("wg8", [1024, 8], F32,
                                kind="ExternalInput").ap()}
    for nm, shape, dt_ in [("route_d", (2048, 24), F32),
                           ("cnt_d", (1, 16), F32),
                           ("xres_d", (1024, 2048), F32),
                           ("hid_d", (1024, 2048), BF16),
                           ("rstdtm_d", (128, 16), F32),
                           ("mrtm_d", (128, 16), F32)]:
        wd[nm] = nc.dram_tensor(nm, list(shape), dt_,
                                kind="ExternalOutput").ap()
    al = Alloc(nc)
    p = Prog(nc)
    cs = declare_consts(nc, al, p)
    b = alloc_main_bufs(al)
    alloc_l1f_bufs(al, b)
    emit_load_x(p, cs, b, x_d, b["xres"][:])
    emit_l1f_routing(p, al, cs, b, wd)
    p.finalize()
    al.close()
    return nc


def kernel_l1f(core_inputs, C=MOE_C):
    consts = host_consts()
    nca = build_l1f_a_only()
    cmaps = []
    for ci in core_inputs:
        m = {"x": np.ascontiguousarray(np.asarray(ci["x"], np.float32)),
             "wg8": np.ascontiguousarray(np.asarray(ci["L1_ffn_wg"],
                                                    np.float32))}
        m.update(consts)
        cmaps.append(m)
    nc_ids = list(range(len(cmaps)))
    ra = run_bass_kernel_spmd(nca, cmaps, core_ids=nc_ids)
    n = len(core_inputs)
    allcnt = np.concatenate(
        [np.asarray(ra.results[i]["cnt_d"], np.float32) for i in range(n)]
        + [np.zeros((8 - n, 16), np.float32)], axis=0)
    ncb = build_l1f_b(C)
    cmaps2 = []
    for i, ci in enumerate(core_inputs):
        w = host_l1f_split(ci)
        m = dict(w)
        m["allcnt"] = allcnt
        m["cmask"] = (np.arange(8) < i).astype(np.float32).reshape(8, 1)
        for nm in ("route_d", "xres_d", "hid_d", "rstdtm_d", "mrtm_d"):
            m[nm] = np.asarray(ra.results[i][nm])
        m.update(consts)
        cmaps2.append(m)
    rb = run_bass_kernel_spmd(ncb, cmaps2, core_ids=nc_ids)
    return [np.asarray(r["out"]) for r in rb.results]



def kernel_chain5(**inputs):
    x = np.asarray(inputs["L0_attn_x"], np.float32)
    core_inputs = []
    for bi in range(8):
        ci = dict(inputs)
        ci["x"] = x[bi]
        core_inputs.append(ci)
    outs = kernel_l0a(core_inputs)
    for bi in range(8):
        core_inputs[bi]["x"] = outs[bi]
    outs = kernel_l0f(core_inputs)
    for bi in range(8):
        core_inputs[bi]["x"] = outs[bi]
    outs = kernel_l1a(core_inputs)
    for bi in range(8):
        core_inputs[bi]["x"] = outs[bi]
    outs = kernel_l1f(core_inputs, C=MOE_C)
    return np.stack(outs, axis=0)



def build_fused_a():
    nc = bass.Bass("TRN2", target_bir_lowering=False, debug=False)
    x_d = nc.dram_tensor("x", [2048, 1024], F32, kind="ExternalInput").ap()
    wd0 = {}
    for nm, shape in L0A_WSPECS:
        wd0[nm] = nc.dram_tensor(nm, list(shape), F32,
                                 kind="ExternalInput").ap()
    wd0["zx_d"] = nc.dram_tensor("zx_d", [4480, 2048], BF16).ap()
    wd0["yf_d"] = nc.dram_tensor("yf_d", [1024, 2048], BF16).ap()
    wd0["xm_d"] = nc.dram_tensor("xm_d", [1024, 512], F32).ap()
    wd0["wo_bf_d"] = nc.dram_tensor("wo_bf_d", [2048, 1024], BF16).ap()
    wdf = {}
    for nm, shape in L0F_WSPECS:
        wdf[nm] = nc.dram_tensor(nm, list(shape), F32,
                                 kind="ExternalInput").ap()
    for nm, shape, dt_ in L0F_CONST_SPECS:
        wdf[nm] = nc.dram_tensor(nm, list(shape), F32,
                                 kind="ExternalInput").ap()
    for nm, shape in L0F_TABLE_SPECS:
        wdf[nm] = nc.dram_tensor(nm, list(shape), F32,
                                 kind="ExternalInput").ap()
    wdf["qkv_d"] = nc.dram_tensor("qkv_d", [3072, 2048], BF16).ap()
    wdf["v_d"] = nc.dram_tensor("v_d", [2048, 1024], BF16).ap()
    wdf["q_d"] = nc.dram_tensor("q_d", [1024, 2048], BF16).ap()
    wdf["k_d"] = nc.dram_tensor("k_d", [1024, 2048], BF16).ap()
    wdf["a_d"] = nc.dram_tensor("a_d", [1024, 2048], BF16).ap()
    wdf["a_dd"] = nc.dram_tensor("a_dd", [2816, 2048], BF16).ap()
    wdf["g_dd"] = nc.dram_tensor("g_dd", [2816, 2048], BF16).ap()
    wdf["xres_d"] = nc.dram_tensor("x0f_d", [1024, 2048], F32).ap()
    wda = {}
    for nm, shape in L1A_WSPECS:
        wda[nm] = nc.dram_tensor(nm, list(shape), F32,
                                 kind="ExternalInput").ap()
    wda["q_dd"] = nc.dram_tensor("q_dd", [1024, 2048], BF16).ap()
    wda["k_dd"] = nc.dram_tensor("k_dd", [1024, 2048], BF16).ap()
    wda["t1_dd"] = nc.dram_tensor("t1_dd", [128, 2048], BF16).ap()
    wda["v_dd"] = nc.dram_tensor("v_dd", [2048, 2048], BF16).ap()
    wda["g_dd"] = nc.dram_tensor("gg_dd", [2048, 2048], BF16).ap()
    wda["bfm_d"] = nc.dram_tensor("bfm_d", [1024, 2048], F32).ap()
    wda["wout_bf_d"] = nc.dram_tensor("wout_bf_d", [2048, 1024], BF16).ap()
    wdr = {"wg8": nc.dram_tensor("wg8", [1024, 8], F32,
                                 kind="ExternalInput").ap()}
    for nm, shape, dt_ in [("route_d", (2048, 24), F32),
                           ("cnt_d", (1, 16), F32),
                           ("xres_d", (1024, 2048), F32),
                           ("hid_d", (1024, 2048), BF16),
                           ("rstdtm_d", (128, 16), F32),
                           ("mrtm_d", (128, 16), F32)]:
        wdr[nm] = nc.dram_tensor(nm, list(shape), dt_,
                                 kind="ExternalOutput").ap()
    al = Alloc(nc)
    p = Prog(nc)
    cs = declare_consts(nc, al, p)
    b = alloc_main_bufs(al)
    alloc_l0f_bufs(al, b)
    alloc_l1a_bufs(b)
    alloc_l1f_bufs(al, b)
    emit_load_x(p, cs, b, x_d, b["xres"][:])
    emit_l0_attn(p, al, cs, b, wd0)
    emit_l0_ffn_block(p, al, cs, b, wdf)
    emit_wout_precast(p, b, wda["wout_t"], wda["wout_bf_d"], "wpc")
    emit_l1_attn(p, al, cs, b, wda)
    emit_l1f_routing(p, al, cs, b, wdr)
    p.finalize()
    al.close()
    return nc


def kernel(**inputs):
    x = np.asarray(inputs["L0_attn_x"], np.float32)
    consts = host_consts()
    fcon = host_l0f_const_stage(host_l0f_consts())
    nca = build_fused_a()
    cmaps = []
    for bi in range(8):
        ci = dict(inputs)
        ci["x"] = x[bi]
        m = {"x": np.ascontiguousarray(x[bi])}
        w0 = host_l0a_weights(ci)
        for nm, _ in L0A_WSPECS:
            m[nm] = w0[nm]
        wf = host_l0f_weights(ci)
        for nm, _ in L0F_WSPECS:
            m[nm] = wf[nm]
        wa = host_l1a_weights(ci)
        for nm, _ in L1A_WSPECS:
            m[nm] = wa[nm]
        m["wg8"] = np.ascontiguousarray(np.asarray(ci["L1_ffn_wg"],
                                                   np.float32))
        m.update(consts)
        m.update(fcon)
        cmaps.append(m)
    ra = run_bass_kernel_spmd(nca, cmaps, core_ids=list(range(8)))
    allcnt = np.concatenate(
        [np.asarray(ra.results[i]["cnt_d"], np.float32) for i in range(8)],
        axis=0)
    ncb = build_l1f_b(MOE_C)
    cmaps2 = []
    for i in range(8):
        ci = dict(inputs)
        w = host_l1f_split(ci)
        m = dict(w)
        m["allcnt"] = allcnt
        m["cmask"] = (np.arange(8) < i).astype(np.float32).reshape(8, 1)
        for nm in ("route_d", "xres_d", "hid_d", "rstdtm_d", "mrtm_d"):
            m[nm] = np.asarray(ra.results[i][nm])
        m.update(consts)
        cmaps2.append(m)
    rb = run_bass_kernel_spmd(ncb, cmaps2, core_ids=list(range(8)))
    return np.stack([np.asarray(r["out"]) for r in rb.results], axis=0)


# version 07 (final)
# speedup vs baseline: 1.2285x; 1.2285x over previous
import math
import numpy as np
import concourse.bass as bass
import concourse.mybir as mybir
from concourse.bass_utils import run_bass_kernel_spmd

F32 = mybir.dt.float32
F32R = mybir.dt.float32r
BF16 = mybir.dt.bfloat16
U32 = mybir.dt.uint32
U8 = mybir.dt.uint8
AX = mybir.AxisListType
ALU = mybir.AluOpType
AF = mybir.ActivationFunctionType

NCORES = 8
BATCH = 8
T = 2048
D = 1024
PD = 8


ENGS = ("PE", "ACT", "DVE", "POOL", "SP")
DMA_QUEUES = ("SP", "POOL", "ACT")
DMA_ROT = 8


def _ap_box(ap):
    t = ap.tensor
    name = t.name
    try:
        space = str(ap.space)
    except Exception:
        space = "?"
    es = mybir.dt.size(ap.dtype)
    lo = hi = 0
    for step, count in ap.ap:
        d = step * (count - 1)
        if d >= 0:
            hi += d
        else:
            lo += d
    off = ap.offset if isinstance(ap.offset, int) else 0
    if "SBUF" in space or "PSUM" in space:
        step0 = ap.ap[0][0] if len(ap.ap) else 1
        pb = (off // step0) if step0 > 0 else 0
        pn = ap.partition_size()
        lo = hi = 0
        for step, count in ap.ap[1:]:
            d = step * (count - 1)
            if d >= 0:
                hi += d
            else:
                lo += d
        off_free = off - pb * step0
        return (name, "S", pb, pb + pn,
                (off_free + lo) * es, (off_free + hi + 1) * es)
    return (name, "D", 0, 1, (off + lo) * es, (off + hi + 1) * es)


def _overlap(a, b):
    return (a[0] == b[0] and a[2] < b[3] and b[2] < a[3]
            and a[4] < b[5] and b[4] < a[5])


class Op:
    __slots__ = ("eng", "fn", "idx", "gidx", "needs_inc", "rank", "is_dma",
                 "note")

    def __init__(self, eng, fn, gidx, is_dma, note=""):
        self.eng = eng
        self.fn = fn
        self.gidx = gidx
        self.is_dma = is_dma
        self.needs_inc = False
        self.rank = None
        self.note = note


class Prog:
    def __init__(self, nc):
        self.nc = nc
        self.ops = []
        self.edges = {}
        self.writes = {}
        self.reads = {}
        self.eng_ops = {e: [] for e in ENGS}
        self.psum_tensors = set()

    def _dep_read(self, box, gidx, deps):
        for (wb, wg) in self.writes.get(box[0], ()):
            if _overlap(box, wb):
                deps.add(wg)

    def _dep_write(self, box, gidx, deps):
        for (wb, wg) in self.writes.get(box[0], ()):
            if _overlap(box, wb):
                deps.add(wg)
        for (rb, rg) in self.reads.get(box[0], ()):
            if _overlap(box, rb):
                deps.add(rg)

    def _record_read(self, box, gidx):
        self.reads.setdefault(box[0], []).append((box, gidx))

    def _record_write(self, box, gidx):
        ws = self.writes.setdefault(box[0], [])
        ws[:] = [(b, g) for (b, g) in ws
                 if not (b[2] >= box[2] and b[3] <= box[3]
                         and b[4] >= box[4] and b[5] <= box[5])]
        ws.append((box, gidx))
        rs = self.reads.get(box[0])
        if rs:
            rs[:] = [(b, g) for (b, g) in rs
                     if not (b[2] >= box[2] and b[3] <= box[3]
                             and b[4] >= box[4] and b[5] <= box[5])]

    def _expand_psum(self, ap, box):
        if "PSUM" in str(ap.space):
            sh = ap.tensor.shape
            es = mybir.dt.size(ap.tensor.dtype)
            return (box[0], "S", 0, 128, 0, int(np.prod(sh[1:])) * es)
        return box

    def op(self, eng, fn, reads=(), writes=(), note="", extra_deps=()):
        gidx = len(self.ops)
        deps = set(extra_deps)
        boxes_r = []
        boxes_w = []
        for ap in reads:
            b = self._expand_psum(ap, _ap_box(ap))
            boxes_r.append(b)
            self._dep_read(b, gidx, deps)
        for ap in writes:
            b = self._expand_psum(ap, _ap_box(ap))
            boxes_w.append(b)
            self._dep_write(b, gidx, deps)
        for b in boxes_r:
            self._record_read(b, gidx)
        for b in boxes_w:
            self._record_write(b, gidx)
        o = Op(eng, fn, gidx, is_dma=False, note=note)
        self.ops.append(o)
        self.eng_ops[eng].append(o)
        deps2 = set()
        for d in deps:
            p = self.ops[d]
            if p.eng == "PE" and eng == "PE" and not p.is_dma:
                continue
            deps2.add(d)
        if deps2:
            self.edges[gidx] = deps2
            for d in deps2:
                self.ops[d].needs_inc = True
        return gidx

    def dma(self, queue, out, in_, reads=(), writes=(), note="", **kw):
        assert queue in DMA_QUEUES
        gidx = len(self.ops)
        deps = set()
        allr = list(reads) + [in_]
        allw = list(writes) + [out]
        boxes_r = []
        boxes_w = []
        for ap in allr:
            b = self._expand_psum(ap, _ap_box(ap))
            boxes_r.append(b)
            self._dep_read(b, gidx, deps)
        for ap in allw:
            b = self._expand_psum(ap, _ap_box(ap))
            boxes_w.append(b)
            self._dep_write(b, gidx, deps)
        for b in boxes_r:
            self._record_read(b, gidx)
        for b in boxes_w:
            self._record_write(b, gidx)

        def fn(eng, _out=out, _in=in_, _kw=kw):
            return eng.dma_start(out=_out, in_=_in, **_kw)

        o = Op(queue, fn, gidx, is_dma=True, note=note)
        self.ops.append(o)
        self.eng_ops[queue].append(o)
        deps3 = set(deps)
        if deps3:
            self.edges[gidx] = deps3
            for d in deps3:
                self.ops[d].needs_inc = True
        return gidx

    def barrier(self):
        lasts = [self.eng_ops[e][-1].gidx for e in ENGS if self.eng_ops[e]]
        for e in ENGS:
            self.op(e, lambda eng: eng.nop(), extra_deps=lasts,
                    note="barrier")

    def finalize(self, final_waits=True):
        nc = self.nc
        for e in ENGS:
            rank = 0
            for o in self.eng_ops[e]:
                if o.needs_inc:
                    rank += 1
                    o.rank = rank
        self._ctxs = []

        def sem(name):
            g = nc.semaphore(name)
            s = g.__enter__()
            self._ctxs.append(g)
            return s

        eng_sem = {e: sem(f"c_{e}") for e in ENGS}
        dma_sems = {e: [sem(f"d_{e}{i}") for i in range(DMA_ROT)]
                    for e in DMA_QUEUES}
        for e in DMA_QUEUES:
            cnt = 0
            for o in self.eng_ops[e]:
                if o.is_dma:
                    o.rank = cnt
                    cnt += 1

        def wait_target(p):
            if p.is_dma:
                s = dma_sems[p.eng][p.rank % DMA_ROT]
                return (s, 16 * (p.rank // DMA_ROT + 1))
            return (eng_sem[p.eng], p.rank)

        blk_g = nc.Block()
        blk = blk_g.__enter__()

        def mk_body(e):
            def body(eng):
                waited = {}
                for o in self.eng_ops[e]:
                    if o.is_dma and o.rank >= DMA_ROT:
                        s = dma_sems[e][o.rank % DMA_ROT]
                        v = 16 * (o.rank // DMA_ROT)
                        if waited.get(id(s), -1) < v:
                            eng.wait_ge(s, v)
                            waited[id(s)] = v
                    deps = self.edges.get(o.gidx)
                    if deps:
                        tgts = {}
                        for d in deps:
                            p = self.ops[d]
                            s, v = wait_target(p)
                            key = id(s)
                            if tgts.get(key, (None, -1))[1] < v:
                                tgts[key] = (s, v)
                        for s, v in tgts.values():
                            if waited.get(id(s), -1) >= v:
                                continue
                            eng.wait_ge(s, v)
                            waited[id(s)] = v
                    ins = o.fn(eng)
                    if o.is_dma:
                        s = dma_sems[e][o.rank % DMA_ROT]
                        ins.then_inc(s, 16)
                    elif o.needs_inc:
                        ins.then_inc(eng_sem[e], 1)
                if final_waits and e in DMA_QUEUES:
                    cnt = sum(1 for o in self.eng_ops[e] if o.is_dma)
                    if cnt:
                        for i in range(DMA_ROT):
                            n = cnt // DMA_ROT + (1 if i < cnt % DMA_ROT else 0)
                            if n:
                                eng.wait_ge(dma_sems[e][i], 16 * n)
            return body

        blk.sync(mk_body("SP"))
        blk.tensor(mk_body("PE"))
        blk.vector(mk_body("DVE"))
        blk.scalar(mk_body("ACT"))
        blk.gpsimd(mk_body("POOL"))
        blk_g.__exit__(None, None, None)
        for g in reversed(self._ctxs):
            g.__exit__(None, None, None)


class Alloc:

    def __init__(self, nc):
        self.nc = nc
        self.ctxs = []
        self.tensors = {}

    def sbuf(self, name, shape, dtype=F32):
        g = self.nc.sbuf_tensor(name, list(shape), dtype)
        t = g.__enter__()
        self.ctxs.append(g)
        self.tensors[name] = t
        return t

    def psum(self, name, shape, dtype=F32):
        g = self.nc.psum_tensor(name, list(shape), dtype)
        t = g.__enter__()
        self.ctxs.append(g)
        self.tensors[name] = t
        return t

    def close(self):
        for g in reversed(self.ctxs):
            g.__exit__(None, None, None)


def r32(ap):
    return ap.bitcast(F32R)


def w32(ap):
    return ap.bitcast(F32R)


XW = 2052
XOFF = 4


def xi_ap(xi, pc, t0, n, shifted=False):
    off = XW * pc + XOFF + t0 - (1 if (shifted and pc >= 4) else 0)
    return xi[:, off:off + n]




def host_consts():
    c = {}
    I = np.eye(128, dtype=np.float32)
    r = np.arange(128)
    c["c_ident"] = I
    c["c_identb"] = I.copy()
    c["c_ones"] = np.ones((128, 128), np.float32)
    c["c_onesrow"] = np.ones((1, 128), np.float32)
    c["c_negones"] = np.full((128, 1), -1.0, np.float32)
    c["c_onesr"] = np.ones((128, 128), np.float32)
    c["c_negbig"] = (-1e30 * I).astype(np.float32)
    c["c_low01"] = (r[:, None] > r[None, :]).astype(np.float32)
    c["c_upp01"] = (r[:, None] < r[None, :]).astype(np.float32)
    c["c_i32"] = np.eye(32, dtype=np.float32)
    c["c_uppib"] = (r[:, None] <= r[None, :]).astype(np.float32)
    c["c_fvals"] = np.tile(np.array([[1e-5, -2.772588722239781]], np.float32),
                           (128, 1))
    return c


CONST_SPECS = [("c_ident", (128, 128), F32), ("c_identb", (128, 128), BF16),
               ("c_ones", (128, 128), F32), ("c_onesrow", (1, 128), F32),
               ("c_negones", (128, 1), F32R), ("c_onesr", (128, 128), F32R), ("c_negbig", (128, 128), F32),
               ("c_low01", (128, 128), F32), ("c_upp01", (128, 128), F32),
               ("c_i32", (32, 32), F32), ("c_uppib", (128, 128), BF16)]


LN16 = 2.772588722239781
FCONSTS = [1e-5, -LN16]


def declare_consts(nc, al, p):
    cs = {}
    for name, shape, dt_ in CONST_SPECS:
        d = nc.dram_tensor(name, list(shape), F32, kind="ExternalInput").ap()
        t = al.sbuf(name + "_s", shape, dt_)
        q = "POOL" if dt_ != F32 else "SP"
        p.dma(q, t[:], d, note=f"ld{name}")
        cs[name] = t
    d = nc.dram_tensor("c_fvals", [128, len(FCONSTS)], F32,
                       kind="ExternalInput").ap()
    t = al.sbuf("c_fvals_s", (128, len(FCONSTS)), F32)
    p.dma("SP", t[:], d, note="ldfvals")
    for i, v in enumerate(FCONSTS):
        nc.const_aps.register(F32, v, t[:, i:i + 1])
    cs["c_fvals"] = t
    return cs



def emit_ln_stats(p, cs, b, x, meanb, rstdb, rstd_tm, tag, xfn=None):
    ones = cs["c_ones"]
    gscr = b["gscr"]
    ps_s, ps_q, ps_t = b["ps4"], b["ps5"], b["ps6"]
    if xfn is None:
        xfn = lambda pc, t0, n: xi_ap(x, pc, t0, n)
    for tc in range(4):
        t0 = tc * 512
        for half in range(2):
            h0 = half * 256
            for pc in range(PD):
                xs = xfn(pc, t0 + h0, 256)
                sq = gscr[:, pc * 256:pc * 256 + 256]
                p.op("ACT", lambda e, xs=xs, sq=sq: e.activation(sq, xs,
                                                                 AF.Square),
                    reads=[xs], writes=[sq], note=f"{tag}sq")
            for pc in range(PD):
                sq = gscr[:, pc * 256:pc * 256 + 256]
                p.op("PE", lambda e, sq=sq, pc=pc, h0=h0: e.matmul(
                    ps_q[:, h0:h0 + 256], ones[:], sq,
                    start=(pc == 0), stop=(pc == PD - 1)),
                    reads=[ones[:], sq], writes=[ps_q[:]], note=f"{tag}ssq")
        for pc in range(PD):
            xs = xfn(pc, t0, 512)
            p.op("PE", lambda e, xs=xs, pc=pc: e.matmul(
                ps_s[:], ones[:], xs, start=(pc == 0),
                stop=(pc == PD - 1)),
                reads=[ones[:], xs], writes=[ps_s[:]], note=f"{tag}sum")
        mb = meanb[:, t0:t0 + 512]
        rb = rstdb[:, t0:t0 + 512]
        tv = gscr[:, 0:512]
        p.op("ACT", lambda e, mb=mb: e.activation(w32(mb), ps_s[:],
                                                  AF.Identity, scale=1.0 / D),
             reads=[ps_s[:]], writes=[mb], note=f"{tag}mean")
        p.op("ACT", lambda e, mb=mb, tv=tv: e.activation(tv, mb, AF.Square),
             reads=[mb], writes=[tv], note=f"{tag}msq")
        p.op("DVE", lambda e, tv=tv: e.scalar_tensor_tensor(
            tv, ps_q[:], 1.0 / D, tv, ALU.mult, ALU.subtract),
            reads=[ps_q[:], tv], writes=[tv], note=f"{tag}var")
        p.op("ACT", lambda e, tv=tv: e.activation(tv, tv, AF.Ln, bias=1e-5),
             reads=[tv, cs["c_fvals"][:]], writes=[tv], note=f"{tag}lnv")
        p.op("ACT", lambda e, tv=tv, rb=rb: e.activation(rb, tv, AF.Exp,
                                                         scale=-0.5),
             reads=[tv], writes=[rb], note=f"{tag}rstd")
        for c4 in range(4):
            cc = tc * 4 + c4
            sl = rstdb[:, t0 + c4 * 128:t0 + c4 * 128 + 128]
            p.op("PE", lambda e, sl=sl, c4=c4: e.transpose(
                ps_t[:, c4 * 128:c4 * 128 + 128], sl, cs["c_ident"][:]),
                reads=[sl, cs["c_ident"][:]], writes=[ps_t[:]],
                note=f"{tag}rtT")
        for c4 in range(4):
            cc = tc * 4 + c4
            p.op("DVE", lambda e, cc=cc, c4=c4: e.tensor_copy(
                rstd_tm[:, cc:cc + 1], ps_t[:, c4 * 128:c4 * 128 + 1]),
                reads=[ps_t[:]], writes=[rstd_tm[:, cc:cc + 1]],
                note=f"{tag}rtC")
        p.op("DVE", lambda e, mb=mb, rb=rb: e.tensor_mul(w32(mb), mb, rb),
             reads=[mb, rb], writes=[mb], note=f"{tag}mr")


def emit_l0a_passA(p, cs, b, x, rstdb, mr, wi_d, zx_d, tag):
    negones = cs["c_negones"]
    wim = b["wim"]
    zstg = b["zstg"]
    nws = b["nws"]
    for m in range(35):
        mw = 128 if m < 34 else 32
        sl = wim[:, (m % 2) * 1024:(m % 2) * 1024 + 1024]
        sl3 = sl.rearrange("p (k w) -> p k w", k=8)[:, :, 0:mw]
        p.dma("POOL", sl3.bitcast(F32R), wi_d.rearrange("(k p) j -> p k j",
              p=128)[:, :, m * 128:m * 128 + mw], note=f"{tag}wiA{m}")
        for k in range(8):
            wk = sl[:, k * 128:k * 128 + mw]
            p.op("PE", lambda e, wk=wk, k=k, mw=mw: e.matmul(
                b["ps4"][0:1, 0:mw], negones[:], r32(wk),
                start=(k == 0), stop=(k == 7)),
                reads=[negones[:], wk], writes=[b["ps4"][:]], note=f"{tag}ws{m}")
        p.op("DVE", lambda e, mw=mw: e.tensor_copy(w32(nws[0:1, 0:mw]),
                                                   b["ps4"][0:1, 0:mw]),
             reads=[b["ps4"][:]], writes=[nws[:]], note=f"{tag}wsc{m}")
        for tc in range(4):
            ps = b[f"ps{tc}"]
            for k in range(8):
                wk = sl[:, k * 128:k * 128 + mw]
                xs = xi_ap(x, k, tc * 512, 512)
                p.op("PE", lambda e, wk=wk, xs=xs, ps=ps, k=k, mw=mw: e.matmul(
                    ps[0:mw, :], r32(wk), r32(xs), start=(k == 0), stop=False),
                    reads=[wk, xs], writes=[ps[:]], note=f"{tag}mm{m}")
            mrr = mr[0:1, tc * 512:tc * 512 + 512]
            p.op("PE", lambda e, mrr=mrr, ps=ps, mw=mw: e.matmul(
                ps[0:mw, :], r32(nws[0:1, 0:mw]), r32(mrr), start=False,
                stop=True),
                reads=[nws[:], mrr], writes=[ps[:]], note=f"{tag}r1{m}")
            stg = zstg[:, tc * 512:tc * 512 + 512]
            rs = rstdb[:, tc * 512:tc * 512 + 512]
            p.op("DVE", lambda e, ps=ps, stg=stg, rs=rs, mw=mw: e.tensor_mul(
                stg[0:mw, :], ps[0:mw, :], rs[0:mw, :]),
                reads=[ps[:], rs], writes=[stg], note=f"{tag}ev{m}")
        p.dma("SP", zx_d[m * 128:m * 128 + mw, :], zstg[0:mw, :],
              note=f"{tag}zxo{m}")


def emit_wo_precast(p, b, wo_d, wo_bf_d, tag):
    stg = b["wos"]
    for i in range(16):
        sl = stg[:, (i % 2) * 1024:(i % 2) * 1024 + 1024]
        p.dma("POOL", sl, wo_d[i * 128:i * 128 + 128, :], note=f"{tag}wc{i}")
        p.dma("SP", wo_bf_d[i * 128:i * 128 + 128, :], sl, note=f"{tag}ws{i}")


def emit_l0a_passB(p, cs, b, ws, zx_d, yf_d, xm_d, dirn, xres, tag):
    fwd = (dirn == 0)
    m32a, m32b = b["m32a"], b["m32b"]
    gscr = b["gscr"]
    dtp = m32a[0:32, :]
    cl = m32b[0:32, :]
    t1 = gscr[0:32, 0:2048]
    t2 = b["u8k"][0:32, 0:2048]

    dtr = b["dtr"]
    p.dma("SP", dtr[:], zx_d[4352:4384, :], note=f"{tag}dtld")
    p.op("DVE", lambda e: e.tensor_scalar(t1, dtr[:], ws["dtb"][:, 0:1], None,
                                          ALU.add),
         reads=[dtr[:], ws["dtb"][:]], writes=[t1], note=f"{tag}dt+b")
    p.op("DVE", lambda e: e.tensor_scalar_max(t2, t1, 0.0),
         reads=[t1], writes=[t2], note=f"{tag}relu")
    p.op("ACT", lambda e: e.activation(dtp, t1, AF.Abs),
         reads=[t1], writes=[dtp], note=f"{tag}abs")
    p.op("ACT", lambda e: e.activation(dtp, dtp, AF.Exp, scale=-1.0),
         reads=[dtp], writes=[dtp], note=f"{tag}exp")
    p.op("ACT", lambda e: e.activation(dtp, dtp, AF.Ln, bias=1.0),
         reads=[dtp], writes=[dtp], note=f"{tag}ln1p")
    p.op("DVE", lambda e: e.tensor_add(dtp, dtp, t2),
         reads=[dtp, t2], writes=[dtp], note=f"{tag}sp")
    nea = b["nea"]
    p.op("ACT", lambda e: e.activation(nea[:, 0:1], ws["alog"][:, 0:1], AF.Exp),
         reads=[ws["alog"][:]], writes=[nea[:]], note=f"{tag}ea")
    p.op("DVE", lambda e: e.tensor_scalar_mul(nea[:, 0:1], nea[:, 0:1], -1.0),
         reads=[nea[:]], writes=[nea[:]], note=f"{tag}nea")
    p.op("DVE", lambda e: e.tensor_scalar(t1, dtp, nea[:, 0:1], None, ALU.mult),
         reads=[dtp, nea[:]], writes=[t1], note=f"{tag}L")
    for cc in range(16):
        s = slice(cc * 128, cc * 128 + 128)
        def scan(e, s=s):
            i = t1[:, s] if fwd else t1[:, s][:, ::-1]
            o = cl[:, s] if fwd else cl[:, s][:, ::-1]
            return e.tensor_tensor_scan(o, i, i, 0.0, ALU.add, ALU.bypass)
        p.op("DVE", scan, reads=[t1[:, s]], writes=[cl[:, s]],
             note=f"{tag}cs{cc}")
    p.op("DVE", lambda e: e.tensor_mul(
        b["i32s"][:], cs["c_i32"][:],
        ws["dvec"][:, 0:1].broadcast_to((32, 32))),
        reads=[cs["c_i32"][:], ws["dvec"][:]], writes=[b["i32s"][:]],
        note=f"{tag}dsel")
    p.op("PE", lambda e: e.matmul(b["ps5"][:, 0:32], cs["c_ones"][0:32, :],
                                  b["i32s"][:], start=True, stop=True),
         reads=[cs["c_ones"][:], b["i32s"][:]], writes=[b["ps5"][:]],
         note=f"{tag}dbm")
    p.op("DVE", lambda e: e.tensor_copy(gscr[:, 0:32], b["ps5"][:, 0:32]),
         reads=[b["ps5"][:]], writes=[gscr[:, 0:32]], note=f"{tag}dbr")
    p.op("DVE", lambda e: e.tensor_copy(
        b["dbt"][:, :].rearrange("p (h w) -> p h w", w=64),
        gscr[:, 0:32].unsqueeze(2).broadcast_to((128, 32, 64))),
        reads=[gscr[:, 0:32]], writes=[b["dbt"][:]], note=f"{tag}dbc")
    p.op("DVE", lambda e: e.memset(b["hst"][:], 0.0),
         reads=[], writes=[b["hst"][:]], note=f"{tag}h0")

    supers = range(4) if fwd else range(3, -1, -1)
    zx3 = zx_d.rearrange("(m p) t -> p m t", p=128)
    doff = 3 if fwd else 0
    xbc3 = b["xbc"][:, :].rearrange("p (m w) -> p m w", w=516)
    for S in supers:
        s0 = S * 512
        if fwd:
            if S == 0:
                p.op("DVE", lambda e: e.memset(xbc3[:, :, 0:3], 0.0),
                     reads=[], writes=[b["xbc"][:]], note=f"{tag}halo0")
                p.dma("SP", xbc3[:, :, 3:515], zx3[:, 16:34, s0:s0 + 512],
                      note=f"{tag}xbcl{S}")
            else:
                p.dma("SP", xbc3[:, :, 0:515], zx3[:, 16:34, s0 - 3:s0 + 512],
                      note=f"{tag}xbcl{S}")
        else:
            if S == 3:
                p.op("DVE", lambda e: e.memset(xbc3[:, :, 512:515], 0.0),
                     reads=[], writes=[b["xbc"][:]], note=f"{tag}halo1")
                p.dma("SP", xbc3[:, :, 0:512], zx3[:, 16:34, s0:s0 + 512],
                      note=f"{tag}xbcl{S}")
            else:
                p.dma("SP", xbc3[:, :, 0:515], zx3[:, 16:34, s0:s0 + 515],
                      note=f"{tag}xbcl{S}")
        for hh in range(2):
            mb = slice(hh * 9, hh * 9 + 9)
            csc = b["cscr"][:, :].rearrange("p (m w) -> p m w", w=512)
            csc2 = b["ygT"][:, 0:4608].rearrange("p (m w) -> p m w", w=512)
            for d in range(4):
                wcol = d if fwd else 3 - d
                wb = ws["convw"][:, wcol::4][:, mb]
                wbc = wb.unsqueeze(2).broadcast_to((128, 9, 512))
                src = xbc3[:, mb, d:d + 512]
                if d == 0:
                    p.op("DVE", lambda e, src=src, wbc=wbc, csc=csc:
                         e.tensor_tensor(csc[:, 0:9, :], src, wbc, ALU.mult),
                         reads=[b["xbc"][:], ws["convw"][:]],
                         writes=[b["cscr"][:]], note=f"{tag}cv0")
                else:
                    p.op("DVE", lambda e, src=src, wbc=wbc, csc2=csc2:
                         e.tensor_tensor(csc2[:, 0:9, :], src, wbc, ALU.mult),
                         reads=[b["xbc"][:], ws["convw"][:]],
                         writes=[b["ygT"][:]], note=f"{tag}cvm{d}")
                    p.op("DVE", lambda e, csc=csc, csc2=csc2:
                         e.tensor_add(csc[:, 0:9, :], csc[:, 0:9, :],
                                      csc2[:, 0:9, :]),
                         reads=[b["cscr"][:], b["ygT"][:]],
                         writes=[b["cscr"][:]], note=f"{tag}cva{d}")
            cb = ws["convb"][:, mb].unsqueeze(2).broadcast_to((128, 9, 512))
            p.op("DVE", lambda e, csc=csc, cb=cb: e.tensor_add(
                csc[:, 0:9, :], csc[:, 0:9, :], cb),
                reads=[ws["convb"][:], b["cscr"][:]], writes=[b["cscr"][:]],
                note=f"{tag}cvb")
            dst = xbc3[:, mb, doff:doff + 512]
            p.op("ACT", lambda e, dst=dst, csc=csc: e.activation(
                dst, csc[:, 0:9, :], AF.Silu),
                reads=[b["cscr"][:]], writes=[b["xbc"][:]], note=f"{tag}cvs")
        chunks = range(4) if fwd else range(3, -1, -1)
        for c in chunks:
            emit_l0a_chunk(p, cs, b, ws, fwd, 4 * S + c, doff + c * 128,
                           xbc3, zx3, tag)
        if not fwd:
            p.dma("SP", b["zz"][:, :].rearrange("p (m w) -> p m w", w=512),
                  yf_d.rearrange("(m p) t -> p m t", p=128)[:, :, s0:s0 + 512],
                  note=f"{tag}yfl{S}")
        emit_l0a_super_out(p, cs, b, ws, fwd, S, yf_d, xm_d, xres, tag)


def emit_l0a_chunk(p, cs, b, ws, fwd, cc, off_c, xbc3, zx3, tag):
    m32a, m32b = b["m32a"], b["m32b"]
    gscr = b["gscr"]
    tmv, eclcb, clcb = b["tmv"], b["eclcb"], b["clcb"]
    ps0, ps1, ps2, ps3, ps6 = b["ps0"], b["ps1"], b["ps2"], b["ps3"], b["ps6"]
    pb7 = b["pb7"]
    ccol = cc * 128
    csl = slice(ccol, ccol + 128)
    dtp = m32a[0:32, :]
    cl = m32b[0:32, :]
    xbaB = xbc3[:, 16, off_c:off_c + 128]
    xbaC = xbc3[:, 17, off_c:off_c + 128]
    zz = b["zz"]
    zld, zsl = zz[:, 0:2048], zz[:, 2048:4096]

    p.dma("SP", zld.rearrange("p (m w) -> p m w", w=128), zx3[:, 0:16, csl],
          note=f"{tag}zld{cc}")
    p.op("ACT", lambda e: e.activation(zsl, zld, AF.Silu),
         reads=[zld], writes=[zsl], note=f"{tag}zsl{cc}")

    CLc, NCLc, DTPc, WFc, ECLc = 0, 32, 64, 96, 128
    for i, s_ in enumerate([cl[:, csl], dtp[:, csl]]):
        p.op("PE", lambda e, s_=s_, i=i: e.transpose(
            ps6[:, i * 64:i * 64 + 32], s_, cs["c_i32"][:]),
            reads=[s_, cs["c_i32"][:]], writes=[ps6[:]], note=f"{tag}tp{cc}")
    p.op("DVE", lambda e: e.tensor_copy(tmv[:, 0:32], ps6[:, 0:32]),
         reads=[ps6[:]], writes=[tmv[:, 0:32]], note=f"{tag}tmv{cc}")
    p.op("DVE", lambda e: e.tensor_copy(tmv[:, 64:96], ps6[:, 64:96]),
         reads=[ps6[:]], writes=[tmv[:, 64:96]], note=f"{tag}tmd{cc}")
    p.op("DVE", lambda e: e.tensor_scalar_mul(tmv[:, 32:64], tmv[:, 0:32],
                                              -1.0),
         reads=[tmv[:, 0:32]], writes=[tmv[:, 32:64]], note=f"{tag}ncl{cc}")
    p.op("ACT", lambda e: e.activation(tmv[:, 128:160], tmv[:, 0:32], AF.Exp),
         reads=[tmv[:, 0:32]], writes=[tmv[:, 128:160]], note=f"{tag}ecl{cc}")

    gcol2 = ccol + (127 if fwd else 0)
    p.op("DVE", lambda e: e.tensor_mul(
        b["i32s"][:], cs["c_i32"][:],
        cl[:, gcol2:gcol2 + 1].broadcast_to((32, 32))),
        reads=[cs["c_i32"][:], cl[:, gcol2:gcol2 + 1]],
        writes=[b["i32s"][:]], note=f"{tag}ec32{cc}")
    p.op("PE", lambda e: e.matmul(ps6[:, 160:192], cs["c_ones"][0:32, :],
                                  b["i32s"][:], start=True, stop=True),
         reads=[cs["c_ones"][:], b["i32s"][:]], writes=[ps6[:]],
         note=f"{tag}ecb{cc}")
    p.op("DVE", lambda e: e.tensor_copy(clcb[:], ps6[:, 160:192]),
         reads=[ps6[:]], writes=[clcb[:]], note=f"{tag}ecc{cc}")
    p.op("ACT", lambda e: e.activation(eclcb[:], clcb[:], AF.Exp),
         reads=[clcb[:]], writes=[eclcb[:]], note=f"{tag}ece{cc}")
    p.op("DVE", lambda e: e.tensor_sub(tmv[:, 96:128], clcb[:], tmv[:, 0:32]),
         reads=[clcb[:], tmv[:, 0:32]], writes=[tmv[:, 96:128]],
         note=f"{tag}wf1{cc}")
    p.op("ACT", lambda e: e.activation(tmv[:, 96:128], tmv[:, 96:128], AF.Exp),
         reads=[tmv[:, 96:128]], writes=[tmv[:, 96:128]], note=f"{tag}wf2{cc}")
    p.op("DVE", lambda e: e.tensor_mul(tmv[:, 96:128], tmv[:, 96:128],
                                       tmv[:, 64:96]),
         reads=[tmv[:, 96:128], tmv[:, 64:96]], writes=[tmv[:, 96:128]],
         note=f"{tag}wf3{cc}")

    p.op("PE", lambda e: e.matmul(ps3[:, 0:128], xbaB, xbaC, start=True,
                                  stop=True),
         reads=[b["xbc"][:]], writes=[ps3[:]], note=f"{tag}st{cc}")
    p.op("PE", lambda e: e.transpose(pb7[:, 0:128], xbaB, cs["c_identb"][:]),
         reads=[b["xbc"][:], cs["c_identb"][:]], writes=[pb7[:]],
         note=f"{tag}btT{cc}")
    p.op("DVE", lambda e: e.tensor_copy(b["btm"][:], pb7[:, 0:128]),
         reads=[pb7[:]], writes=[b["btm"][:]], note=f"{tag}btC{cc}")
    for g4 in range(4):
        for j in range(4):
            hp = g4 * 4 + j
            src = xbc3[:, hp, off_c:off_c + 128]
            p.op("PE", lambda e, src=src, j=j: e.transpose(
                pb7[:, 512 + j * 128:512 + j * 128 + 128], src,
                cs["c_identb"][:]),
                reads=[b["xbc"][:], cs["c_identb"][:]], writes=[pb7[:]],
                note=f"{tag}xT{cc}")
        dst = b["xT"][:, g4 * 512:g4 * 512 + 512]
        p.op("DVE", lambda e, dst=dst: e.tensor_copy(dst, pb7[:, 512:1024]),
             reads=[pb7[:]], writes=[dst], note=f"{tag}xTe{cc}")
        h0 = g4 * 8
        wfb = tmv[:, WFc + h0:WFc + h0 + 8].unsqueeze(2).broadcast_to(
            (128, 8, 64))
        dstw = b["xW"][:, g4 * 512:g4 * 512 + 512].rearrange(
            "p (h w) -> p h w", w=64)
        p.op("DVE", lambda e, dstw=dstw, wfb=wfb: e.tensor_tensor(
            dstw, pb7[:, 512:1024].rearrange("p (h w) -> p h w", w=64), wfb,
            ALU.mult),
            reads=[pb7[:], tmv[:]], writes=[b["xW"][:]], note=f"{tag}xW{cc}")

    for h in range(32):
        gsl_ = ps6[:, 192 + (h % 2) * 128:192 + (h % 2) * 128 + 128]
        esl = b["egr"][:, (h % 2) * 128:(h % 2) * 128 + 128]
        msl = b["mt"][:, (h % 2) * 128:(h % 2) * 128 + 128]
        clb = tmv[:, CLc + h:CLc + h + 1].broadcast_to((128, 128))
        p.op("PE", lambda e, gsl_=gsl_, clb=clb: e.transpose(
            gsl_, clb, cs["c_ident"][:]),
            reads=[tmv[:], cs["c_ident"][:]], writes=[ps6[:]],
            note=f"{tag}g1h{h}")
        mask = cs["c_low01"] if fwd else cs["c_upp01"]
        p.op("PE", lambda e, gsl_=gsl_, mask=mask: e.matmul(
            gsl_, cs["c_negbig"][:], mask[:], start=False, stop=True),
            reads=[cs["c_negbig"][:], mask[:]], writes=[ps6[:]],
            note=f"{tag}g2h{h}")
        p.op("ACT", lambda e, gsl_=gsl_, esl=esl, h=h: e.activation(
            esl, gsl_, AF.Exp, bias=tmv[:, NCLc + h:NCLc + h + 1]),
            reads=[ps6[:], tmv[:]], writes=[esl], note=f"{tag}Eh{h}")
        p.op("DVE", lambda e, esl=esl, msl=msl, h=h: e.scalar_tensor_tensor(
            msl, esl, tmv[:, DTPc + h:DTPc + h + 1], ps3[:, 0:128],
            ALU.mult, ALU.mult),
            reads=[esl, tmv[:], ps3[:]], writes=[msl], note=f"{tag}Mh{h}")
        hs = slice(h * 64, h * 64 + 64)
        qs = slice((h % 8) * 64, (h % 8) * 64 + 64)
        p.op("PE", lambda e, msl=msl, hs=hs, qs=qs: e.matmul(
            ps0[:, qs], msl, b["xT"][:, hs], start=True, stop=True),
            reads=[msl, b["xT"][:, hs]], writes=[ps0[:]], note=f"{tag}yih{h}")
        p.op("PE", lambda e, hs=hs, qs=qs: e.matmul(
            ps1[:, qs], xbaC, b["hst"][:, hs], start=True, stop=True),
            reads=[b["xbc"][:], b["hst"][:, hs]], writes=[ps1[:]],
            note=f"{tag}yoh{h}")
        p.op("PE", lambda e, hs=hs, qs=qs: e.matmul(
            ps2[:, qs], b["btm"][:], b["xW"][:, hs], start=True, stop=True),
            reads=[b["btm"][:], b["xW"][:, hs]], writes=[ps2[:]],
            note=f"{tag}hah{h}")
        if h % 8 == 7:
            h0 = h - 7
            hgs = slice(h0 * 64, h0 * 64 + 512)
            hv = b["hst"][:, hgs].rearrange("p (h w) -> p h w", w=64)
            ecb = eclcb[:, h0:h0 + 8].unsqueeze(2).broadcast_to((128, 8, 64))
            p.op("DVE", lambda e, hv=hv, ecb=ecb: e.tensor_tensor(
                hv, hv, ecb, ALU.mult),
                reads=[b["hst"][:, hgs], eclcb[:]], writes=[b["hst"][:, hgs]],
                note=f"{tag}hd{h0}")
            p.op("DVE", lambda e, hgs=hgs: e.tensor_add(
                b["hst"][:, hgs], b["hst"][:, hgs], ps2[:]),
                reads=[b["hst"][:, hgs], ps2[:]], writes=[b["hst"][:, hgs]],
                note=f"{tag}hu{h0}")
            gg = slice(h0 * 64, h0 * 64 + 512)
            ecb2 = tmv[:, ECLc + h0:ECLc + h0 + 8].unsqueeze(2).broadcast_to(
                (128, 8, 64))
            ysc = gscr[:, 0:512]
            xdsc = gscr[:, 512:1024]
            p.op("DVE", lambda e, ecb2=ecb2, ysc=ysc: e.tensor_tensor(
                ysc.rearrange("p (h w) -> p h w", w=64),
                ps1[:].rearrange("p (h w) -> p h w", w=64), ecb2, ALU.mult),
                reads=[ps1[:], tmv[:]], writes=[ysc], note=f"{tag}q1_{h0}")
            p.op("DVE", lambda e, ysc=ysc: e.tensor_add(ysc, ysc, ps0[:]),
                 reads=[ysc, ps0[:]], writes=[ysc], note=f"{tag}q2_{h0}")
            p.op("DVE", lambda e, gg=gg, xdsc=xdsc: e.tensor_tensor(
                xdsc, b["xT"][:, gg], b["dbt"][:, gg], ALU.mult),
                reads=[b["xT"][:, gg], b["dbt"][:, gg]], writes=[xdsc],
                note=f"{tag}q3_{h0}")
            p.op("DVE", lambda e, gg=gg, ysc=ysc, xdsc=xdsc: e.tensor_add(
                b["ytm"][:, gg], ysc, xdsc),
                reads=[ysc, xdsc], writes=[b["ytm"][:, gg]],
                note=f"{tag}q4_{h0}")

    c_in_super = cc % 4
    for g8 in range(2):
        for j in range(8):
            k = g8 * 8 + j
            src = b["ytm"][:, k * 128:k * 128 + 128]
            p.op("PE", lambda e, src=src, j=j: e.transpose(
                pb7[:, j * 128:j * 128 + 128], src, cs["c_identb"][:]),
                reads=[src, cs["c_identb"][:]], writes=[pb7[:]],
                note=f"{tag}ygT{cc}")
        dst2 = b["ygT"][:, :].rearrange("p (k c w) -> p k c w", c=4, w=128)[
            :, g8 * 8:g8 * 8 + 8, c_in_super, :]
        zsl3 = zsl.rearrange("p (m w) -> p m w", w=128)[:, g8 * 8:g8 * 8 + 8, :]
        p.op("DVE", lambda e, dst2=dst2, zsl3=zsl3: e.tensor_tensor(
            dst2, pb7[:, 0:1024].rearrange("p (j w) -> p j w", w=128), zsl3,
            ALU.mult),
            reads=[pb7[:], zsl], writes=[b["ygT"][:]], note=f"{tag}ygE{cc}")


def emit_l0a_super_out(p, cs, b, ws, fwd, S, yf_d, xm_d, xres, tag):
    s0 = S * 512
    yf3 = yf_d.rearrange("(m p) t -> p m t", p=128)
    wo3 = ws["wo_bf_d"].rearrange("(k p) j -> p k j", p=128)
    pss = [b["ps0"], b["ps1"], b["ps2"], b["ps3"]]
    yfs = b["u8k"][:, :].bitcast(BF16)
    xmh = b["u8k"]
    for mg in range(2):
        for k in range(16):
            sl = b["wos"][:, (k % 2) * 512:(k % 2) * 512 + 512]
            p.dma("SP", sl, wo3[:, k, mg * 512:mg * 512 + 512],
                  note=f"{tag}wol{S}")
            rhs = b["ygT"][:, k * 512:k * 512 + 512]
            for m in range(4):
                lh = sl[:, m * 128:m * 128 + 128]
                p.op("PE", lambda e, lh=lh, rhs=rhs, m=m, k=k: e.matmul(
                    pss[m][:], lh, rhs, start=(k == 0), stop=(k == 15)),
                    reads=[sl, rhs], writes=[pss[m][:]], note=f"{tag}wo{S}")
        if fwd:
            for m in range(4):
                dst = yfs[:, m * 512:m * 512 + 512]
                p.op("DVE", lambda e, dst=dst, m=m: e.tensor_copy(dst,
                                                                  pss[m][:]),
                     reads=[pss[m][:]], writes=[dst], note=f"{tag}yfs{S}")
            p.dma("SP", yf3[:, mg * 4:mg * 4 + 4, s0:s0 + 512],
                  yfs[:, 0:2048].rearrange("p (m w) -> p m w", w=512),
                  note=f"{tag}yfo{S}")
        else:
            for m in range(4):
                mm = mg * 4 + m
                xsl = xmh[:, m * 512:m * 512 + 512]
                ysl = b["zz"][:, mm * 512:mm * 512 + 512]
                p.op("DVE", lambda e, xsl=xsl, ysl=ysl, m=m: e.tensor_add(
                    xsl, pss[m][:], ysl),
                    reads=[pss[m][:], ysl], writes=[xsl], note=f"{tag}xm{S}")
            p.dma("SP",
                  xm_d.rearrange("(m p) w -> p m w", p=128)[:, mg * 4:mg * 4 + 4, :],
                  xmh[:, :].rearrange("p (m w) -> p m w", w=512),
                  note=f"{tag}xmo{S}")
    if fwd:
        return
    wm3 = ws["wm_d"].rearrange("(k p) j -> p k j", p=128)
    for mg in range(2):
        for k in range(8):
            sl = b["wms"][:, 0:512]
            p.dma("POOL", sl, wm3[:, k, mg * 512:mg * 512 + 512],
                  note=f"{tag}wml{S}")
            xsl = b["xml"][:, (k % 2) * 512:(k % 2) * 512 + 512]
            p.dma("POOL", xsl, xm_d.rearrange("(g p) w -> p g w", p=128)[:, k, :],
                  note=f"{tag}xml{S}")
            for m in range(4):
                lh = sl[:, m * 128:m * 128 + 128]
                p.op("PE", lambda e, lh=lh, xsl=xsl, m=m, k=k: e.matmul(
                    pss[m][:], lh, xsl, start=(k == 0), stop=(k == 7)),
                    reads=[sl, xsl], writes=[pss[m][:]], note=f"{tag}wm{S}")
        for m in range(4):
            mm = mg * 4 + m
            th = b["th"][:, (m % 2) * 512:(m % 2) * 512 + 512]
            p.op("ACT", lambda e, th=th, mm=mm, m=m: e.activation(
                th, pss[m][:], AF.Tanh, bias=ws["bmr"][:, mm:mm + 1]),
                reads=[pss[m][:], ws["bmr"][:]], writes=[th], note=f"{tag}th{S}")
            xsl2 = b["gscr"][:, (m % 2) * 512:(m % 2) * 512 + 512]
            p.dma("POOL", xsl2, xm_d.rearrange("(g p) w -> p g w", p=128)[
                :, mm, :], note=f"{tag}xre{S}")
            tmp = b["gscr"][:, 1024 + (m % 2) * 512:1024 + (m % 2) * 512 + 512]
            p.op("DVE", lambda e, th=th, tmp=tmp, xsl2=xsl2:
                 e.scalar_tensor_tensor(tmp, th, 0.1, xsl2, ALU.mult, ALU.add),
                 reads=[th, xsl2], writes=[tmp], note=f"{tag}al{S}")
            xr = xi_ap(xres, mm, s0, 512)
            p.op("DVE", lambda e, tmp=tmp, xr=xr: e.tensor_add(
                w32(xr), xr, tmp),
                reads=[tmp, xr], writes=[xr], note=f"{tag}xr{S}")


def emit_load_x(p, cs, b, x_d, xres, tag="ldx"):
    xstg = b["zz"][:, :].bitcast(F32)
    pads = xres.rearrange("p (c w) -> p c w", w=XW)[:, :, 0:XOFF]
    p.op("DVE", lambda e: e.memset(pads, 0.0),
         reads=[], writes=[xres], note=f"{tag}pad")
    p.op("DVE", lambda e: e.tensor_copy(pads.bitcast(F32R).bitcast(F32)
                                        if False else w32(pads), pads),
         reads=[xres], writes=[xres], note=f"{tag}padr")
    for tt in range(16):
        stg = xstg[:, (tt % 2) * 1024:(tt % 2) * 1024 + 1024]
        p.dma("SP", stg, x_d[tt * 128:tt * 128 + 128, :], note=f"{tag}d{tt}")
        for g in range(2):
            for j in range(4):
                pc = g * 4 + j
                p.op("PE", lambda e, stg=stg, pc=pc, j=j: e.transpose(
                    b["ps6"][:, j * 128:j * 128 + 128],
                    stg[:, pc * 128:pc * 128 + 128], cs["c_ident"][:]),
                    reads=[stg, cs["c_ident"][:]], writes=[b["ps6"][:]],
                    note=f"{tag}t{tt}")
            for j in range(4):
                pc = g * 4 + j
                dst = xi_ap(xres, pc, tt * 128, 128)
                p.op("DVE", lambda e, dst=dst, j=j: e.tensor_copy(
                    w32(dst), b["ps6"][:, j * 128:j * 128 + 128]),
                    reads=[b["ps6"][:]], writes=[dst], note=f"{tag}e{tt}")


def emit_store_x(p, cs, b, xres, out_d, tag="stx"):
    xstg = b["zz"][:, :].bitcast(F32)
    for tt in range(16):
        stg = xstg[:, (tt % 2) * 1024:(tt % 2) * 1024 + 1024]
        for g in range(2):
            for j in range(4):
                pc = g * 4 + j
                src = xi_ap(xres, pc, tt * 128, 128)
                p.op("PE", lambda e, src=src, j=j: e.transpose(
                    b["ps6"][:, j * 128:j * 128 + 128], src, cs["c_ident"][:]),
                    reads=[src, cs["c_ident"][:]], writes=[b["ps6"][:]],
                    note=f"{tag}t{tt}")
            for j in range(4):
                pc = g * 4 + j
                p.op("DVE", lambda e, stg=stg, pc=pc, j=j: e.tensor_copy(
                    stg[:, pc * 128:pc * 128 + 128],
                    b["ps6"][:, j * 128:j * 128 + 128]),
                    reads=[b["ps6"][:]], writes=[stg], note=f"{tag}e{tt}")
        p.dma("SP", out_d[tt * 128:tt * 128 + 128, :], stg, note=f"{tag}d{tt}")


def alloc_main_bufs(al):
    b = {}
    b["xres"] = al.sbuf("xres", (128, 8 * 2052), F32)
    b["meanb"] = al.sbuf("meanb", (128, 2048), F32)
    b["rstdb"] = al.sbuf("rstdb", (128, 2048), F32)
    b["rstd_tm"] = al.sbuf("rstd_tm", (128, 16), F32)
    b["gscr"] = al.sbuf("gscr", (128, 2048), F32)
    b["m32a"] = al.sbuf("m32a", (128, 2048), F32)
    b["m32b"] = al.sbuf("m32b", (128, 2048), F32)
    b["dtr"] = al.sbuf("dtr", (32, 2048), BF16)
    b["nea"] = al.sbuf("nea", (32, 1), F32)
    b["i32s"] = al.sbuf("i32s", (32, 32), F32)
    b["nws"] = al.sbuf("nws", (1, 512), F32)
    b["dbt"] = al.sbuf("dbt", (128, 2048), BF16)
    b["xbc"] = al.sbuf("xbc", (128, 18 * 516), BF16)
    b["cscr"] = al.sbuf("cscr", (128, 4608), BF16)
    b["zz"] = al.sbuf("zz", (128, 4096), BF16)
    b["tmv"] = al.sbuf("tmv", (128, 160), F32)
    b["eclcb"] = al.sbuf("eclcb", (128, 32), F32)
    b["clcb"] = al.sbuf("clcb", (128, 32), F32)
    b["btm"] = al.sbuf("btm", (128, 128), BF16)
    b["xT"] = al.sbuf("xT", (128, 2048), BF16)
    b["xW"] = al.sbuf("xW", (128, 2048), BF16)
    b["mt"] = al.sbuf("mt", (128, 256), BF16)
    b["egr"] = al.sbuf("egr", (128, 256), F32)
    b["hst"] = al.sbuf("hst", (128, 2048), BF16)
    b["ygT"] = al.sbuf("ygT", (128, 8192), BF16)
    b["u8k"] = al.sbuf("u8k", (128, 2048), F32)
    b["wos"] = al.sbuf("wos", (128, 2048), BF16)
    b["wms"] = al.sbuf("wms", (128, 512), BF16)
    b["xml"] = al.sbuf("xml", (128, 1024), BF16)
    b["wim"] = al.sbuf("wim", (128, 2048), F32)
    b["zstg"] = b["dbt"]
    b["ytm"] = b["cscr"]
    b["th"] = b["u8k"]
    for i in range(7):
        b[f"ps{i}"] = al.psum(f"ps{i}", (128, 512), F32)
    b["pb7"] = al.psum("pb7", (128, 1024), BF16)
    return b


def host_l0a_weights(inp):
    w = {}
    for sfx in ("f", "b"):
        w[f"wi_{sfx}"] = np.ascontiguousarray(
            np.asarray(inp[f"L0_attn_Wi_{sfx}"], np.float32))
        conv = np.asarray(inp[f"L0_attn_conv_{sfx}"], np.float32)
        w[f"convw_{sfx}"] = np.ascontiguousarray(
            conv.T.reshape(18, 128, 4).transpose(1, 0, 2).reshape(128, 72))
        convb = np.asarray(inp[f"L0_attn_convb_{sfx}"], np.float32)
        w[f"convb_{sfx}"] = np.ascontiguousarray(convb.reshape(18, 128).T)
        w[f"alog_{sfx}"] = np.asarray(
            inp[f"L0_attn_Alog_{sfx}"], np.float32).reshape(32, 1)
        w[f"dtb_{sfx}"] = np.asarray(
            inp[f"L0_attn_dtb_{sfx}"], np.float32).reshape(32, 1)
        w[f"dvec_{sfx}"] = np.asarray(
            inp[f"L0_attn_D_{sfx}"], np.float32).reshape(32, 1)
        w[f"wo_{sfx}"] = np.ascontiguousarray(
            np.asarray(inp[f"L0_attn_Wo_{sfx}"], np.float32))
    w["wm"] = np.ascontiguousarray(np.asarray(inp["L0_attn_Wm"], np.float32))
    w["bmr"] = np.ascontiguousarray(
        np.asarray(inp["L0_attn_bm"], np.float32).reshape(8, 128).T)
    return w


L0A_WSPECS = [("wi_f", (1024, 4384)), ("convw_f", (128, 72)),
              ("convb_f", (128, 18)), ("alog_f", (32, 1)), ("dtb_f", (32, 1)),
              ("dvec_f", (32, 1)), ("wo_f", (2048, 1024)),
              ("wi_b", (1024, 4384)), ("convw_b", (128, 72)),
              ("convb_b", (128, 18)), ("alog_b", (32, 1)), ("dtb_b", (32, 1)),
              ("dvec_b", (32, 1)), ("wo_b", (2048, 1024)),
              ("wm", (1024, 1024)), ("bmr", (128, 8))]


def emit_l0_attn(p, al, cs, b, wd, tag="l0a"):
    nc = p.nc
    zx_d = wd["zx_d"]
    yf_d = wd["yf_d"]
    xm_d = wd["xm_d"]
    wo_bf_d = wd["wo_bf_d"]
    emit_ln_stats(p, cs, b, b["xres"][:], b["meanb"][:], b["rstdb"][:],
                  b["rstd_tm"][:], tag + "ln")
    SMALLW = [("convw", (128, 72)), ("convb", (128, 18)), ("alog", (32, 1)),
              ("dtb", (32, 1)), ("dvec", (32, 1))]
    for nm, shape in SMALLW:
        b[f"{nm}_s"] = al.sbuf(f"{nm}_s", shape, F32)
    b["bmr_s"] = al.sbuf("bmr_s", (128, 8), F32)
    p.dma("POOL", b["bmr_s"][:], wd["bmr"], note="ldwbmr")
    for dirn, sfx in ((0, "f"), (1, "b")):
        for nm, shape in SMALLW:
            p.dma("POOL", b[f"{nm}_s"][:], wd[f"{nm}_{sfx}"],
                  note=f"ldw{nm}{sfx}")
        ws = {"alog": b["alog_s"], "dtb": b["dtb_s"],
              "dvec": b["dvec_s"], "convw": b["convw_s"],
              "convb": b["convb_s"], "wo_bf_d": wo_bf_d,
              "wm_d": wd["wm"], "bmr": b["bmr_s"]}
        emit_l0a_passA(p, cs, b, b["xres"][:], b["rstdb"][:], b["meanb"][:],
                       wd[f"wi_{sfx}"], zx_d, tag + "A" + sfx)
        emit_wo_precast(p, b, wd[f"wo_{sfx}"], wo_bf_d, tag + "W" + sfx)
        emit_l0a_passB(p, cs, b, ws, zx_d, yf_d, xm_d, dirn, b["xres"][:],
                       tag + "B" + sfx)


def build_l0a_only():
    nc = bass.Bass("TRN2", target_bir_lowering=False, debug=False)
    x_d = nc.dram_tensor("x", [2048, 1024], F32, kind="ExternalInput").ap()
    out_d = nc.dram_tensor("out", [2048, 1024], F32, kind="ExternalOutput").ap()
    wd = {}
    for nm, shape in L0A_WSPECS:
        wd[nm] = nc.dram_tensor(nm, list(shape), F32, kind="ExternalInput").ap()
    wd["zx_d"] = nc.dram_tensor("zx_d", [4480, 2048], BF16).ap()
    wd["yf_d"] = nc.dram_tensor("yf_d", [1024, 2048], BF16).ap()
    wd["xm_d"] = nc.dram_tensor("xm_d", [1024, 512], F32).ap()
    wd["wo_bf_d"] = nc.dram_tensor("wo_bf_d", [2048, 1024], BF16).ap()
    al = Alloc(nc)
    p = Prog(nc)
    cs = declare_consts(nc, al, p)
    b = alloc_main_bufs(al)
    emit_load_x(p, cs, b, x_d, b["xres"][:])
    emit_l0_attn(p, al, cs, b, wd)
    emit_store_x(p, cs, b, b["xres"][:], out_d)
    p.finalize()
    al.close()
    return nc


def kernel_l0a(core_inputs):
    nc = build_l0a_only()
    cmaps = []
    consts = host_consts()
    for ci in core_inputs:
        w = host_l0a_weights(ci)
        m = {"x": np.ascontiguousarray(np.asarray(ci["x"], np.float32))}
        for nm, _ in L0A_WSPECS:
            m[nm] = w[nm]
        for k, v in consts.items():
            m[k] = v
        cmaps.append(m)
    res = run_bass_kernel_spmd(nc, cmaps, core_ids=list(range(len(cmaps))))
    return [np.asarray(r["out"]) for r in res.results]



def host_l0f_weights(inp):
    w = {}
    w["wqkv"] = np.ascontiguousarray(np.asarray(inp["L0_ffn_Wqkv"], np.float32))
    w["wo6"] = np.ascontiguousarray(np.asarray(inp["L0_ffn_Wo"], np.float32))
    W1 = np.asarray(inp["L0_ffn_W1"], np.float32)
    w["w1a"] = np.ascontiguousarray(W1[:, :, :2730])
    w["w1g"] = np.ascontiguousarray(W1[:, :, 2730:])
    w["w26"] = np.ascontiguousarray(np.asarray(inp["L0_ffn_W2"], np.float32))
    b1 = np.asarray(inp["L0_ffn_b1"], np.float32)
    pad = np.zeros((6, 2816 - 2730), np.float32)
    w["b1a"] = np.ascontiguousarray(np.concatenate(
        [b1[:, :2730], pad], 1).reshape(6, 22, 128).transpose(0, 2, 1).copy())
    w["b1g"] = np.ascontiguousarray(np.concatenate(
        [b1[:, 2730:], pad], 1).reshape(6, 22, 128).transpose(0, 2, 1).copy())
    w["b2r"] = np.ascontiguousarray(
        np.asarray(inp["L0_ffn_b2"], np.float32).reshape(6, 8, 128)
        .transpose(0, 2, 1).copy())
    w["lng"] = np.ascontiguousarray(
        np.asarray(inp["L0_ffn_ln_g"], np.float32).reshape(6, 8, 128)
        .transpose(0, 2, 1).copy())
    w["lnb"] = np.ascontiguousarray(
        np.asarray(inp["L0_ffn_ln_b"], np.float32).reshape(6, 8, 128)
        .transpose(0, 2, 1).copy())
    w["qscale"] = np.ascontiguousarray(np.asarray(inp["L0_ffn_q_scale"], np.float32))
    w["kscale"] = np.ascontiguousarray(np.asarray(inp["L0_ffn_k_scale"], np.float32))
    return w


def host_l0f_consts():
    c = {}
    DIM_HEAD, WINDOW, n = 64, 256, 2048
    half = np.arange(0, DIM_HEAD, 2, dtype=np.float32)
    inv_freq = 1.0 / 10000.0 ** (half / DIM_HEAD)
    t = np.arange(n, dtype=np.float32)
    freqs = t[:, None] * inv_freq[None, :]
    freqs = np.concatenate([freqs, freqs], -1)
    base_scale = (half + 0.4 * DIM_HEAD) / (1.4 * DIM_HEAD)
    power = (t - n // 2) / (WINDOW // 2)
    scale = base_scale[None, :] ** power[:, None]
    scale = np.concatenate([scale, scale], -1)
    cos, sin = np.cos(freqs), np.sin(freqs)
    sgn = np.concatenate([-np.ones(32, np.float32), np.ones(32, np.float32)])
    c["rt_csq"] = (8.0 * cos * scale).astype(np.float32)
    c["rt_snq"] = (8.0 * sin * scale * sgn[None, :]).astype(np.float32)
    c["rt_csk"] = (cos / scale).astype(np.float32)
    c["rt_snk"] = (sin / scale * sgn[None, :]).astype(np.float32)
    r = np.arange(128)
    j = np.arange(256)
    for s in (0, 1):
        qi = 128 * s + r[:, None]
        prev_dis = (qi > j[None, :]).astype(np.float32)
        cur_dis = (j[None, :] > qi).astype(np.float32)
        c[f"c_mw{s}"] = np.concatenate([prev_dis, cur_dis], 1)
    c["c_negbigb"] = (-1e30 * np.eye(128)).astype(np.float32)
    c["c_onesb"] = np.ones((128, 256), np.float32)
    return c


L0F_CONST_SPECS = [("c_mw0", (128, 512), BF16), ("c_mw1", (128, 512), BF16),
                   ("c_negbigb", (128, 128), BF16), ("c_onesb", (128, 256), BF16)]
L0F_TABLE_SPECS = [("rt_csq", (128, 1024)), ("rt_snq", (128, 1024)),
                   ("rt_csk", (128, 1024)), ("rt_snk", (128, 1024))]


def emit_l0_ffn(p, al, cs, b, wd, tag="l0f"):
    xi = b["xi"][:]
    b["cw_m"] = (b["xml"][:, 0:512], b["xml"][:, 512:1024])
    b["cw_nb"] = b["wms"][:, 0:128]
    b["cw_on"] = b["wms"][:, 128:384]
    p.dma("POOL", b["cw_m"][0], wd["c_mw0"], note=f"{tag}cm0")
    p.dma("POOL", b["cw_m"][1], wd["c_mw1"], note=f"{tag}cm1")
    p.dma("POOL", b["cw_nb"], wd["c_negbigb"], note=f"{tag}cnb")
    p.dma("POOL", b["cw_on"], wd["c_onesb"], note=f"{tag}con")
    for l in range(6):
        emit_l0f_attn(p, cs, b, wd, l, f"{tag}a{l}")
        emit_l0f_ffn(p, cs, b, wd, l, f"{tag}f{l}")


def emit_l0f_attn(p, cs, b, wd, l, tag):
    xi = b["xi"][:]
    qkv_d = wd["qkv_d"]
    v_d = wd["v_d"]
    q_d = wd["q_d"]
    k_d = wd["k_d"]
    a_d = wd["a_d"]
    wqkv3 = wd["wqkv"].rearrange("l (k p) j -> l p k j", p=128)[l]
    gscr = b["gscr"]
    pss = [b["ps0"], b["ps1"], b["ps2"], b["ps3"]]
    pb7 = b["pb7"]

    qsr = b["qsr"]
    p.dma("POOL", qsr[0:1, 0:64], wd["qscale"][l:l + 1, :], note=f"{tag}qs")
    p.dma("POOL", qsr[0:1, 64:128], wd["kscale"][l:l + 1, :], note=f"{tag}ks")
    p.op("PE", lambda e: e.matmul(b["ps6"][:, 0:128], cs["c_onesrow"][:],
                                  qsr[0:1, 0:128], start=True, stop=True),
         reads=[cs["c_onesrow"][:], qsr], writes=[b["ps6"][:]],
         note=f"{tag}qsb")
    p.op("DVE", lambda e: e.tensor_copy(b["qsb"][:], b["ps6"][:, 0:128]),
         reads=[b["ps6"][:]], writes=[b["qsb"][:]], note=f"{tag}qsc")
    for nm, sl_, tb in (("csq", (0, 64), "rt_csq"), ("snq", (0, 64), "rt_snq"),
                        ("csk", (64, 128), "rt_csk"), ("snk", (64, 128), "rt_snk")):
        dst = b["rt_" + nm]
        qsl = b["qsb"][:, sl_[0]:sl_[1]]
        p.dma("POOL", dst[:, :], wd[tb], note=f"{tag}rtl{nm}")
        p.op("DVE", lambda e, dst=dst, qsl=qsl: e.tensor_tensor(
            dst[:, :].rearrange("p (c w) -> p c w", w=64),
            dst[:, :].rearrange("p (c w) -> p c w", w=64),
            qsl.unsqueeze(1).broadcast_to((128, 16, 64)), ALU.mult),
            reads=[dst[:], b["qsb"][:]], writes=[dst[:]], note=f"{tag}rt{nm}")

    wqr = b["wim"]
    for ts in range(4):
        for ng in range(6):
            for k in range(8):
                kk = ng * 8 + k
                sl = wqr[:, (kk % 4) * 512:(kk % 4) * 512 + 512]
                p.dma("POOL", sl.bitcast(F32R),
                      wqkv3[:, k, ng * 512:ng * 512 + 512], note=f"{tag}wq")
                for t in range(4):
                    lh = xi_ap(xi, k, ts * 512 + t * 128, 128, shifted=True)
                    p.op("PE", lambda e, sl=sl, lh=lh, t=t, k=k: e.matmul(
                        pss[t][:], r32(lh), r32(sl), start=(k == 0),
                        stop=(k == 7)),
                        reads=[sl, xi], writes=[pss[t][:]], note=f"{tag}qkv")
            for t in range(4):
                st = (b["qst"], b["kst"], b["vst"])[ng // 2]
                dst = st[:, t * 1024 + (ng % 2) * 512: t * 1024 + (ng % 2) * 512 + 512]
                p.op("ACT", lambda e, dst=dst, t=t: e.activation(
                    dst, pss[t][:], AF.Identity),
                    reads=[pss[t][:]], writes=[dst], note=f"{tag}qkve")
        for t in range(4):
            ct = ts * 4 + t
            p.dma("SP", v_d[ct * 128:ct * 128 + 128, :],
                  b["vst"][:, t * 1024:t * 1024 + 1024], note=f"{tag}vsp{ct}")
            for qk in range(2):
                st = (b["qst"], b["kst"])[qk]
                dd = (q_d, k_d)[qk]
                cst = (b["rt_csq"], b["rt_csk"])[qk]
                snt = (b["rt_snq"], b["rt_snk"])[qk]
                src = st[:, t * 1024:t * 1024 + 1024]
                emit_l0f_prep(p, cs, b, src, cst, snt, ct, dd, f"{tag}p{qk}{ct}")
    for w in range(8):
        emit_l0f_window(p, cs, b, wd, l, w, tag)
    wo3 = wd["wo6"].rearrange("l (k p) j -> l p k j", p=128)[l]
    for tc in range(4):
        for mg in range(2):
            for k in range(8):
                asl = b["wqs"].bitcast(BF16)[:, (k % 2) * 1024:(k % 2) * 1024 + 512]
                p.dma("SP", asl, a_d.rearrange("(k p) t -> p k t", p=128)[
                    :, k, tc * 512:tc * 512 + 512], note=f"{tag}wral")
                wsl = b["wos"][:, (k % 2) * 512:(k % 2) * 512 + 512]
                p.dma("POOL", wsl, wo3[:, k, mg * 512:mg * 512 + 512],
                      note=f"{tag}wrwo")
                for m in range(4):
                    lh = wsl[:, m * 128:m * 128 + 128]
                    p.op("PE", lambda e, lh=lh, asl=asl, m=m, k=k: e.matmul(
                        pss[m][:], lh, asl, start=(k == 0), stop=(k == 7)),
                        reads=[wsl, asl], writes=[pss[m][:]], note=f"{tag}wo")
            for m in range(4):
                mm = mg * 4 + m
                xsl = xi_ap(xi, mm, tc * 512, 512)
                p.op("DVE", lambda e, xsl=xsl, m=m: e.tensor_add(
                    w32(xsl), xsl, pss[m][:]),
                    reads=[xsl, pss[m][:]], writes=[xsl], note=f"{tag}wor")


def emit_l0f_prep(p, cs, b, src, cst, snt, ct, dd, tag):
    gscr = b["gscr"]
    sq = gscr[:, 0:1024]
    p.op("ACT", lambda e: e.activation(sq, src, AF.Square),
         reads=[src], writes=[sq], note=f"{tag}sq")
    ss = b["sst"][:, 0:16]
    p.op("DVE", lambda e: e.reduce_sum(
        ss.unsqueeze(2), sq.rearrange("p (h w) -> p h w", w=64), AX.X),
        reads=[sq], writes=[ss], note=f"{tag}ssum")
    p.op("ACT", lambda e: e.activation(ss, ss, AF.Sqrt),
         reads=[ss], writes=[ss], note=f"{tag}sqrt")
    p.op("DVE", lambda e: e.tensor_scalar_max(ss, ss, 1e-12),
         reads=[ss], writes=[ss], note=f"{tag}mx")
    p.op("DVE", lambda e: e.reciprocal(ss, ss),
         reads=[ss], writes=[ss], note=f"{tag}rc")
    qa = b["qa"]
    p.op("DVE", lambda e: e.tensor_tensor(
        qa[:].rearrange("p (h w) -> p h w", w=64),
        src.rearrange("p (h w) -> p h w", w=64),
        ss.unsqueeze(2).broadcast_to((128, 16, 64)), ALU.mult),
        reads=[src, ss], writes=[qa[:]], note=f"{tag}nrm")
    qb = b["qb"]
    ctab = cst[:, ct * 64:ct * 64 + 64].unsqueeze(1).broadcast_to((128, 16, 64))
    stab = snt[:, ct * 64:ct * 64 + 64].unsqueeze(1).broadcast_to((128, 16, 64))
    p.op("DVE", lambda e: e.tensor_tensor(
        qb[:].rearrange("p (h w) -> p h w", w=64),
        qa[:].rearrange("p (h w) -> p h w", w=64), ctab, ALU.mult),
        reads=[qa[:], cst[:]], writes=[qb[:]], note=f"{tag}cs")
    qsw = qa[:].rearrange("p (h u v) -> p h u v", u=2, v=32)[:, :, ::-1, :]
    p.op("DVE", lambda e: e.tensor_tensor(
        qa[:].rearrange("p (h u v) -> p h u v", u=2, v=32) if False else
        b["qc"][:].rearrange("p (h u v) -> p h u v", u=2, v=32),
        qsw, stab.rearrange("p h (u v) -> p h u v", v=32), ALU.mult),
        reads=[qa[:], snt[:]], writes=[b["qc"][:]], note=f"{tag}sn")
    p.op("DVE", lambda e: e.tensor_add(qb[:], qb[:], b["qc"][:]),
         reads=[qb[:], b["qc"][:]], writes=[qb[:]], note=f"{tag}add")
    pb7 = b["pb7"]
    for j in range(8):
        blk = qb[:, j * 128:j * 128 + 128]
        p.op("PE", lambda e, blk=blk, j=j: e.transpose(
            pb7[:, j * 128:j * 128 + 128], blk, cs["c_identb"][:]),
            reads=[blk, cs["c_identb"][:]], writes=[pb7[:]], note=f"{tag}T")
    fmst = b["fmst"]
    p.op("DVE", lambda e: e.tensor_copy(fmst[:], pb7[:]),
         reads=[pb7[:]], writes=[fmst[:]], note=f"{tag}Te")
    p.dma("SP", dd.rearrange("(m p) t -> p m t", p=128)[:, :, ct * 128:ct * 128 + 128],
          fmst[:].rearrange("p (m t) -> p m t", t=128), note=f"{tag}sp")


def emit_l0f_window(p, cs, b, wd, l, w, tag):
    q_d, k_d, v_d, a_d = wd["q_d"], wd["k_d"], wd["v_d"], wd["a_d"]
    pb7 = b["pb7"]
    p.dma("SP", b["qw"][:].rearrange("p (m t) -> p m t", t=256),
          q_d.rearrange("(m p) t -> p m t", p=128)[:, :, w * 256:w * 256 + 256],
          note=f"{tag}lq{w}")
    ksl = b["kring"][:, (w % 2) * 2048:(w % 2) * 2048 + 2048]
    p.dma("SP", ksl.rearrange("p (m t) -> p m t", t=256),
          k_d.rearrange("(m p) t -> p m t", p=128)[:, :, w * 256:w * 256 + 256],
          note=f"{tag}lk{w}")
    vsl = b["vring"][:, (w % 2) * 2048:(w % 2) * 2048 + 2048]
    p.dma("SP", vsl.rearrange("p (s t) -> p s t", t=1024),
          v_d.rearrange("(s p) c -> p s c", p=128).rearrange(
              "p (w s) c -> p w s c", s=2)[:, w, :, :],
          note=f"{tag}lv{w}")
    att = b["att_tm"]
    for sub in range(2):
        for h in range(16):
            par = (sub * 16 + h) % 2
            psS = b["ps4"] if par == 0 else b["ps0"]
            psAV = b["ps5"] if par == 0 else b["ps1"]
            pst = b["pstl"] if par == 0 else b["xT"][:, 1024:1536]
            ptsb = b["ptsb"] if par == 0 else b["xT"][:, 1536:2048]
            pbh = pb7[:, par * 512:par * 512 + 512]
            stc = b["sst"][:, 16:19] if par == 0 else b["sst2"][:, 0:3]
            mask = b["cw_m"][sub]
            p.op("PE", lambda e, mask=mask, psS=psS: e.matmul(
                psS[:], b["cw_nb"], mask, start=True, stop=False),
                reads=[b["cw_nb"], mask], writes=[psS[:]],
                note=f"{tag}mk")
            if w == 0:
                p.op("PE", lambda e, psS=psS: e.matmul(
                    psS[:, 0:256], b["cw_nb"], b["cw_on"], start=False,
                    stop=False),
                    reads=[b["cw_nb"], b["cw_on"]], writes=[psS[:]],
                    note=f"{tag}mk0")
            pc, half = h // 2, (h % 2) * 64
            qsl = b["qw"][:, :].rearrange("p (m t) -> p m t", t=256)[
                :, pc, sub * 128:sub * 128 + 128][half:half + 64, :]
            if w > 0:
                kprev = b["kring"][:, ((w - 1) % 2) * 2048 + pc * 256:
                                   ((w - 1) % 2) * 2048 + pc * 256 + 256][
                    half:half + 64, :]
                p.op("PE", lambda e, qsl=qsl, kprev=kprev, psS=psS: e.matmul(
                    psS[:, 0:256], qsl, kprev, start=False, stop=False),
                    reads=[qsl, b["kring"][:]], writes=[psS[:]], note=f"{tag}s0")
            kcur = ksl.rearrange("p (m t) -> p m t", t=256)[:, pc, :][
                half:half + 64, :]
            p.op("PE", lambda e, qsl=qsl, kcur=kcur, psS=psS: e.matmul(
                psS[:, 256:512], qsl, kcur, start=False, stop=True),
                reads=[qsl, b["kring"][:]], writes=[psS[:]], note=f"{tag}s1")
            lsum = stc[:, 1:2]
            def _exp(e, pst=pst, lsum=lsum, psS=psS):
                return e.activation(pst[:], psS[:], AF.Exp,
                                    scale=1.0, accum_out=lsum)
            p.op("ACT", _exp, reads=[psS[:]], writes=[pst[:], lsum],
                 note=f"{tag}ex")
            rcc = stc[:, 2:3]
            p.op("DVE", lambda e, lsum=lsum, rcc=rcc: e.reciprocal(
                rcc, lsum),
                reads=[lsum], writes=[rcc], note=f"{tag}rc")
            for bb in range(4):
                blk = pst[:, bb * 128:bb * 128 + 128]
                p.op("PE", lambda e, blk=blk, bb=bb, pbh=pbh: e.transpose(
                    pbh[:, bb * 128:bb * 128 + 128], blk, cs["c_identb"][:]),
                    reads=[blk, cs["c_identb"][:]], writes=[pb7[:]],
                    note=f"{tag}pT")
            p.op("DVE", lambda e, ptsb=ptsb, pbh=pbh: e.tensor_copy(
                ptsb, pbh),
                reads=[pb7[:]], writes=[ptsb], note=f"{tag}pTe")
            b0 = 2 if w == 0 else 0
            for bb in range(b0, 4):
                lh = ptsb[:, bb * 128:bb * 128 + 128]
                slot = ((w - 1) % 2) if bb < 2 else (w % 2)
                vv = b["vring"][:, slot * 2048 + (bb % 2) * 1024 + h * 64:
                                slot * 2048 + (bb % 2) * 1024 + h * 64 + 64]
                p.op("PE", lambda e, lh=lh, vv=vv, bb=bb, b0=b0, psAV=psAV:
                     e.matmul(psAV[:, 0:64], lh, vv, start=(bb == b0),
                              stop=(bb == 3)),
                    reads=[lh, b["vring"][:]], writes=[psAV[:]], note=f"{tag}av")
            dst = att[:, sub * 1024 + h * 64:sub * 1024 + h * 64 + 64]
            p.op("DVE", lambda e, dst=dst, psAV=psAV, rcc=rcc: e.tensor_scalar(
                dst, psAV[:, 0:64], rcc, None, ALU.mult),
                reads=[psAV[:], rcc], writes=[dst],
                note=f"{tag}ave")
            if h % 2 == 1:
                j = h // 2
                blk = att[:, sub * 1024 + j * 128:sub * 1024 + j * 128 + 128]
                ps2b = b["ps2"][:].bitcast(BF16)
                p.op("PE", lambda e, blk=blk, j=j, ps2b=ps2b: e.transpose(
                    ps2b[:, j * 128:j * 128 + 128], blk, cs["c_identb"][:]),
                    reads=[blk, cs["c_identb"][:]], writes=[b["ps2"][:]],
                    note=f"{tag}aT")
            if h == 15:
                ps2b = b["ps2"][:].bitcast(BF16)
                p.op("DVE", lambda e, ps2b=ps2b: e.tensor_copy(
                    b["fmst"][:], ps2b[:, 0:1024]),
                    reads=[b["ps2"][:]], writes=[b["fmst"][:]],
                    note=f"{tag}aTe")
                p.dma("SP", a_d.rearrange("(m p) t -> p m t", p=128)[
                    :, :, w * 256 + sub * 128:w * 256 + sub * 128 + 128],
                    b["fmst"][:].rearrange("p (m t) -> p m t", t=128),
                    note=f"{tag}asp")


def emit_l0f_ffn(p, cs, b, wd, l, tag):
    xi = b["xi"][:]
    gscr = b["gscr"]
    pss = [b["ps0"], b["ps1"], b["ps2"], b["ps3"]]
    a_dd, g_dd = wd["a_dd"], wd["g_dd"]
    xfn = lambda pc, t0, n: xi_ap(xi, pc, t0, n, shifted=True)
    emit_ln_stats(p, cs, b, None, b["meanb"][:], b["rstdb"][:],
                  b["rstd_tm"][:], tag + "ln", xfn=xfn)
    lngr = b["lngr"]
    p.dma("POOL", lngr[:, 0:8], wd["lng"][l], note=f"{tag}lg")
    p.dma("POOL", lngr[:, 8:16], wd["lnb"][l], note=f"{tag}lb")
    for pc in range(8):
        for tc in range(4):
            xs = xi_ap(xi, pc, tc * 512, 512, shifted=True)
            t_ = gscr[:, 0:512]
            mb = b["meanb"][:, tc * 512:tc * 512 + 512]
            rb = b["rstdb"][:, tc * 512:tc * 512 + 512]
            dst = xln_sl(b, pc, tc * 512, 512)
            p.op("DVE", lambda e, xs=xs, t_=t_, mb=mb: e.tensor_sub(t_, xs, mb),
                 reads=[xs, mb], writes=[t_], note=f"{tag}x1")
            p.op("DVE", lambda e, t_=t_, rb=rb: e.tensor_mul(t_, t_, rb),
                 reads=[t_, rb], writes=[t_], note=f"{tag}x2")
            p.op("DVE", lambda e, t_=t_, dst=dst, pc=pc: e.tensor_scalar(
                dst, t_, lngr[:, pc:pc + 1], lngr[:, 8 + pc:9 + pc],
                ALU.mult, ALU.add),
                reads=[t_, lngr[:]], writes=[dst], note=f"{tag}x3")
    for part, wname, dd in ((0, "w1a", a_dd), (1, "w1g", g_dd)):
        w3 = wd[wname].rearrange("l (k p) j -> l p k j", p=128)[l]
        for m in range(22):
            mw = 128 if m < 21 else 42
            sl = b["cscr"][:, (m % 2) * 2048:(m % 2) * 2048 + 2048]
            sl3 = sl.rearrange("p (k w) -> p k w", k=8)[:, :, 0:mw]
            p.dma("POOL", sl3, w3[:, :, m * 128:m * 128 + mw],
                  note=f"{tag}w1l{part}")
            for tc in range(4):
                ps = pss[tc]
                for k in range(8):
                    wk = sl[:, k * 256:k * 256 + mw]
                    xsl = xln_sl(b, k, tc * 512, 512)
                    p.op("PE", lambda e, wk=wk, xsl=xsl, ps=ps, k=k, mw=mw:
                         e.matmul(ps[0:mw, :], wk, xsl, start=(k == 0),
                                  stop=(k == 7)),
                         reads=[wk, xsl], writes=[ps[:]], note=f"{tag}w1m")
                stg = b["zstg"][:, tc * 512:tc * 512 + 512]
                p.op("ACT", lambda e, ps=ps, stg=stg, mw=mw: e.activation(
                    stg[0:mw, :], ps[0:mw, :], AF.Identity),
                    reads=[ps[:]], writes=[stg], note=f"{tag}w1e")
            p.dma("SP", dd[m * 128:m * 128 + mw, :], b["zstg"][0:mw, :],
                  note=f"{tag}hgo{part}")
    b1s = b["lngr"]
    p.dma("POOL", b1s[:, 16:38], wd["b1a"][l], note=f"{tag}b1a")
    p.dma("POOL", b1s[:, 38:60], wd["b1g"][l], note=f"{tag}b1g")
    p.dma("POOL", b["b2s"][:], wd["b2r"][l], note=f"{tag}b2")
    for tc in range(4):
        for k in range(22):
            kw = 128 if k < 21 else 42
            asl = b["qa"][:]
            p.dma("SP", asl[0:kw, 0:512],
                  a_dd[k * 128:k * 128 + kw, tc * 512:tc * 512 + 512],
                  note=f"{tag}pla")
            p.dma("SP", asl[0:kw, 512:1024],
                  g_dd[k * 128:k * 128 + kw, tc * 512:tc * 512 + 512],
                  note=f"{tag}plg")
            gl = b["qb"][:, 0:512]
            p.op("ACT", lambda e, gl=gl, asl=asl, k=k, kw=kw: e.activation(
                gl[0:kw, :], asl[0:kw, 512:1024], AF.Gelu,
                bias=b1s[0:kw, 38 + k:39 + k]),
                reads=[asl, b1s[:]], writes=[gl], note=f"{tag}gel")
            dst = prod_sl(b, k)
            p.op("DVE", lambda e, dst=dst, asl=asl, gl=gl, k=k, kw=kw:
                 e.scalar_tensor_tensor(dst[0:kw, :], asl[0:kw, 0:512],
                                        b1s[0:kw, 16 + k:17 + k], gl[0:kw, :],
                                        ALU.add, ALU.mult),
                 reads=[asl, gl, b1s[:]], writes=[dst], note=f"{tag}prd")
        for mg in range(2):
            for k in range(22):
                kw = 128 if k < 21 else 42
                wsl = b["wos"][:, (k % 2) * 512:(k % 2) * 512 + 512]
                p.dma("POOL", wsl[0:kw, :],
                      wd["w26"][l][k * 128:k * 128 + kw,
                                   mg * 512:mg * 512 + 512],
                      note=f"{tag}w2l")
                prk = prod_sl(b, k)
                for m in range(4):
                    lh = wsl[0:kw, m * 128:m * 128 + 128]
                    p.op("PE", lambda e, lh=lh, prk=prk, m=m, k=k, kw=kw:
                         e.matmul(pss[m][:], lh, prk[0:kw, :],
                                  start=(k == 0), stop=(k == 21)),
                         reads=[wsl, prk], writes=[pss[m][:]], note=f"{tag}w2m")
            for m in range(4):
                mm = mg * 4 + m
                xsl = xi_ap(xi, mm, tc * 512, 512)
                p.op("DVE", lambda e, xsl=xsl, m=m, mm=mm:
                     e.scalar_tensor_tensor(w32(xsl), pss[m][:],
                                            b["b2s"][:, mm:mm + 1], xsl,
                                            ALU.add, ALU.add),
                     reads=[pss[m][:], b["b2s"][:], xsl], writes=[xsl],
                     note=f"{tag}w2r")


def alloc_l0f_bufs(al, b):
    b["xi"] = b["xres"]
    b["qst"] = b["zz"]
    b["kst"] = b["ygT"][:, 0:4096]
    b["vst"] = b["ygT"][:, 4096:8192]
    b["wqs"] = b["wim"][:, 0:1024]
    xbcb = b["xbc"]
    b["rt_csq"] = xbcb[:, 0:1024]
    b["rt_snq"] = xbcb[:, 1024:2048]
    b["rt_csk"] = xbcb[:, 2048:3072]
    b["rt_snk"] = xbcb[:, 3072:4096]
    b["qa"] = xbcb[:, 4096:5120]
    b["qb"] = xbcb[:, 5120:6144]
    b["qc"] = xbcb[:, 6144:7168]
    b["fmst"] = xbcb[:, 7168:8192]
    b["qw"] = b["dbt"]
    b["kring"] = b["cscr"][:, 0:4096]
    b["vring"] = b["u8k"][:].bitcast(BF16)
    b["att_tm"] = b["hst"]
    b["pstl"] = b["xT"][:, 0:512]
    b["ptsb"] = b["xT"][:, 512:1024]
    b["qsr"] = b["mt"][:].bitcast(F32)[0:1, 0:128]
    b["qsb"] = b["egr"][:, 0:128]
    b["sst"] = b["tmv"][:, 0:20]
    b["lngr"] = b["tmv"][:, 20:84]
    b["b2s"] = b["tmv"][:, 84:92]
    b["sst2"] = b["tmv"][:, 92:112]


def xln_sl(b, pc, t0, n):
    if pc < 4:
        return b["ygT"][:, pc * 2048 + t0:pc * 2048 + t0 + n]
    return b["xbc"][:, (pc - 4) * 2048 + t0:(pc - 4) * 2048 + t0 + n]


def prod_sl(b, k, n=512):
    if k < 8:
        return b["zz"][:, k * 512:k * 512 + n]
    if k < 17:
        return b["cscr"][:, (k - 8) * 512:(k - 8) * 512 + n]
    return b["u8k"][:].bitcast(BF16)[:, (k - 17) * 512:(k - 17) * 512 + n]



def emit_l0_ffn_block(p, al, cs, b, wd, tag="l0f"):
    xres = b["xres"][:]
    xres_d = wd["xres_d"]
    gscr = b["gscr"]
    for pc in range(8):
        p.dma("SP", xres_d[pc * 128:pc * 128 + 128, :]
              if False else xres_d.rearrange("(c p) t -> p c t", p=128)[:, pc, :],
              xi_ap(xres, pc, 0, 2048), note=f"{tag}sp{pc}")
    emit_ln_stats(p, cs, b, xres, b["meanb"][:], b["rstdb"][:],
                  b["rstd_tm"][:], tag + "ln0")
    for pc in range(8):
        for tc in range(4):
            xs = xi_ap(xres, pc, tc * 512, 512)
            t_ = gscr[:, 0:512]
            mb = b["meanb"][:, tc * 512:tc * 512 + 512]
            rb = b["rstdb"][:, tc * 512:tc * 512 + 512]
            p.op("DVE", lambda e, xs=xs, t_=t_, mb=mb: e.tensor_sub(t_, xs, mb),
                 reads=[xs, mb], writes=[t_], note=f"{tag}l1")
            p.op("DVE", lambda e, xs=xs, t_=t_, rb=rb: e.tensor_mul(
                w32(xs), t_, rb),
                reads=[t_, rb], writes=[xs], note=f"{tag}l2")
    emit_l0_ffn(p, al, cs, b, wd, tag)
    for pc in range(8):
        for tc in range(4):
            ld = gscr[:, 512:1024]
            p.dma("SP", ld, xres_d.rearrange("(c p) t -> p c t", p=128)[
                :, pc, tc * 512:tc * 512 + 512], note=f"{tag}rl")
            xs = xi_ap(xres, pc, tc * 512, 512)
            p.op("DVE", lambda e, xs=xs, ld=ld: e.tensor_add(w32(xs), xs, ld),
                 reads=[xs, ld], writes=[xs], note=f"{tag}ra")


L0F_WSPECS = [("wqkv", (6, 1024, 3072)), ("wo6", (6, 1024, 1024)),
              ("w1a", (6, 1024, 2730)), ("w1g", (6, 1024, 2730)),
              ("w26", (6, 2730, 1024)), ("b1a", (6, 128, 22)),
              ("b1g", (6, 128, 22)), ("b2r", (6, 128, 8)),
              ("lng", (6, 128, 8)), ("lnb", (6, 128, 8)),
              ("qscale", (6, 64)), ("kscale", (6, 64))]


def build_l0f_only():
    nc = bass.Bass("TRN2", target_bir_lowering=False, debug=False)
    x_d = nc.dram_tensor("x", [2048, 1024], F32, kind="ExternalInput").ap()
    out_d = nc.dram_tensor("out", [2048, 1024], F32, kind="ExternalOutput").ap()
    wd = {}
    for nm, shape in L0F_WSPECS:
        wd[nm] = nc.dram_tensor(nm, list(shape), F32, kind="ExternalInput").ap()
    for nm, shape, dt_ in L0F_CONST_SPECS:
        wd[nm] = nc.dram_tensor(nm, list(shape), F32, kind="ExternalInput").ap()
    for nm, shape in L0F_TABLE_SPECS:
        wd[nm] = nc.dram_tensor(nm, list(shape), F32, kind="ExternalInput").ap()
    wd["qkv_d"] = nc.dram_tensor("qkv_d", [3072, 2048], BF16).ap()
    wd["v_d"] = nc.dram_tensor("v_d", [2048, 1024], BF16).ap()
    wd["q_d"] = nc.dram_tensor("q_d", [1024, 2048], BF16).ap()
    wd["k_d"] = nc.dram_tensor("k_d", [1024, 2048], BF16).ap()
    wd["a_d"] = nc.dram_tensor("a_d", [1024, 2048], BF16).ap()
    wd["a_dd"] = nc.dram_tensor("a_dd", [2816, 2048], BF16).ap()
    wd["g_dd"] = nc.dram_tensor("g_dd", [2816, 2048], BF16).ap()
    wd["xres_d"] = nc.dram_tensor("xres_d", [1024, 2048], F32).ap()
    al = Alloc(nc)
    p = Prog(nc)
    cs = declare_consts(nc, al, p)
    b = alloc_main_bufs(al)
    alloc_l0f_bufs(al, b)
    emit_load_x(p, cs, b, x_d, b["xres"][:])
    emit_l0_ffn_block(p, al, cs, b, wd)
    emit_store_x(p, cs, b, b["xres"][:], out_d)
    p.finalize()
    al.close()
    return nc


def host_l0f_const_stage(c):
    out = {}
    for nm, shape, dt_ in L0F_CONST_SPECS:
        out[nm] = np.ascontiguousarray(c[nm].astype(np.float32))
    for nm, shape in L0F_TABLE_SPECS:
        v = c[nm].reshape(16, 128, 64).transpose(1, 0, 2).reshape(128, 1024)
        out[nm] = np.ascontiguousarray(v.astype(np.float32))
    return out


def kernel_l0f(core_inputs):
    nc = build_l0f_only()
    consts = host_consts()
    fcon = host_l0f_const_stage(host_l0f_consts())
    cmaps = []
    for ci in core_inputs:
        w = host_l0f_weights(ci)
        m = {"x": np.ascontiguousarray(np.asarray(ci["x"], np.float32))}
        for nm, _ in L0F_WSPECS:
            m[nm] = w[nm]
        m.update(consts)
        m.update(fcon)
        cmaps.append(m)
    res = run_bass_kernel_spmd(nc, cmaps, core_ids=list(range(len(cmaps))))
    return [np.asarray(r["out"]) for r in res.results]



def host_l1a_weights(inp):
    w = {}
    w["wq_t"] = np.ascontiguousarray(np.asarray(inp["L1_attn_Wq"], np.float32).T)
    w["wk_t"] = np.ascontiguousarray(np.asarray(inp["L1_attn_Wk"], np.float32).T)
    w["wv_t"] = np.ascontiguousarray(np.asarray(inp["L1_attn_Wv"], np.float32).T)
    w["wg_t"] = np.ascontiguousarray(np.asarray(inp["L1_attn_Wg"], np.float32).T)
    w["wgk1_t"] = np.ascontiguousarray(np.asarray(inp["L1_attn_Wgk1"], np.float32).T)
    w["wgk2_t"] = np.ascontiguousarray(np.asarray(inp["L1_attn_Wgk2"], np.float32).T)
    w["wout_t"] = np.ascontiguousarray(np.asarray(inp["L1_attn_Wout"], np.float32).T)
    w["bgk2c"] = np.asarray(inp["L1_attn_bgk2"], np.float32).reshape(8, 128).T.copy()
    w["rmsw"] = np.asarray(inp["L1_attn_rms_w"], np.float32).reshape(1, 512).copy()
    return w


L1A_WSPECS = [("wq_t", (1024, 1024)), ("wk_t", (1024, 1024)),
              ("wv_t", (1024, 2048)), ("wg_t", (1024, 2048)),
              ("wgk1_t", (1024, 32)), ("wgk2_t", (32, 1024)),
              ("wout_t", (2048, 1024)), ("bgk2c", (128, 8)), ("rmsw", (1, 512))]


def emit_proj_fm(p, cs, b, x, w_d, out_d, ncols, tag, evac_rstd=True):
    negones = cs["c_negones"]
    wim, zstg, nws = b["wim"], b["zstg"], b["nws"]
    nm = (ncols + 127) // 128
    for m in range(nm):
        mw = min(128, ncols - m * 128)
        sl = wim[:, (m % 2) * 1024:(m % 2) * 1024 + 1024]
        sl3 = sl.rearrange("p (k w) -> p k w", k=8)[:, :, 0:mw]
        p.dma("POOL", sl3.bitcast(F32R), w_d.rearrange("(k p) j -> p k j",
              p=128)[:, :, m * 128:m * 128 + mw], note=f"{tag}wl{m}")
        for k in range(8):
            wk = sl[:, k * 128:k * 128 + mw]
            p.op("PE", lambda e, wk=wk, k=k, mw=mw: e.matmul(
                b["ps4"][0:1, 0:mw], negones[:], r32(wk),
                start=(k == 0), stop=(k == 7)),
                reads=[negones[:], wk], writes=[b["ps4"][:]], note=f"{tag}ws{m}")
        p.op("DVE", lambda e, mw=mw: e.tensor_copy(w32(nws[0:1, 0:mw]),
                                                   b["ps4"][0:1, 0:mw]),
             reads=[b["ps4"][:]], writes=[nws[:]], note=f"{tag}wc{m}")
        for tc in range(4):
            ps = b[f"ps{tc}"]
            for k in range(8):
                wk = sl[:, k * 128:k * 128 + mw]
                xs = xi_ap(x, k, tc * 512, 512)
                p.op("PE", lambda e, wk=wk, xs=xs, ps=ps, k=k, mw=mw: e.matmul(
                    ps[0:mw, :], r32(wk), r32(xs), start=(k == 0), stop=False),
                    reads=[wk, xs], writes=[ps[:]], note=f"{tag}mm{m}")
            mrr = b["meanb"][0:1, tc * 512:tc * 512 + 512]
            p.op("PE", lambda e, mrr=mrr, ps=ps, mw=mw: e.matmul(
                ps[0:mw, :], r32(nws[0:1, 0:mw]), r32(mrr), start=False,
                stop=True),
                reads=[nws[:], mrr], writes=[ps[:]], note=f"{tag}r1{m}")
            stg = zstg[:, tc * 512:tc * 512 + 512]
            rs = b["rstdb"][:, tc * 512:tc * 512 + 512]
            p.op("DVE", lambda e, ps=ps, stg=stg, rs=rs, mw=mw: e.tensor_mul(
                stg[0:mw, :], ps[0:mw, :], rs[0:mw, :]),
                reads=[ps[:], rs], writes=[stg], note=f"{tag}ev{m}")
        p.dma("SP", out_d[m * 128:m * 128 + mw, :], zstg[0:mw, :],
              note=f"{tag}o{m}")


def emit_proj_tm(p, cs, b, x, w_d, out_d, ncols, tag, silu=False):
    negones = cs["c_negones"]
    pss = [b["ps0"], b["ps1"], b["ps2"], b["ps3"]]
    w3 = w_d.rearrange("(k p) j -> p k j", p=128)
    nn = ncols // 512
    for ts in range(4):
        for ng in range(nn):
            for k in range(8):
                sl = b["wim"][:, (k % 2) * 512:(k % 2) * 512 + 512]
                p.dma("POOL", sl.bitcast(F32R),
                      w3[:, k, ng * 512:ng * 512 + 512], note=f"{tag}wl")
                p.op("PE", lambda e, sl=sl, k=k: e.matmul(
                    b["ps4"][0:1, :], negones[:], r32(sl),
                    start=(k == 0), stop=(k == 7)),
                    reads=[negones[:], sl], writes=[b["ps4"][:]],
                    note=f"{tag}ws")
                for t in range(4):
                    lh = xi_ap(x, k, ts * 512 + t * 128, 128)
                    p.op("PE", lambda e, sl=sl, lh=lh, t=t, k=k: e.matmul(
                        pss[t][:], r32(lh), r32(sl), start=(k == 0),
                        stop=False),
                        reads=[sl, lh], writes=[pss[t][:]], note=f"{tag}mm")
            p.op("DVE", lambda e: e.tensor_copy(w32(b["nws"][0:1, :]),
                                                b["ps4"][0:1, :]),
                 reads=[b["ps4"][:]], writes=[b["nws"][:]], note=f"{tag}wc")
            for t in range(4):
                mrr = b["meanb"][0:1, ts * 512 + t * 128:ts * 512 + t * 128 + 128]
                p.op("PE", lambda e, mrr=mrr, t=t: e.matmul(
                    pss[t][:], r32(mrr), r32(b["nws"][0:1, :]), start=False,
                    stop=True),
                    reads=[b["nws"][:], mrr], writes=[pss[t][:]],
                    note=f"{tag}r1")
                ct = ts * 4 + t
                stg = b["zstg"][:, t * 512:t * 512 + 512]
                rcol = b["rstd_tm"][:, ct:ct + 1]
                if silu:
                    p.op("ACT", lambda e, stg=stg, rcol=rcol, t=t: e.activation(
                        stg, pss[t][:], AF.Silu, scale=rcol),
                        reads=[pss[t][:], b["rstd_tm"][:]], writes=[stg],
                        note=f"{tag}evs")
                else:
                    p.op("DVE", lambda e, stg=stg, rcol=rcol, t=t:
                         e.tensor_scalar(stg, pss[t][:], rcol, None, ALU.mult),
                         reads=[pss[t][:], b["rstd_tm"][:]], writes=[stg],
                         note=f"{tag}ev")
            for t in range(4):
                p.dma("SP", out_d[ts * 512 + t * 128:ts * 512 + t * 128 + 128,
                                  ng * 512:ng * 512 + 512],
                      b["zstg"][:, t * 512:t * 512 + 512], note=f"{tag}o")


def emit_l1_attn(p, al, cs, b, wd, tag="l1a"):
    xres = b["xres"][:]
    gscr = b["gscr"]
    pss = [b["ps0"], b["ps1"], b["ps2"], b["ps3"]]
    pb7 = b["pb7"]
    emit_ln_stats(p, cs, b, xres, b["meanb"][:], b["rstdb"][:],
                  b["rstd_tm"][:], tag + "ln")
    emit_proj_fm(p, cs, b, xres, wd["wq_t"], wd["q_dd"], 1024, tag + "q")
    emit_proj_fm(p, cs, b, xres, wd["wk_t"], wd["k_dd"], 1024, tag + "k")
    emit_proj_fm(p, cs, b, xres, wd["wgk1_t"], wd["t1_dd"], 32, tag + "t1")
    emit_proj_tm(p, cs, b, xres, wd["wv_t"], wd["v_dd"], 2048, tag + "v")
    emit_proj_tm(p, cs, b, xres, wd["wg_t"], wd["g_dd"], 2048, tag + "g",
                 silu=True)
    t1s = b["dtr"]
    p.dma("SP", t1s[:], wd["t1_dd"][0:32, :], note=f"{tag}t1l")
    wg2 = b["qsb"][:].bitcast(BF16)[0:32, 0:128]
    bgk = b["b2s"]
    p.dma("POOL", bgk[:], wd["bgk2c"], note=f"{tag}bgk")
    gscr = b["gscr"]
    for pc in range(8):
        p.dma("POOL", wg2, wd["wgk2_t"][:, pc * 128:pc * 128 + 128],
              note=f"{tag}w2l{pc}")
        for tc in range(4):
            p.op("PE", lambda e, tc=tc: e.matmul(
                b["ps4"][:], wg2, t1s[:, tc * 512:tc * 512 + 512],
                start=True, stop=True),
                reads=[wg2, t1s[:]], writes=[b["ps4"][:]], note=f"{tag}gm")
            zt = gscr[:, 0:512]
            p.op("DVE", lambda e, zt=zt, pc=pc: e.tensor_scalar(
                zt, b["ps4"][:], bgk[:, pc:pc + 1], None, ALU.add),
                reads=[b["ps4"][:], bgk[:]], writes=[zt], note=f"{tag}gz")
            t_ = gscr[:, 512:1024]
            p.op("DVE", lambda e, zt=zt, t_=t_: e.tensor_scalar(
                t_, zt, -1.0, 0.0, ALU.mult, ALU.max),
                reads=[zt], writes=[t_], note=f"{tag}gr")
            p.op("ACT", lambda e, zt=zt: e.activation(zt, zt, AF.Abs),
                 reads=[zt], writes=[zt], note=f"{tag}ga")
            p.op("ACT", lambda e, zt=zt: e.activation(zt, zt, AF.Exp,
                                                      scale=-1.0),
                 reads=[zt], writes=[zt], note=f"{tag}ge")
            p.op("ACT", lambda e, zt=zt: e.activation(zt, zt, AF.Ln, bias=1.0),
                 reads=[zt, cs["c_fvals"][:]], writes=[zt], note=f"{tag}gl")
            p.op("DVE", lambda e, zt=zt, t_=t_: e.tensor_add(zt, zt, t_),
                 reads=[zt, t_], writes=[zt], note=f"{tag}gs")
            bs = gscr[:, 1024:1536]
            p.op("DVE", lambda e, zt=zt, bs=bs: e.tensor_scalar_mul(
                bs, zt, -1.0 / 32.0),
                reads=[zt], writes=[bs], note=f"{tag}gn")
            for j in range(4):
                s = slice(1024 + j * 128, 1024 + j * 128 + 128)
                p.op("DVE", lambda e, s=s: e.tensor_tensor_scan(
                    gscr[:, s], gscr[:, s], gscr[:, s], 0.0, ALU.add,
                    ALU.bypass),
                    reads=[gscr[:, s]], writes=[gscr[:, s]], note=f"{tag}bc")
            p.dma("SP", wd["bfm_d"][pc * 128:pc * 128 + 128,
                                    tc * 512:tc * 512 + 512], bs,
                  note=f"{tag}bo{pc}")
    rwb = b["rwb"]
    for half in range(4):
        p.dma("SP", b["qsr"][0:1, 0:128],
              wd["rmsw"][0:1, half * 128:half * 128 + 128], note=f"{tag}rw{half}")
        p.op("PE", lambda e, half=half: e.matmul(
            b["ps5"][:, 0:128], cs["c_onesrow"][:], b["qsr"][0:1, 0:128],
            start=True, stop=True),
            reads=[cs["c_onesrow"][:], b["qsr"]], writes=[b["ps5"][:]],
            note=f"{tag}rwm")
        p.op("DVE", lambda e, half=half: e.tensor_copy(
            rwb[:, half * 128:half * 128 + 128], b["ps5"][:, 0:128]),
            reads=[b["ps5"][:]], writes=[rwb[:]], note=f"{tag}rwc")
    p.op("DVE", lambda e: e.memset(b["Sst"][:], 0.0),
         reads=[], writes=[b["Sst"][:]], note=f"{tag}S0")
    for ccg in range(8):
        for ci2 in range(2):
            cc = ccg * 2 + ci2
            emit_l1a_chunk(p, cs, b, wd, cc, tag)
        emit_l1a_out2(p, cs, b, wd, ccg, tag)


def emit_l1a_chunk(p, cs, b, wd, cc, tag):
    gscr = b["gscr"]
    pb7 = b["pb7"]
    csl = slice(cc * 128, cc * 128 + 128)
    bfml = b["bfml"][:, (cc % 2) * 1024:(cc % 2) * 1024 + 1024]
    p.dma("SP", bfml.rearrange("p (m w) -> p m w", w=128),
          wd["bfm_d"].rearrange("(m p) t -> p m t", p=128)[:, :, csl],
          note=f"{tag}lb{cc}")
    qld = b["qld"][:, (cc % 2) * 1024:(cc % 2) * 1024 + 1024]
    kld = b["kld"][:, (cc % 2) * 1024:(cc % 2) * 1024 + 1024]
    p.dma("SP", qld.rearrange("p (m w) -> p m w", w=128),
          wd["q_dd"].rearrange("(m p) t -> p m t", p=128)[:, :, csl],
          note=f"{tag}lq{cc}")
    p.dma("SP", kld.rearrange("p (m w) -> p m w", w=128),
          wd["k_dd"].rearrange("(m p) t -> p m t", p=128)[:, :, csl],
          note=f"{tag}lk{cc}")
    vld = b["vld"][:, (cc % 2) * 2048:(cc % 2) * 2048 + 2048]
    gld = b["gld"][:, (cc % 2) * 2048:(cc % 2) * 2048 + 2048]
    p.dma("SP", vld, wd["v_dd"][csl, :], note=f"{tag}lv{cc}")
    p.dma("SP", gld, wd["g_dd"][csl, :], note=f"{tag}lg{cc}")
    eb, enb, ebml = b["ebq"], b["enb"], b["ebml"]
    for pc in range(8):
        bs = bfml[:, pc * 128:pc * 128 + 128]
        p.op("ACT", lambda e, bs=bs, pc=pc: e.activation(
            eb[:, pc * 128:pc * 128 + 128], bs, AF.Exp, bias=-LN16),
            reads=[bs, cs["c_fvals"][:]], writes=[eb[:]], note=f"{tag}eb")
        p.op("ACT", lambda e, bs=bs, pc=pc: e.activation(
            enb[:, pc * 128:pc * 128 + 128], bs, AF.Exp, scale=-1.0),
            reads=[bs], writes=[enb[:]], note=f"{tag}en")
        ecol = bfml[:, pc * 128 + 127:pc * 128 + 128]
        t_ = gscr[:, 0:128]
        p.op("DVE", lambda e, bs=bs, ecol=ecol, t_=t_: e.tensor_scalar(
            t_, bs, ecol, None, ALU.subtract),
            reads=[bs, b["bfml"][:]], writes=[t_], note=f"{tag}em1")
        p.op("ACT", lambda e, t_=t_, pc=pc: e.activation(
            ebml[:, pc * 128:pc * 128 + 128], t_, AF.Exp, scale=-1.0),
            reads=[t_], writes=[ebml[:]], note=f"{tag}em2")
    qt, kt, kdf = b["qt"], b["kt"], b["kdf"]
    p.op("DVE", lambda e: e.tensor_mul(qt[:], qld, eb[:]),
         reads=[qld, eb[:]], writes=[qt[:]], note=f"{tag}qt")
    p.op("DVE", lambda e: e.tensor_mul(kt[:], kld, enb[:]),
         reads=[kld, enb[:]], writes=[kt[:]], note=f"{tag}kt")
    p.op("DVE", lambda e: e.tensor_mul(kdf[:], kld, ebml[:]),
         reads=[kld, ebml[:]], writes=[kdf[:]], note=f"{tag}kd")
    for j in range(8):
        p.op("PE", lambda e, j=j: e.transpose(
            pb7[:, j * 128:j * 128 + 128], kdf[:, j * 128:j * 128 + 128],
            cs["c_identb"][:]),
            reads=[kdf[:], cs["c_identb"][:]], writes=[pb7[:]],
            note=f"{tag}kT")
    p.op("DVE", lambda e: e.tensor_copy(b["kdt"][:], pb7[:]),
         reads=[pb7[:]], writes=[b["kdt"][:]], note=f"{tag}kTe")
    ebl = b["sst"][:, 0:8]
    ecols = bfml.rearrange("p (m w) -> p m w", w=128)[:, :, 127]
    p.op("ACT", lambda e: e.activation(ebl, ecols, AF.Exp),
         reads=[b["bfml"][:]], writes=[b["sst"][:, 0:8]], note=f"{tag}ebl")
    ps4, ps5 = b["ps4"], b["ps5"]
    for h in range(4):
        hsl = slice(h * 256, h * 256 + 256)
        vsl = vld[:, h * 512:h * 512 + 512]
        for dc in range(2):
            lh = kt[:, h * 256 + dc * 128:h * 256 + dc * 128 + 128]
            rh = qt[:, h * 256 + dc * 128:h * 256 + dc * 128 + 128]
            p.op("PE", lambda e, lh=lh, rh=rh, dc=dc: e.matmul(
                ps4[:, 0:128], lh, rh, start=(dc == 0), stop=(dc == 1)),
                reads=[lh, rh], writes=[ps4[:]], note=f"{tag}A{h}")
        amt = b["amt"]
        p.op("DVE", lambda e: e.tensor_tensor(amt[:], ps4[:, 0:128],
                                              cs["c_uppib"][:], ALU.mult),
             reads=[ps4[:], cs["c_uppib"][:]], writes=[amt[:]],
             note=f"{tag}Am{h}")
        for dc in range(2):
            lh = qt[:, h * 256 + dc * 128:h * 256 + dc * 128 + 128]
            Ssl = b["Sst"][:, (h * 2 + dc) * 512:(h * 2 + dc) * 512 + 512]
            p.op("PE", lambda e, lh=lh, Ssl=Ssl, dc=dc: e.matmul(
                ps5[:], lh, Ssl, start=(dc == 0), stop=False),
                reads=[lh, Ssl], writes=[ps5[:]], note=f"{tag}oi{h}")
        p.op("PE", lambda e, vsl=vsl: e.matmul(ps5[:], amt[:], vsl,
                                               start=False, stop=True),
             reads=[amt[:], vld], writes=[ps5[:]], note=f"{tag}oa{h}")
        for dc in range(2):
            lh = b["kdt"][:, h * 256 + dc * 128:h * 256 + dc * 128 + 128]
            p.op("PE", lambda e, lh=lh, vsl=vsl, dc=dc: e.matmul(
                b[f"ps{dc}"][:], lh, vsl, start=True, stop=True),
                reads=[lh, vld], writes=[b[f"ps{dc}"][:]], note=f"{tag}su{h}")
        for dc in range(2):
            Ssl = b["Sst"][:, (h * 2 + dc) * 512:(h * 2 + dc) * 512 + 512]
            ebcol = b["sst"][:, h * 2 + dc:h * 2 + dc + 1]
            p.op("DVE", lambda e, Ssl=Ssl, ebcol=ebcol, dc=dc:
                 e.scalar_tensor_tensor(Ssl, Ssl, ebcol, b[f"ps{dc}"][:],
                                        ALU.mult, ALU.add),
                 reads=[Ssl, b["sst"][:, 0:8], b[f"ps{dc}"][:]], writes=[Ssl],
                 note=f"{tag}sd{h}")
        sq = gscr[:, 0:512]
        p.op("ACT", lambda e, sq=sq: e.activation(sq, ps5[:], AF.Square),
             reads=[ps5[:]], writes=[sq], note=f"{tag}r1{h}")
        ms = b["sst"][:, 8:9]
        p.op("DVE", lambda e, sq=sq, ms=ms: e.reduce_sum(
            ms.unsqueeze(2), sq.unsqueeze(1), AX.X),
            reads=[sq], writes=[b["sst"][:, 8:9]], note=f"{tag}r2{h}")
        p.op("DVE", lambda e, ms=ms: e.tensor_scalar(
            ms, ms, 1.0 / 512.0, 1e-5, ALU.mult, ALU.add),
            reads=[ms], writes=[ms], note=f"{tag}r3{h}")
        p.op("ACT", lambda e, ms=ms: e.activation(ms, ms, AF.Ln),
             reads=[ms], writes=[ms], note=f"{tag}r4{h}")
        p.op("ACT", lambda e, ms=ms: e.activation(ms, ms, AF.Exp, scale=-0.5),
             reads=[ms], writes=[ms], note=f"{tag}r5{h}")
        on = gscr[:, 512:1024]
        p.op("DVE", lambda e, on=on, ms=ms: e.scalar_tensor_tensor(
            on, ps5[:], ms, b["rwb"][:], ALU.mult, ALU.mult),
            reads=[ps5[:], ms, b["rwb"][:]], writes=[on], note=f"{tag}r6{h}")
        dst = b["ogt"][:, h * 512:h * 512 + 512]
        gsl = gld[:, h * 512:h * 512 + 512]
        p.op("DVE", lambda e, dst=dst, on=on, gsl=gsl: e.tensor_tensor(
            dst, on, gsl, ALU.mult),
            reads=[on, gld], writes=[dst], note=f"{tag}r7{h}")
    for g8 in range(2):
        for j in range(8):
            kk = g8 * 8 + j
            src = b["ogt"][:, kk * 128:kk * 128 + 128]
            p.op("PE", lambda e, src=src, j=j: e.transpose(
                pb7[:, j * 128:j * 128 + 128], src, cs["c_identb"][:]),
                reads=[src, cs["c_identb"][:]], writes=[pb7[:]],
                note=f"{tag}oT{cc}")
        dst2 = b["ogT2"][:, :].rearrange("p (k c w) -> p k c w", c=2, w=128)[
            :, g8 * 8:g8 * 8 + 8, cc % 2, :]
        p.op("DVE", lambda e, dst2=dst2: e.tensor_copy(
            dst2, pb7[:, 0:1024].rearrange("p (j w) -> p j w", w=128)),
            reads=[pb7[:]], writes=[b["ogT2"][:]], note=f"{tag}oTe{cc}")


def emit_l1a_out2(p, cs, b, wd, ccg, tag):
    xres = b["xres"][:]
    pss = [b["ps0"], b["ps1"], b["ps2"], b["ps3"]]
    wo3 = wd["wout_bf_d"].rearrange("(k p) j -> p k j", p=128)
    s0 = ccg * 256
    for mg in range(2):
        for k in range(16):
            wsl = b["wos"][:, (k % 2) * 512:(k % 2) * 512 + 512]
            p.dma("POOL", wsl, wo3[:, k, mg * 512:mg * 512 + 512],
                  note=f"{tag}wol{ccg}")
            rhs = b["ogT2"][:, k * 256:k * 256 + 256]
            for m in range(4):
                lh = wsl[:, m * 128:m * 128 + 128]
                p.op("PE", lambda e, lh=lh, rhs=rhs, m=m, k=k: e.matmul(
                    pss[m][:, 0:256], lh, rhs, start=(k == 0), stop=(k == 15)),
                    reads=[wsl, rhs], writes=[pss[m][:]], note=f"{tag}wo{ccg}")
        for m in range(4):
            mm = mg * 4 + m
            xsl = xi_ap(xres, mm, s0, 256)
            p.op("DVE", lambda e, xsl=xsl, m=m: e.tensor_add(
                w32(xsl), xsl, pss[m][:, 0:256]),
                reads=[xsl, pss[m][:]], writes=[xsl], note=f"{tag}wr{ccg}")


def emit_wout_precast(p, b, w_d, w_bf_d, tag):
    stg = b["wos"]
    for i in range(16):
        sl = stg[:, (i % 2) * 1024:(i % 2) * 1024 + 1024]
        p.dma("POOL", sl, w_d[i * 128:i * 128 + 128, :], note=f"{tag}c{i}")
        p.dma("SP", w_bf_d[i * 128:i * 128 + 128, :], sl, note=f"{tag}s{i}")


def alloc_l1a_bufs(b):
    b["bfml"] = b["wim"]
    b["qld"] = b["dbt"]
    b["kld"] = b["zz"][:, 0:2048]
    b["vld"] = b["ygT"][:, 0:4096]
    b["gld"] = b["ygT"][:, 4096:8192]
    xb = b["xbc"]
    b["ebq"] = xb[:, 0:1024]
    b["enb"] = xb[:, 1024:2048]
    b["ebml"] = xb[:, 2048:3072]
    b["qt"] = xb[:, 3072:4096]
    b["kt"] = xb[:, 4096:5120]
    b["kdf"] = xb[:, 5120:6144]
    b["kdt"] = xb[:, 6144:7168]
    b["amt"] = xb[:, 7168:7296]
    b["Sst"] = b["cscr"][:, 0:4096]
    b["ogt"] = b["hst"]
    b["ogT2"] = b["u8k"][:].bitcast(BF16)
    b["rwb"] = b["xml"][:].bitcast(F32)


def build_l1a_only():
    nc = bass.Bass("TRN2", target_bir_lowering=False, debug=False)
    x_d = nc.dram_tensor("x", [2048, 1024], F32, kind="ExternalInput").ap()
    out_d = nc.dram_tensor("out", [2048, 1024], F32, kind="ExternalOutput").ap()
    wd = {}
    for nm, shape in L1A_WSPECS:
        wd[nm] = nc.dram_tensor(nm, list(shape), F32, kind="ExternalInput").ap()
    wd["q_dd"] = nc.dram_tensor("q_dd", [1024, 2048], BF16).ap()
    wd["k_dd"] = nc.dram_tensor("k_dd", [1024, 2048], BF16).ap()
    wd["t1_dd"] = nc.dram_tensor("t1_dd", [128, 2048], BF16).ap()
    wd["v_dd"] = nc.dram_tensor("v_dd", [2048, 2048], BF16).ap()
    wd["g_dd"] = nc.dram_tensor("g_dd", [2048, 2048], BF16).ap()
    wd["bfm_d"] = nc.dram_tensor("bfm_d", [1024, 2048], F32).ap()
    wd["wout_bf_d"] = nc.dram_tensor("wout_bf_d", [2048, 1024], BF16).ap()
    al = Alloc(nc)
    p = Prog(nc)
    cs = declare_consts(nc, al, p)
    b = alloc_main_bufs(al)
    alloc_l0f_bufs(al, b)
    alloc_l1a_bufs(b)
    emit_load_x(p, cs, b, x_d, b["xres"][:])
    emit_wout_precast(p, b, wd["wout_t"], wd["wout_bf_d"], "wpc")
    emit_l1_attn(p, al, cs, b, wd)
    emit_store_x(p, cs, b, b["xres"][:], out_d)
    p.finalize()
    al.close()
    return nc


def kernel_l1a(core_inputs):
    nc = build_l1a_only()
    consts = host_consts()
    cmaps = []
    for ci in core_inputs:
        w = host_l1a_weights(ci)
        m = {"x": np.ascontiguousarray(np.asarray(ci["x"], np.float32))}
        for nm, _ in L1A_WSPECS:
            m[nm] = w[nm]
        m.update(consts)
        cmaps.append(m)
    res = run_bass_kernel_spmd(nc, cmaps, core_ids=list(range(len(cmaps))))
    return [np.asarray(r["out"]) for r in res.results]



MOE_C = 5120


def host_l1f_weights(inp):
    w = {}
    w["wg8"] = np.ascontiguousarray(np.asarray(inp["L1_ffn_wg"], np.float32))
    fc1 = np.asarray(inp["L1_ffn_fc1_w"], np.float32)
    w["fc1t"] = np.ascontiguousarray(fc1.transpose(0, 2, 1))
    w["fc2w"] = np.ascontiguousarray(np.asarray(inp["L1_ffn_fc2_w"], np.float32))
    w["b1c"] = np.ascontiguousarray(
        np.asarray(inp["L1_ffn_fc1_b"], np.float32).reshape(8, 32, 128)
        .transpose(0, 2, 1).copy())
    w["b2m"] = np.ascontiguousarray(np.asarray(inp["L1_ffn_fc2_b"], np.float32))
    w["lngam"] = np.asarray(inp["L1_ffn_ln_gamma"], np.float32).reshape(1, 1024).copy()
    w["lnbet"] = np.asarray(inp["L1_ffn_ln_beta"], np.float32).reshape(1, 1024).copy()
    return w


L1F_WSPECS = [("wg8", (1024, 8)), ("fc1t", (8, 1024, 4096)),
              ("fc2w", (8, 4096, 1024)), ("b1c", (8, 128, 32)),
              ("b2m", (8, 1024)), ("lngam", (1, 1024)), ("lnbet", (1, 1024))]


def emit_l1f_routing(p, al, cs, b, wd, tag="l1r"):
    xres = b["xres"][:]
    gscr = b["gscr"]
    emit_ln_stats(p, cs, b, xres, b["meanb"][:], b["rstdb"][:],
                  b["rstd_tm"][:], tag + "ln")
    for tc in range(4):
        for c4 in range(4):
            cc = tc * 4 + c4
            sl = b["meanb"][:, cc * 128:cc * 128 + 128]
            p.op("PE", lambda e, sl=sl, c4=c4: e.transpose(
                b["ps6"][:, c4 * 128:c4 * 128 + 128], sl, cs["c_ident"][:]),
                reads=[sl, cs["c_ident"][:]], writes=[b["ps6"][:]],
                note=f"{tag}mrT")
        for c4 in range(4):
            cc = tc * 4 + c4
            p.op("DVE", lambda e, cc=cc, c4=c4: e.tensor_copy(
                b["mrtm"][:, cc:cc + 1], b["ps6"][:, c4 * 128:c4 * 128 + 1]),
                reads=[b["ps6"][:]], writes=[b["mrtm"][:, cc:cc + 1]],
                note=f"{tag}mrC")
    p.dma("SP", wd["rstdtm_d"], b["rstd_tm"][:], note=f"{tag}rts")
    p.dma("SP", wd["mrtm_d"], b["mrtm"][:], note=f"{tag}mrs")
    for pc in range(8):
        xs = xi_ap(xres, pc, 0, 2048)
        p.dma("SP", wd["xres_d"][pc * 128:pc * 128 + 128, :], xs,
              note=f"{tag}xs{pc}")
        tv = gscr[:, 0:2048]
        hb = b["zz"][:, 0:2048]
        p.op("DVE", lambda e, xs=xs, tv=tv: e.tensor_sub(
            tv, xs, b["meanb"][:]),
            reads=[xs, b["meanb"][:]], writes=[tv], note=f"{tag}hs{pc}")
        p.op("DVE", lambda e, tv=tv, hb=hb: e.tensor_mul(
            hb, tv, b["rstdb"][:]),
            reads=[tv, b["rstdb"][:]], writes=[hb], note=f"{tag}hm{pc}")
        p.dma("SP", wd["hid_d"][pc * 128:pc * 128 + 128, :], hb,
              note=f"{tag}hw{pc}")
    wgs = gscr[:, 1984:2048]
    p.dma("SP", wgs.rearrange("p (k j) -> p k j", j=8),
          wd["wg8"].rearrange("(k p) j -> p k j", p=128), note=f"{tag}wgl")
    for k in range(8):
        sl = wgs[:, k * 8:k * 8 + 8]
        p.op("PE", lambda e, sl=sl, k=k: e.matmul(
            b["ps4"][0:1, 0:8], cs["c_ones"][:, 0:1], sl,
            start=(k == 0), stop=(k == 7)),
            reads=[cs["c_ones"][:], sl], writes=[b["ps4"][:]],
            note=f"{tag}ws")
    p.op("DVE", lambda e: e.tensor_scalar_mul(b["nwsr"],
                                              b["ps4"][0:1, 0:8], -1.0),
         reads=[b["ps4"][:]], writes=[b["nwsr"]], note=f"{tag}wc")
    off = b["moff"][0:1, 0:16]
    p.op("DVE", lambda e: e.memset(off, 0.0), reads=[], writes=[off],
         note=f"{tag}o0")
    for ct in range(16):
        t0 = ct * 128
        ps = b["ps5"]
        for k in range(8):
            lh = xi_ap(xres, k, t0, 128)
            sl = wgs[:, k * 8:k * 8 + 8]
            p.op("PE", lambda e, lh=lh, sl=sl, k=k: e.matmul(
                ps[:, 0:8], lh, sl, start=(k == 0), stop=False),
                reads=[lh, sl], writes=[ps[:]], note=f"{tag}sm")
        mrr = b["meanb"][0:1, t0:t0 + 128]
        p.op("PE", lambda e, mrr=mrr: e.matmul(
            ps[:, 0:8], mrr, b["nwsr"], start=False, stop=True),
            reads=[mrr, b["nwsr"]], writes=[ps[:]], note=f"{tag}r1")
        sc = gscr[:, 0:8]
        rcol = b["rstd_tm"][:, ct:ct + 1]
        p.op("DVE", lambda e, sc=sc, rcol=rcol: e.tensor_scalar(
            sc, b["ps5"][:, 0:8], rcol, None, ALU.mult),
            reads=[b["ps5"][:], b["rstd_tm"][:]], writes=[sc], note=f"{tag}sc")
        mx = gscr[:, 8:9]
        p.op("DVE", lambda e, sc=sc, mx=mx: e.reduce_max(
            mx.unsqueeze(2), sc.unsqueeze(1), AX.X, negate=True),
            reads=[sc], writes=[mx], note=f"{tag}mx")
        lsum = gscr[:, 9:10]
        p.op("ACT", lambda e, sc=sc, mx=mx, lsum=lsum: e.activation(
            sc, sc, AF.Exp, bias=mx, accum_out=lsum),
            reads=[sc, mx], writes=[sc, lsum], note=f"{tag}ex")
        rl = gscr[:, 10:11]
        p.op("DVE", lambda e, lsum=lsum, rl=rl: e.reciprocal(rl, lsum),
             reads=[lsum], writes=[rl], note=f"{tag}rc")
        p.op("DVE", lambda e, sc=sc, rl=rl: e.tensor_scalar(
            sc, sc, rl, None, ALU.mult),
            reads=[sc, rl], writes=[sc], note=f"{tag}nm")
        srt = gscr[:, 16:24]
        p.op("DVE", lambda e, sc=sc, srt=srt: e.max(srt, sc),
             reads=[sc], writes=[srt], note=f"{tag}srt")
        gs = gscr[:, 32:33]
        p.op("DVE", lambda e, srt=srt, gs=gs: e.tensor_tensor(
            gs, srt[:, 0:1], srt[:, 1:2], ALU.add),
            reads=[srt], writes=[gs], note=f"{tag}gs")
        p.op("DVE", lambda e, gs=gs: e.tensor_scalar_max(gs, gs,
                                                         1.1920929e-07),
             reads=[gs], writes=[gs], note=f"{tag}gc")
        p.op("DVE", lambda e, gs=gs: e.reciprocal(gs, gs),
             reads=[gs], writes=[gs], note=f"{tag}gr")
        rt = b["route"][:, ct * 24:ct * 24 + 24]
        p.op("DVE", lambda e, srt=srt, gs=gs, rt=rt: e.tensor_scalar(
            rt[:, 0:2], srt[:, 0:2], gs, None, ALU.mult),
            reads=[srt, gs], writes=[b["route"][:]], note=f"{tag}gn")
        for j in range(2):
            oh = rt[:, 4 + j * 8:12 + j * 8]
            p.op("DVE", lambda e, oh=oh, sc=sc, srt=srt, j=j: e.tensor_tensor(
                oh, sc, srt[:, j:j + 1].broadcast_to((128, 8)), ALU.is_equal),
                reads=[sc, srt], writes=[b["route"][:]], note=f"{tag}oh{j}")
            ps2_ = b["ps6"]
            p.op("PE", lambda e, oh=oh, j=j: e.matmul(
                ps2_[:, 0:8], cs["c_upp01"][:], oh, start=True, stop=False),
                reads=[cs["c_upp01"][:], oh], writes=[ps2_[:]],
                note=f"{tag}rk{j}")
            p.op("PE", lambda e, j=j: e.matmul(
                ps2_[:, 0:8], cs["c_onesrow"][:], off[0:1, j * 8:j * 8 + 8],
                start=False, stop=True),
                reads=[cs["c_onesrow"][:], off], writes=[ps2_[:]],
                note=f"{tag}ro{j}")
            tmp8 = gscr[:, 64 + j * 8:72 + j * 8]
            p.op("DVE", lambda e, oh=oh, tmp8=tmp8: e.tensor_tensor(
                tmp8, ps2_[:, 0:8], oh, ALU.mult),
                reads=[ps2_[:], oh], writes=[tmp8], note=f"{tag}rm{j}")
            p.op("DVE", lambda e, tmp8=tmp8, rt=rt, j=j: e.reduce_sum(
                rt[:, 2 + j:3 + j].unsqueeze(2), tmp8.unsqueeze(1), AX.X),
                reads=[tmp8], writes=[b["route"][:]], note=f"{tag}rr{j}")
            p.op("PE", lambda e, oh=oh, j=j: e.matmul(
                b["ps4"][0:1, 0:8], cs["c_ones"][:, 0:1], oh, start=True,
                stop=True),
                reads=[cs["c_ones"][:], oh], writes=[b["ps4"][:]],
                note=f"{tag}cs{j}")
            p.op("DVE", lambda e, j=j: e.tensor_add(
                off[0:1, j * 8:j * 8 + 8], off[0:1, j * 8:j * 8 + 8],
                b["ps4"][0:1, 0:8]),
                reads=[off, b["ps4"][:]], writes=[off], note=f"{tag}oa{j}")
    for ct in range(16):
        p.dma("SP", wd["route_d"][ct * 128:ct * 128 + 128, :],
              b["route"][:, ct * 24:ct * 24 + 24], note=f"{tag}rs{ct}")
    p.dma("SP", wd["cnt_d"], off, note=f"{tag}cn")


def alloc_l1f_bufs(al, b):
    xbf = b["xbc"][:].bitcast(F32)
    b["route"] = xbf[:, 0:384]
    b["mrtm"] = xbf[:, 384:400]
    b["moff"] = xbf[0:1, 400:416]
    b["nwsr"] = xbf[0:1, 416:424]


def host_l1f_split(inp):
    w = {}
    fc1 = np.asarray(inp["L1_ffn_fc1_w"], np.float32)
    fc2 = np.asarray(inp["L1_ffn_fc2_w"], np.float32)
    b1 = np.asarray(inp["L1_ffn_fc1_b"], np.float32)
    for e in range(8):
        w[f"fc1t{e}"] = np.ascontiguousarray(fc1[e].T)
        w[f"fc2{e}"] = np.ascontiguousarray(fc2[e])
        w[f"b1{e}"] = np.ascontiguousarray(
            b1[e].reshape(32, 128).T.copy())
    w["b2m"] = np.ascontiguousarray(np.asarray(inp["L1_ffn_fc2_b"],
                                               np.float32))
    w["lngam"] = np.asarray(inp["L1_ffn_ln_gamma"],
                            np.float32).reshape(1, 1024).copy()
    w["lnbet"] = np.asarray(inp["L1_ffn_ln_beta"],
                            np.float32).reshape(1, 1024).copy()
    return w


L1F_B_WSPECS = ([(f"fc1t{e}", (1024, 4096)) for e in range(8)]
                + [(f"fc2{e}", (4096, 1024)) for e in range(8)]
                + [(f"b1{e}", (128, 32)) for e in range(8)]
                + [("b2m", (8, 1024)), ("lngam", (1, 1024)),
                   ("lnbet", (1, 1024))])


def emit_l1f_geff(p, cs, b, wd, C, tag="gf"):
    allc, cmk = b["allc"], b["cmk"]
    p.dma("SP", allc[:], wd["allcnt"], note=f"{tag}ac")
    p.dma("SP", cmk[:], wd["cmask"], note=f"{tag}cm")
    p.dma("SP", b["rstm"][:], wd["rstdtm_d"], note=f"{tag}rt")
    p.dma("SP", b["mrtm"][:], wd["mrtm_d"], note=f"{tag}mt")
    p.dma("SP", b["routeb"][:].rearrange("p (c j) -> p c j", j=24),
          wd["route_d"].rearrange("(c p) j -> p c j", p=128), note=f"{tag}rl")
    mall = b["prows"][0:8, 16:32]
    p.op("DVE", lambda e: e.tensor_scalar(mall, allc[:], cmk[:, 0:1], None,
                                          ALU.mult),
         reads=[allc[:], cmk[:]], writes=[b["prows"][:]], note=f"{tag}msk")
    ps0, ps1 = b["psB"][0], b["psB"][1]
    p.op("PE", lambda e: e.matmul(ps0[0:1, 0:16], cs["c_ones"][0:8, 0:1],
                                  mall, start=True, stop=True),
         reads=[cs["c_ones"][:], mall], writes=[ps0[:]], note=f"{tag}pf")
    p.op("PE", lambda e: e.matmul(ps1[0:1, 0:16], cs["c_ones"][0:8, 0:1],
                                  allc[:], start=True, stop=True),
         reads=[cs["c_ones"][:], allc[:]], writes=[ps1[:]], note=f"{tag}tt")
    prow = b["prows"][0:1, 0:16]
    p.op("DVE", lambda e: e.tensor_copy(prow, ps0[0:1, 0:16]),
         reads=[ps0[:]], writes=[b["prows"][:]], note=f"{tag}pc")
    p1e = b["prows"][0:1, 32:40]
    p.op("DVE", lambda e: e.tensor_add(p1e, prow[0:1, 8:16], ps1[0:1, 0:8]),
         reads=[b["prows"][:], ps1[:]], writes=[b["prows"][:]],
         note=f"{tag}p1")
    for i, row in enumerate([prow[0:1, 0:8], p1e]):
        ps = b["psB"][i]
        p.op("PE", lambda e, row=row, ps=ps: e.matmul(
            ps[:, 0:8], cs["c_onesrow"][:], row, start=True, stop=True),
            reads=[cs["c_onesrow"][:], b["prows"][:]], writes=[ps[:]],
            note=f"{tag}bc{i}")
    p0t, p1t = b["p0t"], b["p1t"]
    p.op("DVE", lambda e: e.tensor_copy(p0t[:], b["psB"][0][:, 0:8]),
         reads=[b["psB"][0][:]], writes=[p0t[:]], note=f"{tag}b0")
    p.op("DVE", lambda e: e.tensor_copy(p1t[:], b["psB"][1][:, 0:8]),
         reads=[b["psB"][1][:]], writes=[p1t[:]], note=f"{tag}b1")
    scr = b["scr32"]
    for ct in range(16):
        rsl = b["routeb"][:, ct * 24:ct * 24 + 24]
        gsl = b["geffb"][:, ct * 8:ct * 8 + 8]
        for j, pt in ((0, p0t), (1, p1t)):
            oh = rsl[:, 4 + j * 8:12 + j * 8]
            tmp = scr[:, j * 8:j * 8 + 8]
            p.op("DVE", lambda e, oh=oh, pt=pt, tmp=tmp: e.tensor_tensor(
                tmp, oh, pt[:], ALU.mult),
                reads=[b["routeb"][:], pt[:]], writes=[b["scr32"][:]],
                note=f"{tag}pm{j}")
            loc = scr[:, 16 + j:17 + j]
            p.op("DVE", lambda e, tmp=tmp, loc=loc: e.reduce_sum(
                loc.unsqueeze(2), tmp.unsqueeze(1), AX.X),
                reads=[b["scr32"][:]], writes=[b["scr32"][:]],
                note=f"{tag}ps{j}")
            p.op("DVE", lambda e, loc=loc, rsl=rsl, j=j: e.tensor_add(
                loc, loc, rsl[:, 2 + j:3 + j]),
                reads=[b["scr32"][:], b["routeb"][:]], writes=[b["scr32"][:]],
                note=f"{tag}pl{j}")
            kp = scr[:, 18 + j:19 + j]
            p.op("DVE", lambda e, loc=loc, kp=kp: e.tensor_scalar(
                kp, loc, float(C), None, ALU.is_lt),
                reads=[b["scr32"][:]], writes=[b["scr32"][:]],
                note=f"{tag}kp{j}")
            cj = scr[:, 20 + j * 8:28 + j * 8]
            p.op("DVE", lambda e, oh=oh, rsl=rsl, kp=kp, cj=cj, j=j:
                 e.tensor_scalar(cj, oh, rsl[:, j:j + 1], kp, ALU.mult,
                                 ALU.mult),
                 reads=[b["routeb"][:], b["scr32"][:]],
                 writes=[b["scr32"][:]], note=f"{tag}cj{j}")
        p.op("DVE", lambda e, gsl=gsl: e.tensor_add(
            gsl, scr[:, 20:28], scr[:, 28:36]),
            reads=[b["scr32"][:]], writes=[b["geffb"][:]], note=f"{tag}ge")


def emit_l1f_experts(p, cs, b, wd, tag="ex"):
    hid3 = b["hid3"][:].rearrange("p (k t) -> p k t", t=2048)
    p.dma("SP", hid3, wd["hid_d"].rearrange("(k p) t -> p k t", p=128),
          note=f"{tag}hl")
    for nm, row in (("gamt", wd["lngam"]), ("bett", wd["lnbet"])):
        rsb = b["prows"][0:1, 48 + (0 if nm == "gamt" else 1) * 1024:
                         48 + (1 if nm == "gamt" else 2) * 1024]
        p.dma("SP", rsb, row, note=f"{tag}{nm}l")
        for ng in range(2):
            ps = b["psB"][2 + ng]
            p.op("PE", lambda e, rsb=rsb, ng=ng, ps=ps: e.matmul(
                ps[:, 0:512], cs["c_onesrow"][:],
                rsb[0:1, ng * 512:ng * 512 + 512], start=True, stop=True),
                reads=[cs["c_onesrow"][:], b["prows"][:]], writes=[ps[:]],
                note=f"{tag}{nm}b")
        for ng in range(2):
            sl = b[nm][:, ng * 512:ng * 512 + 512]
            p.op("DVE", lambda e, sl=sl, ng=ng: e.tensor_copy(
                sl, b["psB"][2 + ng][:, 0:512]),
                reads=[b["psB"][2 + ng][:]], writes=[b[nm][:]],
                note=f"{tag}{nm}c")
    p.dma("SP", b["b2s"][:], wd["b2m"], note=f"{tag}b2l")
    for tg in range(2):
        tgo = tg * 1024
        p.op("DVE", lambda e: e.memset(b["acc"][:], 0.0),
             reads=[], writes=[b["acc"][:]], note=f"{tag}a0{tg}")
        for e_ in range(8):
            p.dma("SP", b["b1sb"][:], wd[f"b1{e_}"], note=f"{tag}b1{tg}{e_}")
            for m in range(32):
                wsl = b["w1s"][:, (m % 2) * 1024:(m % 2) * 1024 + 1024]
                wsl3 = wsl.rearrange("p (k j) -> p k j", j=128)
                p.dma("POOL", wsl3,
                      wd[f"fc1t{e_}"][:, m * 128:m * 128 + 128].rearrange(
                          "(k p) j -> p k j", p=128), note=f"{tag}w1{m}")
                pa = [b["psB"][2 * (m % 2)], b["psB"][2 * (m % 2) + 1]]
                for k in range(8):
                    lh = wsl3[:, k, :]
                    for tb in range(2):
                        rh = hid3[:, k, tgo + tb * 512:tgo + tb * 512 + 512]
                        ps = pa[tb]
                        p.op("PE", lambda e, lh=lh, rh=rh, ps=ps, k=k:
                             e.matmul(ps[:, 0:512], lh, rh, start=(k == 0),
                                      stop=(k == 7)),
                             reads=[wsl, b["hid3"][:]],
                             writes=[ps[:]], note=f"{tag}f1")
                for tb in range(2):
                    hsl = b["htg"][:, m * 1024 + tb * 512:
                                   m * 1024 + tb * 512 + 512]
                    ps = pa[tb]
                    p.op("ACT", lambda e, hsl=hsl, ps=ps, m=m: e.activation(
                        hsl, ps[:, 0:512], AF.Silu,
                        bias=b["b1sb"][:, m:m + 1]),
                        reads=[ps[:], b["b1sb"][:]], writes=[hsl],
                        note=f"{tag}si")
            for ng in range(2):
                for kk in range(32):
                    w2 = b["w2s"][:, (kk % 2) * 512:(kk % 2) * 512 + 512]
                    p.dma("POOL", w2,
                          wd[f"fc2{e_}"][kk * 128:kk * 128 + 128,
                                         ng * 512:ng * 512 + 512],
                          note=f"{tag}w2{ng}{kk}")
                    for tq in range(8):
                        lh = b["htg"][:, kk * 1024 + tq * 128:
                                      kk * 1024 + tq * 128 + 128]
                        ps = b["psB"][tq]
                        p.op("PE", lambda e, lh=lh, w2=w2, ps=ps, kk=kk:
                             e.matmul(ps[:, 0:512], lh, w2, start=(kk == 0),
                                      stop=(kk == 31)),
                             reads=[lh, w2], writes=[ps[:]],
                             note=f"{tag}f2")
                for tq in range(8):
                    asl = b["acc"][:, tq * 1024 + ng * 512:
                                   tq * 1024 + ng * 512 + 512]
                    gcol = b["geffb"][:, (tg * 8 + tq) * 8 + e_:
                                      (tg * 8 + tq) * 8 + e_ + 1]
                    ps = b["psB"][tq]
                    p.op("DVE", lambda e, asl=asl, gcol=gcol, ps=ps:
                         e.scalar_tensor_tensor(asl, ps[:, 0:512], gcol, asl,
                                                ALU.mult, ALU.add),
                         reads=[ps[:], b["geffb"][:], asl],
                         writes=[b["acc"][:]], note=f"{tag}cb")
        for tq in range(8):
            ct = tg * 8 + tq
            gsl = b["geffb"][:, ct * 8:ct * 8 + 8]
            ps0 = b["psB"][0]
            p.op("PE", lambda e, gsl=gsl: e.transpose(
                ps0[0:8, 0:128], gsl, cs["c_ident"][:]),
                reads=[b["geffb"][:], cs["c_ident"][:]], writes=[ps0[:]],
                note=f"{tag}gT{tq}")
            p.op("DVE", lambda e: e.tensor_copy(b["gtr"][:], ps0[0:8, 0:128]),
                 reads=[ps0[:]], writes=[b["gtr"][:]], note=f"{tag}gc{tq}")
            for ng in range(2):
                ps1 = b["psB"][1]
                p.op("PE", lambda e, ng=ng: e.matmul(
                    ps1[:, 0:512], b["gtr"][:],
                    b["b2s"][:, ng * 512:ng * 512 + 512], start=True,
                    stop=True),
                    reads=[b["gtr"][:], b["b2s"][:]], writes=[ps1[:]],
                    note=f"{tag}b2m{tq}")
                asl = b["acc"][:, tq * 1024 + ng * 512:
                               tq * 1024 + ng * 512 + 512]
                p.op("DVE", lambda e, asl=asl: e.tensor_add(
                    asl, asl, ps1[:, 0:512]),
                    reads=[asl, ps1[:]], writes=[b["acc"][:]],
                    note=f"{tag}b2a{tq}")
        for tq in range(8):
            ct = tg * 8 + tq
            t0 = ct * 128
            xld3 = b["xld"][:].rearrange("p (k t) -> p k t", t=128)
            p.dma("SP", xld3,
                  wd["xres_d"][:, t0:t0 + 128].rearrange("(k p) t -> p k t",
                                                         p=128),
                  note=f"{tag}xl{ct}")
            for k in range(8):
                ps = b["psB"][2 + k // 4]
                p.op("PE", lambda e, k=k, ps=ps: e.transpose(
                    ps[:, (k % 4) * 128:(k % 4) * 128 + 128], xld3[:, k, :],
                    cs["c_ident"][:]),
                    reads=[b["xld"][:], cs["c_ident"][:]], writes=[ps[:]],
                    note=f"{tag}xT{ct}")
            for hk in range(2):
                p.op("DVE", lambda e, hk=hk: e.tensor_copy(
                    b["xrt"][:, hk * 512:hk * 512 + 512],
                    b["psB"][2 + hk][:, 0:512]),
                    reads=[b["psB"][2 + hk][:]], writes=[b["xrt"][:]],
                    note=f"{tag}xc{ct}")
            rcol = b["rstm"][:, ct:ct + 1]
            mcol = b["mrtm"][:, ct:ct + 1]
            p.op("DVE", lambda e, rcol=rcol, mcol=mcol: e.tensor_scalar(
                b["ht"][:], b["xrt"][:], rcol, mcol, ALU.mult, ALU.subtract),
                reads=[b["xrt"][:], b["rstm"][:], b["mrtm"][:]],
                writes=[b["ht"][:]], note=f"{tag}hid{ct}")
            asl = b["acc"][:, tq * 1024:tq * 1024 + 1024]
            p.op("DVE", lambda e, asl=asl: e.tensor_add(b["ht"][:], b["ht"][:],
                                                        asl),
                 reads=[b["ht"][:], asl], writes=[b["ht"][:]],
                 note=f"{tag}res{ct}")
            sm = b["scr32"][:, 0:1]
            sq = b["scr32"][:, 1:2]
            p.op("DVE", lambda e, sm=sm: e.reduce_sum(
                sm.unsqueeze(2), b["ht"][:].unsqueeze(1), AX.X),
                reads=[b["ht"][:]], writes=[b["scr32"][:]],
                note=f"{tag}sm{ct}")
            p.op("ACT", lambda e, sq=sq: e.activation(
                b["sqb"][:], b["ht"][:], AF.Square, accum_out=sq),
                reads=[b["ht"][:]], writes=[b["sqb"][:], b["scr32"][:]],
                note=f"{tag}sq{ct}")
            mu = b["scr32"][:, 2:3]
            p.op("DVE", lambda e, sm=sm, mu=mu: e.tensor_scalar_mul(
                mu, sm, 1.0 / 1024.0),
                reads=[b["scr32"][:]], writes=[b["scr32"][:]],
                note=f"{tag}mu{ct}")
            msq = b["scr32"][:, 3:4]
            p.op("ACT", lambda e, mu=mu, msq=msq: e.activation(
                msq, mu, AF.Square),
                reads=[b["scr32"][:]], writes=[b["scr32"][:]],
                note=f"{tag}ms{ct}")
            var = b["scr32"][:, 4:5]
            p.op("DVE", lambda e, sq=sq, msq=msq, var=var:
                 e.scalar_tensor_tensor(var, sq, 1.0 / 1024.0, msq, ALU.mult,
                                        ALU.subtract),
                 reads=[b["scr32"][:]], writes=[b["scr32"][:]],
                 note=f"{tag}va{ct}")
            p.op("ACT", lambda e, var=var: e.activation(var, var, AF.Ln,
                                                        bias=1e-5),
                 reads=[b["scr32"][:], cs["c_fvals"][:]],
                 writes=[b["scr32"][:]], note=f"{tag}lv{ct}")
            r2 = b["scr32"][:, 5:6]
            p.op("ACT", lambda e, var=var, r2=r2: e.activation(
                r2, var, AF.Exp, scale=-0.5),
                reads=[b["scr32"][:]], writes=[b["scr32"][:]],
                note=f"{tag}r2{ct}")
            mur = b["scr32"][:, 6:7]
            p.op("DVE", lambda e, mu=mu, r2=r2, mur=mur: e.tensor_mul(
                mur, mu, r2),
                reads=[b["scr32"][:]], writes=[b["scr32"][:]],
                note=f"{tag}mr{ct}")
            p.op("DVE", lambda e, r2=r2, mur=mur: e.tensor_scalar(
                b["sqb"][:], b["ht"][:], r2, mur, ALU.mult, ALU.subtract),
                reads=[b["ht"][:], b["scr32"][:]], writes=[b["sqb"][:]],
                note=f"{tag}nm{ct}")
            p.op("DVE", lambda e: e.tensor_mul(b["sqb"][:], b["sqb"][:],
                                               b["gamt"][:]),
                 reads=[b["sqb"][:], b["gamt"][:]], writes=[b["sqb"][:]],
                 note=f"{tag}ga{ct}")
            p.op("DVE", lambda e: e.tensor_add(b["xrt"][:], b["xrt"][:],
                                               b["bett"][:]),
                 reads=[b["xrt"][:], b["bett"][:]], writes=[b["xrt"][:]],
                 note=f"{tag}be{ct}")
            p.op("DVE", lambda e: e.tensor_add(b["ob"][:], b["sqb"][:],
                                               b["xrt"][:]),
                 reads=[b["sqb"][:], b["xrt"][:]], writes=[b["ob"][:]],
                 note=f"{tag}ou{ct}")
            p.dma("SP", wd["out_d"][t0:t0 + 128, :], b["ob"][:],
                  note=f"{tag}os{ct}")


def build_l1f_b(C):
    nc = bass.Bass("TRN2", target_bir_lowering=False, debug=False)
    wd = {}
    for nm, shape in L1F_B_WSPECS:
        wd[nm] = nc.dram_tensor(nm, list(shape), F32,
                                kind="ExternalInput").ap()
    for nm, shape, dt_ in [("xres_d", (1024, 2048), F32),
                           ("hid_d", (1024, 2048), BF16),
                           ("route_d", (2048, 24), F32),
                           ("allcnt", (8, 16), F32), ("cmask", (8, 1), F32),
                           ("rstdtm_d", (128, 16), F32),
                           ("mrtm_d", (128, 16), F32)]:
        wd[nm] = nc.dram_tensor(nm, list(shape), dt_,
                                kind="ExternalInput").ap()
    wd["out_d"] = nc.dram_tensor("out", [2048, 1024], F32,
                                 kind="ExternalOutput").ap()
    al = Alloc(nc)
    p = Prog(nc)
    cs = declare_consts(nc, al, p)
    b = {}
    b["hid3"] = al.sbuf("hid3", (128, 16384), BF16)
    b["htg"] = al.sbuf("htg", (128, 32768), BF16)
    b["acc"] = al.sbuf("acc", (128, 8192), F32)
    b["w1s"] = al.sbuf("w1s", (128, 2048), BF16)
    b["w2s"] = al.sbuf("w2s", (128, 1024), BF16)
    b["b1sb"] = al.sbuf("b1sb", (128, 32), F32)
    b["routeb"] = al.sbuf("routeb", (128, 384), F32)
    b["geffb"] = al.sbuf("geffb", (128, 128), F32)
    b["allc"] = al.sbuf("allc", (8, 16), F32)
    b["cmk"] = al.sbuf("cmk", (8, 1), F32)
    b["p0t"] = al.sbuf("p0t", (128, 8), F32)
    b["p1t"] = al.sbuf("p1t", (128, 8), F32)
    b["prows"] = al.sbuf("prows", (128, 2096), F32)
    b["rstm"] = al.sbuf("rstm", (128, 16), F32)
    b["mrtm"] = al.sbuf("mrtm", (128, 16), F32)
    b["gamt"] = al.sbuf("gamt", (128, 1024), F32)
    b["bett"] = al.sbuf("bett", (128, 1024), F32)
    b["b2s"] = al.sbuf("b2s", (8, 1024), F32)
    b["gtr"] = al.sbuf("gtr", (8, 128), F32)
    b["xld"] = al.sbuf("xld", (128, 1024), F32)
    b["xrt"] = al.sbuf("xrt", (128, 1024), F32)
    b["ht"] = al.sbuf("ht", (128, 1024), F32)
    b["sqb"] = al.sbuf("sqb", (128, 1024), F32)
    b["ob"] = al.sbuf("ob", (128, 1024), F32)
    b["scr32"] = al.sbuf("scr32", (128, 40), F32)
    b["psB"] = [al.psum(f"psB{i}", (128, 512), F32) for i in range(8)]
    emit_l1f_geff(p, cs, b, wd, C)
    emit_l1f_experts(p, cs, b, wd)
    p.finalize()
    al.close()
    return nc


def build_l1f_a_only():
    nc = bass.Bass("TRN2", target_bir_lowering=False, debug=False)
    x_d = nc.dram_tensor("x", [2048, 1024], F32, kind="ExternalInput").ap()
    wd = {"wg8": nc.dram_tensor("wg8", [1024, 8], F32,
                                kind="ExternalInput").ap()}
    for nm, shape, dt_ in [("route_d", (2048, 24), F32),
                           ("cnt_d", (1, 16), F32),
                           ("xres_d", (1024, 2048), F32),
                           ("hid_d", (1024, 2048), BF16),
                           ("rstdtm_d", (128, 16), F32),
                           ("mrtm_d", (128, 16), F32)]:
        wd[nm] = nc.dram_tensor(nm, list(shape), dt_,
                                kind="ExternalOutput").ap()
    al = Alloc(nc)
    p = Prog(nc)
    cs = declare_consts(nc, al, p)
    b = alloc_main_bufs(al)
    alloc_l1f_bufs(al, b)
    emit_load_x(p, cs, b, x_d, b["xres"][:])
    emit_l1f_routing(p, al, cs, b, wd)
    p.finalize()
    al.close()
    return nc


def kernel_l1f(core_inputs, C=MOE_C):
    consts = host_consts()
    nca = build_l1f_a_only()
    cmaps = []
    for ci in core_inputs:
        m = {"x": np.ascontiguousarray(np.asarray(ci["x"], np.float32)),
             "wg8": np.ascontiguousarray(np.asarray(ci["L1_ffn_wg"],
                                                    np.float32))}
        m.update(consts)
        cmaps.append(m)
    nc_ids = list(range(len(cmaps)))
    ra = run_bass_kernel_spmd(nca, cmaps, core_ids=nc_ids)
    n = len(core_inputs)
    allcnt = np.concatenate(
        [np.asarray(ra.results[i]["cnt_d"], np.float32) for i in range(n)]
        + [np.zeros((8 - n, 16), np.float32)], axis=0)
    ncb = build_l1f_b(C)
    cmaps2 = []
    for i, ci in enumerate(core_inputs):
        w = host_l1f_split(ci)
        m = dict(w)
        m["allcnt"] = allcnt
        m["cmask"] = (np.arange(8) < i).astype(np.float32).reshape(8, 1)
        for nm in ("route_d", "xres_d", "hid_d", "rstdtm_d", "mrtm_d"):
            m[nm] = np.asarray(ra.results[i][nm])
        m.update(consts)
        cmaps2.append(m)
    rb = run_bass_kernel_spmd(ncb, cmaps2, core_ids=nc_ids)
    return [np.asarray(r["out"]) for r in rb.results]



def kernel_chain5(**inputs):
    x = np.asarray(inputs["L0_attn_x"], np.float32)
    core_inputs = []
    for bi in range(8):
        ci = dict(inputs)
        ci["x"] = x[bi]
        core_inputs.append(ci)
    outs = kernel_l0a(core_inputs)
    for bi in range(8):
        core_inputs[bi]["x"] = outs[bi]
    outs = kernel_l0f(core_inputs)
    for bi in range(8):
        core_inputs[bi]["x"] = outs[bi]
    outs = kernel_l1a(core_inputs)
    for bi in range(8):
        core_inputs[bi]["x"] = outs[bi]
    outs = kernel_l1f(core_inputs, C=MOE_C)
    return np.stack(outs, axis=0)



def build_fused_a():
    nc = bass.Bass("TRN2", target_bir_lowering=False, debug=False)
    x_d = nc.dram_tensor("x", [2048, 1024], F32, kind="ExternalInput").ap()
    wd0 = {}
    for nm, shape in L0A_WSPECS:
        wd0[nm] = nc.dram_tensor(nm, list(shape), F32,
                                 kind="ExternalInput").ap()
    wd0["zx_d"] = nc.dram_tensor("zx_d", [4480, 2048], BF16).ap()
    wd0["yf_d"] = nc.dram_tensor("yf_d", [1024, 2048], BF16).ap()
    wd0["xm_d"] = nc.dram_tensor("xm_d", [1024, 512], F32).ap()
    wd0["wo_bf_d"] = nc.dram_tensor("wo_bf_d", [2048, 1024], BF16).ap()
    wdf = {}
    for nm, shape in L0F_WSPECS:
        wdf[nm] = nc.dram_tensor(nm, list(shape), F32,
                                 kind="ExternalInput").ap()
    for nm, shape, dt_ in L0F_CONST_SPECS:
        wdf[nm] = nc.dram_tensor(nm, list(shape), F32,
                                 kind="ExternalInput").ap()
    for nm, shape in L0F_TABLE_SPECS:
        wdf[nm] = nc.dram_tensor(nm, list(shape), F32,
                                 kind="ExternalInput").ap()
    wdf["qkv_d"] = nc.dram_tensor("qkv_d", [3072, 2048], BF16).ap()
    wdf["v_d"] = nc.dram_tensor("v_d", [2048, 1024], BF16).ap()
    wdf["q_d"] = nc.dram_tensor("q_d", [1024, 2048], BF16).ap()
    wdf["k_d"] = nc.dram_tensor("k_d", [1024, 2048], BF16).ap()
    wdf["a_d"] = nc.dram_tensor("a_d", [1024, 2048], BF16).ap()
    wdf["a_dd"] = nc.dram_tensor("a_dd", [2816, 2048], BF16).ap()
    wdf["g_dd"] = nc.dram_tensor("g_dd", [2816, 2048], BF16).ap()
    wdf["xres_d"] = nc.dram_tensor("x0f_d", [1024, 2048], F32).ap()
    wda = {}
    for nm, shape in L1A_WSPECS:
        wda[nm] = nc.dram_tensor(nm, list(shape), F32,
                                 kind="ExternalInput").ap()
    wda["q_dd"] = nc.dram_tensor("q_dd", [1024, 2048], BF16).ap()
    wda["k_dd"] = nc.dram_tensor("k_dd", [1024, 2048], BF16).ap()
    wda["t1_dd"] = nc.dram_tensor("t1_dd", [128, 2048], BF16).ap()
    wda["v_dd"] = nc.dram_tensor("v_dd", [2048, 2048], BF16).ap()
    wda["g_dd"] = nc.dram_tensor("gg_dd", [2048, 2048], BF16).ap()
    wda["bfm_d"] = nc.dram_tensor("bfm_d", [1024, 2048], F32).ap()
    wda["wout_bf_d"] = nc.dram_tensor("wout_bf_d", [2048, 1024], BF16).ap()
    wdr = {"wg8": nc.dram_tensor("wg8", [1024, 8], F32,
                                 kind="ExternalInput").ap()}
    for nm, shape, dt_ in [("route_d", (2048, 24), F32),
                           ("cnt_d", (1, 16), F32),
                           ("xres_d", (1024, 2048), F32),
                           ("hid_d", (1024, 2048), BF16),
                           ("rstdtm_d", (128, 16), F32),
                           ("mrtm_d", (128, 16), F32)]:
        wdr[nm] = nc.dram_tensor(nm, list(shape), dt_,
                                 kind="ExternalOutput").ap()
    al = Alloc(nc)
    p = Prog(nc)
    cs = declare_consts(nc, al, p)
    b = alloc_main_bufs(al)
    alloc_l0f_bufs(al, b)
    alloc_l1a_bufs(b)
    alloc_l1f_bufs(al, b)
    emit_load_x(p, cs, b, x_d, b["xres"][:])
    emit_l0_attn(p, al, cs, b, wd0)
    emit_l0_ffn_block(p, al, cs, b, wdf)
    emit_wout_precast(p, b, wda["wout_t"], wda["wout_bf_d"], "wpc")
    emit_l1_attn(p, al, cs, b, wda)
    emit_l1f_routing(p, al, cs, b, wdr)
    p.finalize()
    al.close()
    return nc


def kernel(**inputs):
    x = np.asarray(inputs["L0_attn_x"], np.float32)
    consts = host_consts()
    fcon = host_l0f_const_stage(host_l0f_consts())
    nca = build_fused_a()
    cmaps = []
    for bi in range(8):
        ci = dict(inputs)
        ci["x"] = x[bi]
        m = {"x": np.ascontiguousarray(x[bi])}
        w0 = host_l0a_weights(ci)
        for nm, _ in L0A_WSPECS:
            m[nm] = w0[nm]
        wf = host_l0f_weights(ci)
        for nm, _ in L0F_WSPECS:
            m[nm] = wf[nm]
        wa = host_l1a_weights(ci)
        for nm, _ in L1A_WSPECS:
            m[nm] = wa[nm]
        m["wg8"] = np.ascontiguousarray(np.asarray(ci["L1_ffn_wg"],
                                                   np.float32))
        m.update(consts)
        m.update(fcon)
        cmaps.append(m)
    ra = run_bass_kernel_spmd(nca, cmaps, core_ids=list(range(8)))
    allcnt = np.concatenate(
        [np.asarray(ra.results[i]["cnt_d"], np.float32) for i in range(8)],
        axis=0)
    ncb = build_l1f_b(MOE_C)
    cmaps2 = []
    for i in range(8):
        ci = dict(inputs)
        w = host_l1f_split(ci)
        m = dict(w)
        m["allcnt"] = allcnt
        m["cmask"] = (np.arange(8) < i).astype(np.float32).reshape(8, 1)
        for nm in ("route_d", "xres_d", "hid_d", "rstdtm_d", "mrtm_d"):
            m[nm] = np.asarray(ra.results[i][nm])
        m.update(consts)
        cmaps2.append(m)
    rb = run_bass_kernel_spmd(ncb, cmaps2, core_ids=list(range(8)))
    return np.stack([np.asarray(r["out"]) for r in rb.results], axis=0)
